# version 07
# baseline (speedup 1.0000x reference)
import numpy as np
import ml_dtypes
import concourse.bass as bass
import concourse.mybir as mybir
import concourse.bass_utils as _bu
from concourse.bass_utils import run_bass_kernel_spmd


F32 = mybir.dt.float32
BF16 = mybir.dt.bfloat16
FP8 = mybir.dt.float8e4
AF = mybir.ActivationFunctionType
ALU = mybir.AluOpType

NCORES = 8
T = 4096
NT = 32
NSC = 4
E = 1024
H = 16

ENGS = ("tensor", "scalar", "vector", "gpsimd", "sync")
import os
FULL_SERIAL = os.environ.get("FULL_SERIAL", "0") == "1"


class Prog:

    def __init__(self, nc):
        self.nc = nc
        self.sems = {}
        self.cnt = {}
        self.items = {e: [] for e in ENGS}
        self.last_w = {}
        self.readers = {}
        self.waited = {e: {} for e in ENGS}
        self.dma_sems = set()

    def sem(self, name):
        if name not in self.sems:
            self.sems[name] = self.nc.alloc_semaphore(name=name)
            self.cnt[name] = 0
        return self.sems[name]

    def _need(self, eng, sem_name, val, waits):
        if val <= 0:
            return
        w = self.waited[eng]
        if w.get(sem_name, 0) >= val:
            return
        w[sem_name] = val
        waits.append((sem_name, val))

    def op(self, eng, fn, reads=(), writes=(), sem_name=None, inc=0):
        if sem_name is None:
            sem_name = "e_" + eng
        self.sem(sem_name)
        waits = []
        if FULL_SERIAL:
            for sname in list(self.cnt):
                self._need(eng, sname, self.cnt[sname], waits)

        def dep(sname, v):
            self._need(eng, sname, v, waits)

        for k in reads:
            lw = self.last_w.get(k)
            if lw:
                dep(lw[0], lw[1])
        for k in writes:
            lw = self.last_w.get(k)
            if lw:
                dep(lw[0], lw[1])
            for sname, v in self.readers.get(k, {}).items():
                dep(sname, v)
        val = None
        if inc:
            self.cnt[sem_name] += inc
            val = self.cnt[sem_name]
        sems = self.sems

        def thunk(h, waits=waits, fn=fn, sem_name=sem_name, inc=inc):
            for sn, v in waits:
                h.wait_ge(sems[sn], v)
            ins = fn(h)
            if inc and ins is not None:
                ins.then_inc(sems[sem_name], inc)

        self.items[eng].append(thunk)
        if inc:
            for k in reads:
                self.readers.setdefault(k, {})[sem_name] = val
            for k in writes:
                self.last_w[k] = (sem_name, val)
                self.readers[k] = {}
        return val

    def raw(self, eng, fn):
        self.items[eng].append(fn)

    def flush(self):
        nc = self.nc
        items = self.items
        with nc.Block() as blk:
            if items["gpsimd"]:
                blk.gpsimd(lambda h, l=tuple(items["gpsimd"]): [f(h) for f in l])
            if items["sync"]:
                blk.sync(lambda h, l=tuple(items["sync"]): [f(h) for f in l])
            if items["tensor"]:
                blk.tensor(lambda h, l=tuple(items["tensor"]): [f(h) for f in l])
            if items["scalar"]:
                blk.scalar(lambda h, l=tuple(items["scalar"]): [f(h) for f in l])
            if items["vector"]:
                blk.vector(lambda h, l=tuple(items["vector"]): [f(h) for f in l])
        self.items = {e: [] for e in ENGS}


def build_nc():
    nc = bass.Bass("TRN2", target_bir_lowering=False)

    x = nc.dram_tensor("x", [T, E], F32, kind="ExternalInput").ap()
    wat = nc.dram_tensor("wat", [3 * E, E], F32, kind="ExternalInput").ap()
    wpr = nc.dram_tensor("wpr", [E, E], F32, kind="ExternalInput").ap()
    fk = nc.dram_tensor("fk", [E, E], BF16, kind="ExternalInput").ap()
    s_in = nc.dram_tensor("s_in", [128, 1], F32, kind="ExternalInput").ap()
    s_k = nc.dram_tensor("s_k", [128, 1], F32, kind="ExternalInput").ap()
    fbr = nc.dram_tensor("fbr", [128, E], BF16, kind="ExternalInput").ap()
    mkv = nc.dram_tensor("mkv", [128, 1], F32, kind="ExternalInput").ap()
    out = nc.dram_tensor("out", [T, E], F32, kind="ExternalOutput").ap()

    d_xnb = nc.dram_tensor("d_xnb", [128, 8, T], BF16).ap()
    d_k = nc.dram_tensor("d_k", [128, NT, 1024], BF16).ap()
    d_v = nc.dram_tensor("d_v", [128, NT, 1040], BF16).ap()
    d_pf = nc.dram_tensor("d_pf", [128, NT, 520], F32).ap()
    cc_in = nc.dram_tensor("cc_in", [1024, 65], F32).ap()
    cc_out = nc.dram_tensor("cc_out", [2048, 65], F32).ap()

    from contextlib import ExitStack
    ctx = ExitStack()
    sbuf = lambda n, s, d: ctx.enter_context(nc.sbuf_tensor(n, s, d))
    psum = lambda n, s, d: ctx.enter_context(nc.psum_tensor(n, s, d))
    with ctx:
        wq_ar = sbuf("wq_ar", [128, 16384], BF16)
        xnb = sbuf("xnb", [128, 8192], BF16)
        watq = sbuf("watq", [128, 8192], BF16)
        wprT = sbuf("wprT", [128, 8192], BF16)
        kq = sbuf("kq", [128, 8192], FP8)
        xq = sbuf("xq", [128, 2048], FP8)
        k_sb = sbuf("k_sb", [128, 8192], BF16)
        v_sb = sbuf("v_sb", [128, 8320], BF16)
        pf_sb = sbuf("pf_sb", [128, 4160], F32)
        pf_bf = sbuf("pf_bf", [128, 4160], BF16)
        carry = sbuf("carry", [128, 520], F32)
        sinit = sbuf("sinit", [128, 520], F32)
        sinit_bf = sbuf("sinit_bf", [128, 520], BF16)
        xt = sbuf("xt", [128, 4096], F32)
        outt = sbuf("outt", [128, 4096], F32)
        scm = sbuf("scm", [128, 2048], BF16)
        kf = sbuf("kf", [128, 1024], BF16)
        yt = sbuf("yt", [128, 1024], BF16)
        wstg = sbuf("wstg", [128, 2048], BF16)
        psc = sbuf("psc", [128, 2048], F32)
        st = sbuf("st", [128, 64], F32)
        cols = sbuf("cols", [128, 8], F32)
        iot = sbuf("iot", [128, 128], F32)
        ident = sbuf("ident", [128, 128], BF16)
        mask = sbuf("mask", [128, 128], BF16)
        fbb = sbuf("fbb", [128, 1024], BF16)

        pA = psum("pA", [128, 512], F32)
        pB = psum("pB", [128, 512], F32)
        pC = psum("pC", [128, 512], F32)
        pD = psum("pD", [128, 512], F32)
        pS = psum("pS", [128, 1024], F32)
        tr1 = psum("tr1", [128, 1024], BF16)
        den_ps = psum("den_ps", [128, 512], F32)

        P = Prog(nc)
        LD, STQ, CC = "q_ld", "q_st", "q_cc"

        rr = {"sync": 0, "gpsimd": 0}

        def dma(queue, dst, src, reads, writes):
            base = LD if queue == "sync" else STQ
            idx = rr[queue]
            rr[queue] = (idx + 1) % 4
            sem_name = f"{base}{idx}"
            P.sem(sem_name)
            prev = P.cnt[sem_name]

            def fn(h, d=dst, s=src, sem_name=sem_name, prev=prev):
                if prev:
                    h.wait_ge(P.sems[sem_name], prev)
                return h.dma_start(d, s)

            P.op(queue, fn, reads=reads, writes=writes, sem_name=sem_name, inc=16)

        P.op("gpsimd", lambda h: h.iota(iot[:, :], [[1, 128]], channel_multiplier=-1,
                                        allow_small_or_imprecise_dtypes=True),
             writes=["iot"], inc=1)
        P.op("vector", lambda h: h.tensor_single_scalar(mask[:, :], iot[:, :], 0.0, ALU.is_ge),
             reads=["iot"], writes=["mask"], inc=1)
        P.op("vector", lambda h: h.tensor_single_scalar(ident[:, :], iot[:, :], 0.0, ALU.is_equal),
             reads=["iot"], writes=["ident"], inc=1)
        P.op("vector", lambda h: h.memset(cols[:, 0:1], -0.5), writes=["c_nh"], inc=1)
        P.op("vector", lambda h: h.memset(cols[:, 5:6], 1e-5), writes=["c_eps"], inc=1)
        dma("sync", st[:, 0:1], s_in, [], ["st_sc"])
        dma("sync", st[:, 1:2], s_k, [], ["st_sc"])
        dma("sync", st[:, 2:3], mkv, [], ["st_mk"])
        dma("sync", fbb[:, :], fbr, [], ["fbb"])
        P.op("vector", lambda h: h.reciprocal(cols[:, 1:2], st[:, 0:1]),
             reads=["st_sc"], writes=["c_si"], inc=1)
        P.op("vector", lambda h: h.reciprocal(cols[:, 2:3], st[:, 1:2]),
             reads=["st_sc"], writes=["c_sk"], inc=1)
        P.op("vector", lambda h: h.tensor_mul(cols[:, 3:4], st[:, 0:1], st[:, 1:2]),
             reads=["st_sc"], writes=["c_sp"], inc=1)

        def w_slice(kind, r, trp, trkey):
            if kind == "kv":
                src = wat[1024 + 128 * r:1024 + 128 * (r + 1), :]
            elif kind == "q":
                src = wat[128 * r:128 * (r + 1), :]
            else:
                src = wpr[128 * r:128 * (r + 1), :]
            slot = r % 2
            stg = wstg[:, 1024 * slot:1024 * (slot + 1)]
            skey = f"wstg{slot}"
            dma("gpsimd", stg, src, [], [skey])
            for e in range(8):
                P.op("tensor",
                     lambda h, e=e, stg=stg, trp=trp: h.transpose(
                         trp[:, 128 * e:128 * (e + 1)], stg[:, 128 * e:128 * (e + 1)],
                         ident[:, :]),
                     reads=[skey, "ident"],
                     writes=[trkey] if e in (0, 7) else [],
                     inc=1 if e in (0, 7) else 0)
            dstt, stride, dkey = {"kv": (wq_ar, 2048, "w_kv"),
                                  "q": (watq, 1024, "w_q"),
                                  "pr": (wprT, 1024, "w_pr")}[kind]
            base = 128 * r
            P.op("vector",
                 lambda h, trp=trp, dstt=dstt, stride=stride, base=base: h.tensor_copy(
                     dstt[:, :].rearrange("p (ec c) -> p ec c", c=stride)[:, :, base:base + 128],
                     trp[:, :].rearrange("p (ec c) -> p ec c", ec=8)),
                 reads=[trkey], writes=[dkey], inc=1)

        for r in range(16):
            w_slice("kv", r, tr1, "tr1")

        kvbank = [(pA, "pA"), (pB, "pB"), (pC, "pC"), (pD, "pD")]

        def s_region(hh):
            g, par = hh // 2, hh % 2
            off = 65 * g if g < 7 else 512 + 65 * (g - 7)
            key = "pS0" if g < 7 else "pS1"
            return pS[64 * par:64 * (par + 1), off:off + 65], key, (0, 64 * par)

        def pf_col(g):
            return 65 * g

        def ln_stage(t):
            slot = t % 2
            xslot = t % 4
            xts = xt[:, 1024 * xslot:1024 * (xslot + 1)]
            xkey = f"xt{xslot}"
            dma("sync", xts, x[128 * t:128 * (t + 1), :], [], [xkey])
            P.op("vector", lambda h, xts=xts: h.bn_stats(st[:, 16:22], xts[:, 0:512]),
                 reads=[xkey], writes=["stA"], inc=1)
            P.op("vector", lambda h, xts=xts: h.bn_stats(st[:, 22:28], xts[:, 512:1024]),
                 reads=[xkey], writes=["stA"], inc=1)
            P.op("vector", lambda h: h.bn_aggr(st[:, 12:14], st[:, 16:28]),
                 reads=["stA"], writes=["stB"], inc=1)
            P.op("scalar", lambda h: h.activation(st[:, 15:16], st[:, 13:14], AF.Ln,
                                                  bias=cols[:, 5:6]),
                 reads=["stB", "c_eps"], writes=["stD"], inc=1)
            P.op("scalar", lambda h: h.activation(st[:, 16:17], st[:, 15:16], AF.Exp,
                                                  scale=cols[:, 0:1]),
                 reads=["stD", "c_nh"], writes=["stE"], inc=1)
            P.op("vector", lambda h, xts=xts, slot=slot: h.tensor_scalar(
                scm[:, 1024 * slot:1024 * (slot + 1)], xts, st[:, 12:13], st[:, 16:17],
                ALU.subtract, ALU.mult),
                 reads=[xkey, "stB", "stE"], writes=[f"ytm{slot}"], inc=1)

        def heavy_stage(t):
            sc, tl = divmod(t, 8)
            slot = t % 2
            ytm = scm[:, 1024 * slot:1024 * (slot + 1)]
            ykey = f"ytm{slot}"
            for e in range(8):
                P.op("tensor", lambda h, e=e, ytm=ytm: h.transpose(
                    tr1[:, 128 * e:128 * (e + 1)], ytm[:, 128 * e:128 * (e + 1)], ident[:, :]),
                     reads=[ykey, "ident"], writes=["tr1"] if e in (0, 7) else [],
                     inc=1 if e in (0, 7) else 0)
            P.op("scalar", lambda h, tl=tl: h.copy(
                xnb[:, :].rearrange("p (ec c) -> p ec c", ec=8)[:, :, 128 * tl:128 * (tl + 1)],
                tr1[:, :].rearrange("p (ec c) -> p ec c", ec=8)),
                 reads=["tr1"], writes=[f"xnb{tl}"], inc=1)
            for n in range(4):
                bank, bkey = kvbank[n]
                for e in range(8):
                    P.op("tensor", lambda h, e=e, n=n, tl=tl, bank=bank: h.matmul(
                        bank[:, :],
                        xnb[:, 1024 * e + 128 * tl:1024 * e + 128 * (tl + 1)],
                        wq_ar[:, 2048 * e + 512 * n:2048 * e + 512 * (n + 1)],
                        start=(e == 0), stop=(e == 7), skip_group_check=True),
                         reads=[f"xnb{tl}", "w_kv"],
                         writes=[bkey] if e in (0, 7) else [],
                         inc=1 if e in (0, 7) else 0)
                if n < 2:
                    pe = psc[:, 1024 * (n % 2):1024 * (n % 2) + 512]
                    pr = psc[:, 1024 * (n % 2) + 512:1024 * (n % 2) + 1024]
                    pky = f"psc{n % 2}"
                    P.op("scalar", lambda h, bank=bank, pe=pe: h.activation(pe, bank[:, :], AF.Exp),
                         reads=[bkey], writes=[pky + "e"], inc=1)
                    P.op("scalar", lambda h, bank=bank, pr=pr: h.activation(pr, bank[:, :], AF.Relu),
                         reads=[bkey], writes=[pky + "r"], inc=1)
                    P.op("vector", lambda h, pe=pe, pr=pr, tl=tl, n=n: h.scalar_tensor_tensor(
                        k_sb[:, 1024 * tl + 512 * n:1024 * tl + 512 * (n + 1)],
                        pe, 1.0, pr, ALU.min, ALU.add),
                         reads=[pky + "e", pky + "r", bkey], writes=[f"k{tl}"], inc=1)
                else:
                    hh0 = 8 * (n - 2)
                    P.op("vector", lambda h, bank=bank, tl=tl, hh0=hh0: h.tensor_copy(
                        v_sb[:, 1040 * tl:1040 * (tl + 1)].rearrange(
                            "p (hh ee) -> p hh ee", hh=16)[:, hh0:hh0 + 8, 0:64],
                        bank[:, :].rearrange("p (hh ee) -> p hh ee", hh=8)),
                         reads=[bkey], writes=[f"v{tl}"], inc=1)
            P.op("vector", lambda h, tl=tl: h.memset(
                v_sb[:, 1040 * tl:1040 * (tl + 1)].rearrange(
                    "p (hh ee) -> p hh ee", hh=16)[:, :, 64:65], 1.0),
                 writes=[f"v{tl}"], inc=1)
            for hh in range(H):
                sl, skey, tp = s_region(hh)
                P.op("tensor", lambda h, hh=hh, tl=tl, sl=sl, tp=tp: h.matmul(
                    sl,
                    k_sb[:, 1024 * tl + 64 * hh:1024 * tl + 64 * (hh + 1)],
                    v_sb[:, 1040 * tl + 65 * hh:1040 * tl + 65 * (hh + 1)],
                    start=True, stop=True, tile_position=tp, skip_group_check=True),
                     reads=[f"k{tl}", f"v{tl}"],
                     writes=[skey] if hh in (0, 1, 12, 13, 14, 15) else [],
                     inc=1 if hh in (0, 1, 12, 13, 14, 15) else 0)
            if t == 0:
                P.op("vector", lambda h: h.tensor_copy(pf_sb[:, 0:455], pS[:, 0:455]),
                     reads=["pS0"], writes=["pf0"], inc=1)
                P.op("vector", lambda h: h.tensor_copy(pf_sb[:, 455:520], pS[:, 512:577]),
                     reads=["pS1"], writes=["pf0"], inc=1)
            else:
                prev = carry if tl == 0 else pf_sb[:, 520 * (tl - 1):520 * tl]
                pkey = "carry" if tl == 0 else f"pf{tl - 1}"
                P.op("vector", lambda h, tl=tl, prev=prev: h.tensor_add(
                    pf_sb[:, 520 * tl:520 * tl + 455], prev[:, 0:455], pS[:, 0:455]),
                     reads=["pS0", pkey], writes=[f"pf{tl}"], inc=1)
                P.op("vector", lambda h, tl=tl, prev=prev: h.tensor_add(
                    pf_sb[:, 520 * tl + 455:520 * tl + 520], prev[:, 455:520], pS[:, 512:577]),
                     reads=["pS1", pkey], writes=[f"pf{tl}"], inc=1)
            if sc < 3:
                dma("gpsimd", d_xnb[:, :, 1024 * sc + 128 * tl:1024 * sc + 128 * (tl + 1)],
                    xnb[:, :].rearrange("p (ec c) -> p ec c", ec=8)[:, :, 128 * tl:128 * (tl + 1)],
                    [f"xnb{tl}"], [])
                dma("gpsimd", d_k[:, t, :], k_sb[:, 1024 * tl:1024 * (tl + 1)],
                    [f"k{tl}"], [])
                dma("gpsimd", d_v[:, t, :], v_sb[:, 1040 * tl:1040 * (tl + 1)],
                    [f"v{tl}"], [])
            dma("gpsimd", d_pf[:, t, :], pf_sb[:, 520 * tl:520 * (tl + 1)],
                [f"pf{tl}"], [])
            if tl == 7:
                P.op("vector", lambda h: h.tensor_copy(carry[:, :], pf_sb[:, 520 * 7:520 * 8]),
                     reads=["pf7"], writes=["carry"], inc=1)

        ln_stage(0)
        for t in range(NT):
            heavy_stage(t)
            if t + 1 < NT:
                ln_stage(t + 1)
            if t < 8:
                w_slice("q", t, tr1, "tr1")
            elif t < 16:
                w_slice("pr", t - 8, tr1, "tr1")
            elif t < 24:
                e = t - 16
                slot = e % 2
                stg = wstg[:, 1024 * slot:1024 * (slot + 1)]
                skey = f"wstg{slot}"
                dma("gpsimd", stg, fk[128 * e:128 * (e + 1), :], [], [skey])
                P.op("scalar",
                     lambda h, e=e, stg=stg: h.activation(
                         kq[:, 1024 * e:1024 * (e + 1)], stg, AF.Copy, scale=cols[:, 2:3]),
                     reads=[skey, "c_sk"], writes=["kq"], inc=1)

        dma("gpsimd", cc_in.rearrange("(g p) e -> p g e", p=128),
            pf_sb[:, 520 * 7:520 * 8].rearrange("p (g e) -> p g e", g=8),
            ["pf7"], ["cc_in"])
        P.op("gpsimd", lambda h: h.collective_compute(
            "AllGather", ALU.bypass,
            replica_groups=[[0, 1], [2, 3], [4, 5], [6, 7]],
            ins=[cc_in], outs=[cc_out]),
             reads=["cc_in"], writes=["cc_out"], sem_name=CC, inc=1)
        dma("sync", sinit[:, :].rearrange("p (g e) -> p g e", g=8),
            cc_out[0:1024, :].rearrange("(g p) e -> p g e", p=128),
            ["cc_out"], ["sinit_r"])
        P.op("vector", lambda h: h.tensor_scalar_mul(sinit[:, :], sinit[:, :], st[:, 2:3]),
             reads=["sinit_r", "st_mk"], writes=["sinit"], inc=1)
        P.op("vector", lambda h: h.tensor_copy(sinit_bf[:, :], sinit[:, :]),
             reads=["sinit"], writes=["sinitb"], inc=1)

        q_sb = wq_ar
        xnb_keys = [f"xnb{i}" for i in range(8)]
        SCMF = ["scmF0", "scmF1", "scmF2", "scmF3"]
        SCMR = ["scmR0", "scmR1", "scmR2", "scmR3"]

        def b_load_slot(sc, i, skip_xnb=False, pf_only=False):
            if not pf_only:
                if not skip_xnb:
                    dma("sync",
                        xnb[:, :].rearrange("p (ec c) -> p ec c", ec=8)[:, :, 128 * i:128 * (i + 1)],
                        d_xnb[:, :, 1024 * sc + 128 * i:1024 * sc + 128 * (i + 1)],
                        [], [f"xnb{i}"])
                dma("sync", k_sb[:, 1024 * i:1024 * (i + 1)], d_k[:, 8 * sc + i, :],
                    [], [f"k{i}"])
                dma("sync", v_sb[:, 1040 * i:1040 * (i + 1)], d_v[:, 8 * sc + i, :],
                    [], [f"v{i}"])
            cg = 8 * sc + i - 1
            if cg >= 0:
                dma("sync", pf_sb[:, 520 * i:520 * (i + 1)], d_pf[:, cg, :],
                    [], [f"pf{i}"])
                P.op("vector", lambda h, i=i: h.tensor_add(
                    pf_sb[:, 520 * i:520 * (i + 1)], pf_sb[:, 520 * i:520 * (i + 1)],
                    sinit[:, :]),
                     reads=["sinit", f"pf{i}"], writes=[f"pf{i}"], inc=1)
                P.op("vector", lambda h, i=i: h.tensor_copy(
                    pf_bf[:, 520 * i:520 * (i + 1)], pf_sb[:, 520 * i:520 * (i + 1)]),
                     reads=[f"pf{i}"], writes=[f"pfb{i}"], inc=1)

        def b_qproj(sc):
            for qc in range(8):
                lob, lkey = (pA, "pA") if qc % 2 == 0 else (pC, "pC")
                hib, hkey = (pB, "pB") if qc % 2 == 0 else (pD, "pD")
                for e in range(8):
                    P.op("tensor", lambda h, e=e, qc=qc, lob=lob: h.matmul(
                        lob[:, :], watq[:, 1024 * e + 128 * qc:1024 * e + 128 * (qc + 1)],
                        xnb[:, 1024 * e:1024 * e + 512],
                        start=(e == 0), stop=(e == 7), skip_group_check=True),
                         reads=["w_q"] + xnb_keys,
                         writes=[lkey] if e in (0, 7) else [], inc=1 if e in (0, 7) else 0)
                    P.op("tensor", lambda h, e=e, qc=qc, hib=hib: h.matmul(
                        hib[:, :], watq[:, 1024 * e + 128 * qc:1024 * e + 128 * (qc + 1)],
                        xnb[:, 1024 * e + 512:1024 * e + 1024],
                        start=(e == 0), stop=(e == 7), skip_group_check=True),
                         reads=["w_q"] + xnb_keys,
                         writes=[hkey] if e in (0, 7) else [], inc=1 if e in (0, 7) else 0)
                for half, (bank, bkey) in enumerate(((lob, lkey), (hib, hkey))):
                    pe = psc[:, 1024 * (qc % 2):1024 * (qc % 2) + 512]
                    pr = psc[:, 1024 * (qc % 2) + 512:1024 * (qc % 2) + 1024]
                    pky = f"psc{qc % 2}"
                    P.op("scalar", lambda h, bank=bank, pe=pe: h.activation(pe, bank[:, :], AF.Exp),
                         reads=[bkey], writes=[pky + "e"], inc=1)
                    P.op("scalar", lambda h, bank=bank, pr=pr: h.activation(pr, bank[:, :], AF.Relu),
                         reads=[bkey], writes=[pky + "r"], inc=1)
                    P.op("vector", lambda h, pe=pe, pr=pr, qc=qc, half=half: h.scalar_tensor_tensor(
                        q_sb[:, 1024 * qc + 512 * half:1024 * qc + 512 * (half + 1)],
                        pe, 1.0, pr, ALU.min, ALU.add),
                         reads=[pky + "e", pky + "r", bkey], writes=[f"q{qc}"], inc=1)

        def b_ffn(sc):
            for tt in range(8):
                xqs = xq[:, 1024 * (tt % 2):1024 * (tt % 2 + 1)]
                xqkey = f"xq{tt % 2}"
                P.op("scalar", lambda h, tt=tt, xqs=xqs: h.activation(
                    xqs.rearrange("p (ec c) -> p ec c", ec=8),
                    xnb[:, :].rearrange("p (ec c) -> p ec c", ec=8)[:, :, 128 * tt:128 * (tt + 1)],
                    AF.Copy, scale=cols[:, 1:2]),
                     reads=[f"xnb{tt}", "c_si"], writes=[xqkey], inc=1)
                lob, lkey = (pA, "pA") if tt % 2 == 0 else (pC, "pC")
                hib, hkey = (pB, "pB") if tt % 2 == 0 else (pD, "pD")
                for e in range(8):
                    P.op("tensor", lambda h, e=e, xqs=xqs, lob=lob: h.matmul(
                        lob[:, :], xqs[:, 128 * e:128 * (e + 1)],
                        kq[:, 1024 * e:1024 * e + 512],
                        start=(e == 0), stop=(e == 7), skip_group_check=True),
                         reads=[xqkey, "kq"],
                         writes=[lkey] if e in (0, 7) else [], inc=1 if e in (0, 7) else 0)
                    P.op("tensor", lambda h, e=e, xqs=xqs, hib=hib: h.matmul(
                        hib[:, :], xqs[:, 128 * e:128 * (e + 1)],
                        kq[:, 1024 * e + 512:1024 * e + 1024],
                        start=(e == 0), stop=(e == 7), skip_group_check=True),
                         reads=[xqkey, "kq"],
                         writes=[hkey] if e in (0, 7) else [], inc=1 if e in (0, 7) else 0)
                for half, (bank, bkey) in enumerate(((lob, lkey), (hib, hkey))):
                    j = 2 * (tt % 2) + half
                    dst = scm[:, 512 * j:512 * (j + 1)]
                    P.op("scalar", lambda h, bank=bank, dst=dst: h.activation(
                        dst, bank[:, :], AF.Copy, scale=cols[:, 3:4]),
                         reads=[bkey, "c_sp"], writes=[f"scmF{j}"] + SCMR, inc=1)
                    P.op("vector", lambda h, dst=dst, tt=tt, half=half: h.tensor_add(
                        q_sb[:, 8192 + 1024 * tt + 512 * half:8192 + 1024 * tt + 512 * (half + 1)],
                        dst, fbb[:, 512 * half:512 * (half + 1)]),
                         reads=[f"scmF{j}", "fbb"], writes=[f"ffn{tt}"], inc=1)

        def b_ktr(c):
            kfb = kf if c % 2 == 0 else yt
            for g in range(8):
                P.op("tensor", lambda h, g=g, c=c: h.transpose(
                    tr1[:, 128 * g:128 * (g + 1)],
                    k_sb[:, 1024 * c + 128 * g:1024 * c + 128 * (g + 1)], ident[:, :]),
                     reads=[f"k{c}", "ident"], writes=["tr1"] if g in (0, 7) else [],
                     inc=1 if g in (0, 7) else 0)
            P.op("scalar", lambda h, kfb=kfb: h.copy(kfb[:, :], tr1[:, :]),
                 reads=["tr1"], writes=[f"kf{c % 2}"], inc=1)

        def num_region(c, hh):
            if c % 2 == 0:
                nb, nkey = (pC, "pC") if hh < 8 else (pD, "pD")
            else:
                nb, nkey = (pS[:, 0:512], "pS0") if hh < 8 else (pS[:, 512:1024], "pS1")
            return nb, nkey, 64 * (hh % 8)

        def b_scan_core(sc, c):
            kfb = kf if c % 2 == 0 else yt
            kkey = f"kf{c % 2}"
            dkey = "dnE" if c % 2 == 0 else "dnO"
            dbase = 16 * (c % 2)
            def scT_round(r):
                par = r % 2
                heads = [8 * (r // 2) + par + 2 * i for i in range(4)]
                bank, bkey = (pA, "pA") if par == 0 else (pB, "pB")
                for i, hh in enumerate(heads):
                    g = hh // 2
                    P.op("tensor", lambda h, g=g, par=par, c=c, bank=bank, i=i, kfb=kfb:
                         h.matmul(
                             bank[:, 128 * i:128 * (i + 1)],
                             kfb[64 * par:64 * (par + 1), 128 * g:128 * (g + 1)],
                             q_sb[64 * par:64 * (par + 1),
                                  1024 * g + 128 * c:1024 * g + 128 * (c + 1)],
                             start=True, stop=True, skip_group_check=True),
                         reads=[kkey, f"q{g}"],
                         writes=[bkey] if i in (0, 3) else [],
                         inc=1 if i in (0, 3) else 0)
                wkeys = [SCMR[r]] + (SCMF if c == 0 else [])
                P.op("vector", lambda h, bank=bank, r=r, par=par: h.tensor_tensor(
                    scm[:, :].rearrange("p (s c2) -> p s c2", s=16)[
                        :, 8 * (r // 2) + par:8 * (r // 2) + par + 7:2, :],
                    bank[:, :].rearrange("p (s c2) -> p s c2", s=4),
                    mask[:, :].unsqueeze(1).broadcast_to([128, 4, 128]),
                    ALU.mult),
                     reads=[bkey, "mask"], writes=wkeys, inc=1)

            def num_round(r):
                par = r % 2
                heads = [8 * (r // 2) + par + 2 * i for i in range(4)]
                dkey = "dnE" if c % 2 == 0 else "dnO"
                dbase = 16 * (c % 2)
                for hh in heads:
                    g, p2 = hh // 2, hh % 2
                    nb, nkey, noff = num_region(c, hh)
                    if sc == 0 and c == 0:
                        rhs = sinit_bf[64 * p2:64 * (p2 + 1), 65 * g:65 * (g + 1)]
                        rkey = "sinitb"
                    else:
                        pfo = 520 * c + 65 * g
                        rhs = pf_bf[64 * p2:64 * (p2 + 1), pfo:pfo + 65]
                        rkey = f"pfb{c}"
                    P.op("tensor", lambda h, hh=hh, c=c, nb=nb, noff=noff: h.matmul(
                        nb[:, noff:noff + 64],
                        scm[:, 128 * hh:128 * (hh + 1)],
                        v_sb[:, 1040 * c + 65 * hh:1040 * c + 65 * hh + 64],
                        start=True, stop=False, skip_group_check=True),
                         reads=[SCMR[r], f"v{c}"], writes=[nkey], inc=1)
                    P.op("tensor", lambda h, hh=hh, c=c, dbase=dbase: h.matmul(
                        den_ps[:, dbase + hh:dbase + hh + 1],
                        scm[:, 128 * hh:128 * (hh + 1)],
                        v_sb[:, 1040 * c + 65 * hh + 64:1040 * c + 65 * (hh + 1)],
                        start=True, stop=False, skip_group_check=True),
                         reads=[SCMR[r], f"v{c}"], writes=[dkey], inc=1)
                    P.op("tensor", lambda h, g=g, p2=p2, c=c, nb=nb, noff=noff, rhs=rhs:
                         h.matmul(
                             nb[:, noff:noff + 64],
                             q_sb[64 * p2:64 * (p2 + 1),
                                  1024 * g + 128 * c:1024 * g + 128 * (c + 1)],
                             rhs[:, 0:64], start=False, stop=True, skip_group_check=True),
                         reads=[f"q{g}", rkey], writes=[nkey], inc=1)
                    P.op("tensor", lambda h, g=g, p2=p2, c=c, hh=hh, dbase=dbase, rhs=rhs:
                         h.matmul(
                             den_ps[:, dbase + hh:dbase + hh + 1],
                             q_sb[64 * p2:64 * (p2 + 1),
                                  1024 * g + 128 * c:1024 * g + 128 * (c + 1)],
                             rhs[:, 64:65], start=False, stop=True, skip_group_check=True),
                         reads=[f"q{g}", rkey], writes=[dkey], inc=1)

            scT_round(0)
            scT_round(1)
            num_round(0)
            scT_round(2)
            num_round(1)
            scT_round(3)
            num_round(2)
            num_round(3)

        def b_scan_tail(sc, c):
            ysl = wstg[:, 1024 * (c % 2):1024 * (c % 2 + 1)]
            ykey = f"ytw{c % 2}"
            dkey = "dnE" if c % 2 == 0 else "dnO"
            dbase = 16 * (c % 2)
            P.op("vector", lambda h, dbase=dbase: h.tensor_scalar_add(
                st[:, 32:48], den_ps[:, dbase:dbase + 16], 1e-6),
                 reads=[dkey], writes=["dens2"], inc=1)
            P.op("vector", lambda h: h.reciprocal(st[:, 48:64], st[:, 32:48]),
                 reads=["dens2"], writes=["rec"], inc=1)
            for bh in range(2):
                nb, nkey, _ = num_region(c, 8 * bh)
                P.op("vector", lambda h, nb=nb, bh=bh, ysl=ysl: h.tensor_tensor(
                    ysl[:, 512 * bh:512 * (bh + 1)].rearrange("p (s e) -> p s e", s=8),
                    nb[:, :].rearrange("p (s e) -> p s e", s=8),
                    st[:, 48 + 8 * bh:56 + 8 * bh].unsqueeze(2).broadcast_to([128, 8, 64]),
                    ALU.mult),
                     reads=[nkey, "rec"], writes=[ykey], inc=1)
            for g in range(8):
                P.op("tensor", lambda h, g=g, ysl=ysl: h.transpose(
                    tr1[:, 128 * g:128 * (g + 1)], ysl[:, 128 * g:128 * (g + 1)], ident[:, :]),
                     reads=[ykey, "ident"], writes=["tr1"] if g in (0, 7) else [],
                     inc=1 if g in (0, 7) else 0)
            P.op("scalar", lambda h, c=c: h.copy(
                xnb[:, :].rearrange("p (ec c2) -> p ec c2", ec=8)[:, :, 128 * c:128 * (c + 1)],
                tr1[:, :].rearrange("p (ec c2) -> p ec c2", ec=8)),
                 reads=["tr1"], writes=[f"xnb{c}"], inc=1)

        def b_xpre(sc, c):
            tglob = 8 * sc + c
            slot = c % 4
            dma("sync", xt[:, 1024 * slot:1024 * (slot + 1)],
                x[128 * tglob:128 * (tglob + 1), :], [], [f"xt{slot}"])

        def b_outproj(sc, c):
            tglob = 8 * sc + c
            slot = c % 4
            xts = xt[:, 1024 * slot:1024 * (slot + 1)]
            xkey = f"xt{slot}"
            ot = outt[:, 1024 * (c % 2):1024 * (c % 2 + 1)]
            s2 = 2 + c % 2
            ot2 = outt[:, 1024 * s2:1024 * (s2 + 1)]
            okey, okey2 = f"ot{c % 2}", f"ot{2 + c % 2}"
            for half, (bank, bkey) in enumerate(((pA, "pA"), (pB, "pB"))):
                for e in range(8):
                    P.op("tensor", lambda h, e=e, c=c, bank=bank, half=half: h.matmul(
                        bank[:, :], xnb[:, 1024 * e + 128 * c:1024 * e + 128 * (c + 1)],
                        wprT[:, 1024 * e + 512 * half:1024 * e + 512 * (half + 1)],
                        start=(e == 0), stop=(e == 7), skip_group_check=True),
                         reads=[f"xnb{c}", "w_pr"],
                         writes=[bkey] if e in (0, 7) else [], inc=1 if e in (0, 7) else 0)
                P.op("vector", lambda h, ot=ot, xts=xts, bank=bank, half=half: h.tensor_add(
                    ot[:, 512 * half:512 * (half + 1)], xts[:, 512 * half:512 * (half + 1)],
                    bank[:, :]),
                     reads=[xkey, bkey], writes=[okey], inc=1)
            P.op("gpsimd", lambda h, ot=ot, ot2=ot2, c=c: h.tensor_tensor(
                ot2[:, :], ot[:, :],
                q_sb[:, 8192 + 1024 * c:8192 + 1024 * (c + 1)], ALU.add),
                 reads=[okey, f"ffn{c}"], writes=[okey2], inc=1)
            dma("gpsimd", out[128 * tglob:128 * (tglob + 1), :], ot2[:, :], [okey2], [])

        SC_ORDER = [3, 2, 1, 0]
        for si, sc in enumerate(SC_ORDER):
            if si == 0:
                for i in range(8):
                    b_load_slot(sc, i, pf_only=True)
            b_qproj(sc)
            b_ffn(sc)
            b_ktr(0)
            if si == 0:
                b_xpre(sc, 0)
                b_xpre(sc, 1)
            for c in range(8):
                i2 = 8 * si + c + 2
                if i2 < NT:
                    b_xpre(SC_ORDER[i2 // 8], i2 % 8)
                b_scan_core(sc, c)
                if c + 1 < 8:
                    b_ktr(c + 1)
                b_scan_tail(sc, c)
                b_outproj(sc, c)
                if si + 1 < NSC:
                    b_load_slot(SC_ORDER[si + 1], c)

        for i in range(4):
            for base, eng2 in ((STQ, "gpsimd"), (LD, "sync")):
                sn = f"{base}{i}"
                if sn in P.cnt and P.cnt[sn]:
                    P.raw(eng2, lambda h, sn=sn, v=P.cnt[sn]: h.wait_ge(P.sems[sn], v))
        P.flush()
    return nc


def kernel(L0_attn_x, L0_attn_w_attn, L0_attn_w_proj, L0_ffn_kernel,
           L0_ffn_input_scale, L0_ffn_kernel_scale, L0_ffn_bias):
    x = np.ascontiguousarray(np.asarray(L0_attn_x, dtype=np.float32))
    wat = np.ascontiguousarray(np.asarray(L0_attn_w_attn, dtype=np.float32))
    wpr = np.ascontiguousarray(np.asarray(L0_attn_w_proj, dtype=np.float32))
    fk = np.ascontiguousarray(np.asarray(L0_ffn_kernel))
    fb = np.asarray(L0_ffn_bias)
    s_in = np.asarray(L0_ffn_input_scale, dtype=np.float32).ravel()[0]
    s_k = np.asarray(L0_ffn_kernel_scale, dtype=np.float32).ravel()[0]
    sin_col = np.ascontiguousarray(np.broadcast_to(s_in, (128, 1))).astype(np.float32)
    sk_col = np.ascontiguousarray(np.broadcast_to(s_k, (128, 1))).astype(np.float32)
    fbr = np.ascontiguousarray(np.broadcast_to(fb.reshape(1, E), (128, E)))

    nc = build_nc()
    in_maps = []
    for c in range(NCORES):
        b, half = divmod(c, 2)
        xs = np.ascontiguousarray(x[b, half * T:(half + 1) * T, :])
        mk = np.full((128, 1), 1.0 if half == 1 else 0.0, dtype=np.float32)
        in_maps.append({"x": xs, "wat": wat, "wpr": wpr, "fk": fk,
                        "s_in": sin_col, "s_k": sk_col, "fbr": fbr, "mkv": mk})
    res = run_bass_kernel_spmd(nc, in_maps, core_ids=list(range(NCORES)))
    outp = np.empty((4, 8192, E), dtype=np.float32)
    for c in range(NCORES):
        b, half = divmod(c, 2)
        outp[b, half * T:(half + 1) * T, :] = res.results[c]["out"]
    return outp


# version 08
# speedup vs baseline: 1.1488x; 1.1488x over previous
import numpy as np
import ml_dtypes
import concourse.bass as bass
import concourse.mybir as mybir
import concourse.bass_utils as _bu
from concourse.bass_utils import run_bass_kernel_spmd


F32 = mybir.dt.float32
BF16 = mybir.dt.bfloat16
FP8 = mybir.dt.float8e4
AF = mybir.ActivationFunctionType
ALU = mybir.AluOpType

NCORES = 8
T = 4096
NT = 32
NSC = 4
E = 1024
H = 16

ENGS = ("tensor", "scalar", "vector", "gpsimd", "sync")
import os
FULL_SERIAL = os.environ.get("FULL_SERIAL", "0") == "1"


class Prog:

    def __init__(self, nc):
        self.nc = nc
        self.sems = {}
        self.cnt = {}
        self.items = {e: [] for e in ENGS}
        self.last_w = {}
        self.readers = {}
        self.waited = {e: {} for e in ENGS}
        self.dma_sems = set()

    def sem(self, name):
        if name not in self.sems:
            self.sems[name] = self.nc.alloc_semaphore(name=name)
            self.cnt[name] = 0
        return self.sems[name]

    def _need(self, eng, sem_name, val, waits):
        if val <= 0:
            return
        w = self.waited[eng]
        if w.get(sem_name, 0) >= val:
            return
        w[sem_name] = val
        waits.append((sem_name, val))

    def op(self, eng, fn, reads=(), writes=(), sem_name=None, inc=0):
        if sem_name is None:
            sem_name = "e_" + eng
        self.sem(sem_name)
        waits = []
        if FULL_SERIAL:
            for sname in list(self.cnt):
                self._need(eng, sname, self.cnt[sname], waits)

        def dep(sname, v):
            self._need(eng, sname, v, waits)

        for k in reads:
            lw = self.last_w.get(k)
            if lw:
                dep(lw[0], lw[1])
        for k in writes:
            lw = self.last_w.get(k)
            if lw:
                dep(lw[0], lw[1])
            for sname, v in self.readers.get(k, {}).items():
                dep(sname, v)
        val = None
        if inc:
            self.cnt[sem_name] += inc
            val = self.cnt[sem_name]
        sems = self.sems

        def thunk(h, waits=waits, fn=fn, sem_name=sem_name, inc=inc):
            for sn, v in waits:
                h.wait_ge(sems[sn], v)
            ins = fn(h)
            if inc and ins is not None:
                ins.then_inc(sems[sem_name], inc)

        self.items[eng].append(thunk)
        if inc:
            for k in reads:
                self.readers.setdefault(k, {})[sem_name] = val
            for k in writes:
                self.last_w[k] = (sem_name, val)
                self.readers[k] = {}
        return val

    def raw(self, eng, fn):
        self.items[eng].append(fn)

    def flush(self):
        nc = self.nc
        items = self.items
        with nc.Block() as blk:
            if items["gpsimd"]:
                blk.gpsimd(lambda h, l=tuple(items["gpsimd"]): [f(h) for f in l])
            if items["sync"]:
                blk.sync(lambda h, l=tuple(items["sync"]): [f(h) for f in l])
            if items["tensor"]:
                blk.tensor(lambda h, l=tuple(items["tensor"]): [f(h) for f in l])
            if items["scalar"]:
                blk.scalar(lambda h, l=tuple(items["scalar"]): [f(h) for f in l])
            if items["vector"]:
                blk.vector(lambda h, l=tuple(items["vector"]): [f(h) for f in l])
        self.items = {e: [] for e in ENGS}


def build_nc():
    nc = bass.Bass("TRN2", target_bir_lowering=False)

    x = nc.dram_tensor("x", [T, E], F32, kind="ExternalInput").ap()
    wat = nc.dram_tensor("wat", [3 * E, E], F32, kind="ExternalInput").ap()
    wpr = nc.dram_tensor("wpr", [E, E], F32, kind="ExternalInput").ap()
    fk = nc.dram_tensor("fk", [E, E], BF16, kind="ExternalInput").ap()
    s_in = nc.dram_tensor("s_in", [128, 1], F32, kind="ExternalInput").ap()
    s_k = nc.dram_tensor("s_k", [128, 1], F32, kind="ExternalInput").ap()
    fbr = nc.dram_tensor("fbr", [128, E], BF16, kind="ExternalInput").ap()
    mkv = nc.dram_tensor("mkv", [128, 1], F32, kind="ExternalInput").ap()
    out = nc.dram_tensor("out", [T, E], F32, kind="ExternalOutput").ap()

    d_xnb = nc.dram_tensor("d_xnb", [128, 8, T], BF16).ap()
    d_k = nc.dram_tensor("d_k", [128, NT, 1024], BF16).ap()
    d_v = nc.dram_tensor("d_v", [128, NT, 1040], BF16).ap()
    d_pf = nc.dram_tensor("d_pf", [128, NT, 520], F32).ap()
    cc_in = nc.dram_tensor("cc_in", [1024, 65], F32).ap()
    cc_out = nc.dram_tensor("cc_out", [2048, 65], F32).ap()

    from contextlib import ExitStack
    ctx = ExitStack()
    sbuf = lambda n, s, d: ctx.enter_context(nc.sbuf_tensor(n, s, d))
    psum = lambda n, s, d: ctx.enter_context(nc.psum_tensor(n, s, d))
    with ctx:
        wq_ar = sbuf("wq_ar", [128, 16384], BF16)
        xnb = sbuf("xnb", [128, 8192], BF16)
        watq = sbuf("watq", [128, 8192], BF16)
        wprT = sbuf("wprT", [128, 8192], BF16)
        kq = sbuf("kq", [128, 8192], FP8)
        xq = sbuf("xq", [128, 2048], FP8)
        k_sb = sbuf("k_sb", [128, 8192], BF16)
        v_sb = sbuf("v_sb", [128, 8320], BF16)
        pf_sb = sbuf("pf_sb", [128, 4160], F32)
        pf_bf = sbuf("pf_bf", [128, 4160], BF16)
        carry = sbuf("carry", [128, 520], F32)
        sinit = sbuf("sinit", [128, 520], F32)
        sinit_bf = sbuf("sinit_bf", [128, 520], BF16)
        xt = sbuf("xt", [128, 4096], F32)
        outt = sbuf("outt", [128, 4096], F32)
        scm = sbuf("scm", [128, 2048], BF16)
        kf = sbuf("kf", [128, 1024], BF16)
        yt = sbuf("yt", [128, 1024], BF16)
        wstg = sbuf("wstg", [128, 2048], BF16)
        psc = sbuf("psc", [128, 2048], F32)
        st = sbuf("st", [128, 64], F32)
        cols = sbuf("cols", [128, 8], F32)
        iot = sbuf("iot", [128, 128], F32)
        ident = sbuf("ident", [128, 128], BF16)
        mask = sbuf("mask", [128, 128], BF16)
        fbb = sbuf("fbb", [128, 1024], BF16)

        pA = psum("pA", [128, 512], F32)
        pB = psum("pB", [128, 512], F32)
        pC = psum("pC", [128, 512], F32)
        pD = psum("pD", [128, 512], F32)
        pS = psum("pS", [128, 1024], F32)
        tr1 = psum("tr1", [128, 1024], BF16)
        den_ps = psum("den_ps", [128, 512], F32)

        P = Prog(nc)
        LD, STQ, CC = "q_ld", "q_st", "q_cc"

        rr = {"sync": 0, "gpsimd": 0}

        def dma(queue, dst, src, reads, writes):
            base = LD if queue == "sync" else STQ
            idx = rr[queue]
            rr[queue] = (idx + 1) % 4
            sem_name = f"{base}{idx}"
            P.sem(sem_name)
            prev = P.cnt[sem_name]

            def fn(h, d=dst, s=src, sem_name=sem_name, prev=prev):
                if prev:
                    h.wait_ge(P.sems[sem_name], prev)
                return h.dma_start(d, s)

            P.op(queue, fn, reads=reads, writes=writes, sem_name=sem_name, inc=16)

        P.op("gpsimd", lambda h: h.iota(iot[:, :], [[1, 128]], channel_multiplier=-1,
                                        allow_small_or_imprecise_dtypes=True),
             writes=["iot"], inc=1)
        P.op("vector", lambda h: h.tensor_single_scalar(mask[:, :], iot[:, :], 0.0, ALU.is_ge),
             reads=["iot"], writes=["mask"], inc=1)
        P.op("vector", lambda h: h.tensor_single_scalar(ident[:, :], iot[:, :], 0.0, ALU.is_equal),
             reads=["iot"], writes=["ident"], inc=1)
        P.op("vector", lambda h: h.memset(cols[:, 0:1], -0.5), writes=["c_nh"], inc=1)
        P.op("vector", lambda h: h.memset(cols[:, 5:6], 1e-5), writes=["c_eps"], inc=1)
        dma("sync", st[:, 0:1], s_in, [], ["st_sc"])
        dma("sync", st[:, 1:2], s_k, [], ["st_sc"])
        dma("sync", st[:, 2:3], mkv, [], ["st_mk"])
        dma("sync", fbb[:, :], fbr, [], ["fbb"])
        P.op("vector", lambda h: h.reciprocal(cols[:, 1:2], st[:, 0:1]),
             reads=["st_sc"], writes=["c_si"], inc=1)
        P.op("vector", lambda h: h.reciprocal(cols[:, 2:3], st[:, 1:2]),
             reads=["st_sc"], writes=["c_sk"], inc=1)
        P.op("vector", lambda h: h.tensor_mul(cols[:, 3:4], st[:, 0:1], st[:, 1:2]),
             reads=["st_sc"], writes=["c_sp"], inc=1)

        def w_dma(kind, r, slot):
            if kind == "kv":
                srcw = wat[1024 + 128 * r:1024 + 128 * (r + 1), :]
            elif kind == "q":
                srcw = wat[128 * r:128 * (r + 1), :]
            elif kind == "fk":
                srcw = fk[128 * r:128 * (r + 1), :]
            else:
                srcw = wpr[128 * r:128 * (r + 1), :]
            dma("gpsimd", wstg[:, 1024 * slot:1024 * (slot + 1)], srcw, [], [f"wstg{slot}"])

        def w_slice(kind, r, trp, trkey, slot):
            stg = wstg[:, 1024 * slot:1024 * (slot + 1)]
            skey = f"wstg{slot}"
            for e in range(8):
                P.op("tensor",
                     lambda h, e=e, stg=stg, trp=trp: h.transpose(
                         trp[:, 128 * e:128 * (e + 1)], stg[:, 128 * e:128 * (e + 1)],
                         ident[:, :]),
                     reads=[skey, "ident"],
                     writes=[trkey] if e in (0, 7) else [],
                     inc=1 if e in (0, 7) else 0)
            dstt, stride, dkey = {"kv": (wq_ar, 2048, "w_kv"),
                                  "q": (watq, 1024, "w_q"),
                                  "pr": (wprT, 1024, "w_pr")}[kind]
            base = 128 * r
            P.op("vector",
                 lambda h, trp=trp, dstt=dstt, stride=stride, base=base: h.tensor_copy(
                     dstt[:, :].rearrange("p (ec c) -> p ec c", c=stride)[:, :, base:base + 128],
                     trp[:, :].rearrange("p (ec c) -> p ec c", ec=8)),
                 reads=[trkey], writes=[dkey], inc=1)

        WPLAN = ([("kv", r) for r in range(16)] + [("q", r) for r in range(8)]
                 + [("pr", r) for r in range(8)] + [("fk", r) for r in range(8)])
        w_dma(*WPLAN[0], 0)
        w_dma(*WPLAN[1], 1)
        for r in range(16):
            w_slice("kv", r, tr1, "tr1", r % 2)
            if r + 2 < len(WPLAN):
                w_dma(*WPLAN[r + 2], (r + 2) % 2)

        kvbank = [(pA, "pA"), (pB, "pB"), (pC, "pC"), (pD, "pD")]

        def s_region(hh):
            g, par = hh // 2, hh % 2
            off = 65 * g if g < 7 else 512 + 65 * (g - 7)
            key = "pS0" if g < 7 else "pS1"
            return pS[64 * par:64 * (par + 1), off:off + 65], key, (0, 64 * par)

        def pf_col(g):
            return 65 * g

        def ln_a(t):
            xslot = t % 4
            xts = xt[:, 1024 * xslot:1024 * (xslot + 1)]
            xkey = f"xt{xslot}"
            dma("sync", xts, x[128 * t:128 * (t + 1), :], [], [xkey])
            P.op("vector", lambda h, xts=xts: h.bn_stats(st[:, 16:22], xts[:, 0:512]),
                 reads=[xkey], writes=["stA"], inc=1)
            P.op("vector", lambda h, xts=xts: h.bn_stats(st[:, 22:28], xts[:, 512:1024]),
                 reads=[xkey], writes=["stA"], inc=1)
            P.op("vector", lambda h: h.bn_aggr(st[:, 12:14], st[:, 16:28]),
                 reads=["stA"], writes=["stB"], inc=1)
            P.op("scalar", lambda h: h.activation(st[:, 15:16], st[:, 13:14], AF.Ln,
                                                  bias=cols[:, 5:6]),
                 reads=["stB", "c_eps"], writes=["stD"], inc=1)
            P.op("scalar", lambda h: h.activation(st[:, 16:17], st[:, 15:16], AF.Exp,
                                                  scale=cols[:, 0:1]),
                 reads=["stD", "c_nh"], writes=["stE"], inc=1)

        def ln_b(t):
            slot = t % 2
            xslot = t % 4
            xts = xt[:, 1024 * xslot:1024 * (xslot + 1)]
            P.op("vector", lambda h, xts=xts, slot=slot: h.tensor_scalar(
                scm[:, 1024 * slot:1024 * (slot + 1)], xts, st[:, 12:13], st[:, 16:17],
                ALU.subtract, ALU.mult),
                 reads=[f"xt{xslot}", "stB", "stE"], writes=[f"ytm{slot}"], inc=1)

        def heavy_stage(t):
            sc, tl = divmod(t, 8)
            slot = t % 2
            ytm = scm[:, 1024 * slot:1024 * (slot + 1)]
            ykey = f"ytm{slot}"
            for e in range(8):
                P.op("tensor", lambda h, e=e, ytm=ytm: h.transpose(
                    tr1[:, 128 * e:128 * (e + 1)], ytm[:, 128 * e:128 * (e + 1)], ident[:, :]),
                     reads=[ykey, "ident"], writes=["tr1"] if e in (0, 7) else [],
                     inc=1 if e in (0, 7) else 0)
            P.op("scalar", lambda h, tl=tl: h.copy(
                xnb[:, :].rearrange("p (ec c) -> p ec c", ec=8)[:, :, 128 * tl:128 * (tl + 1)],
                tr1[:, :].rearrange("p (ec c) -> p ec c", ec=8)),
                 reads=["tr1"], writes=[f"xnb{tl}"], inc=1)
            for n in range(4):
                bank, bkey = kvbank[n]
                for e in range(8):
                    P.op("tensor", lambda h, e=e, n=n, tl=tl, bank=bank: h.matmul(
                        bank[:, :],
                        xnb[:, 1024 * e + 128 * tl:1024 * e + 128 * (tl + 1)],
                        wq_ar[:, 2048 * e + 512 * n:2048 * e + 512 * (n + 1)],
                        start=(e == 0), stop=(e == 7), skip_group_check=True),
                         reads=[f"xnb{tl}", "w_kv"],
                         writes=[bkey] if e in (0, 7) else [],
                         inc=1 if e in (0, 7) else 0)
                if n < 2:
                    pe = psc[:, 1024 * (n % 2):1024 * (n % 2) + 512]
                    pr = psc[:, 1024 * (n % 2) + 512:1024 * (n % 2) + 1024]
                    pky = f"psc{n % 2}"
                    P.op("scalar", lambda h, bank=bank, pe=pe: h.activation(pe, bank[:, :], AF.Exp),
                         reads=[bkey], writes=[pky + "e"], inc=1)
                    P.op("scalar", lambda h, bank=bank, pr=pr: h.activation(pr, bank[:, :], AF.Relu),
                         reads=[bkey], writes=[pky + "r"], inc=1)
                    P.op("vector", lambda h, pe=pe, pr=pr, tl=tl, n=n: h.scalar_tensor_tensor(
                        k_sb[:, 1024 * tl + 512 * n:1024 * tl + 512 * (n + 1)],
                        pe, 1.0, pr, ALU.min, ALU.add),
                         reads=[pky + "e", pky + "r", bkey], writes=[f"k{tl}"], inc=1)
                else:
                    hh0 = 8 * (n - 2)
                    P.op("vector", lambda h, bank=bank, tl=tl, hh0=hh0: h.tensor_copy(
                        v_sb[:, 1040 * tl:1040 * (tl + 1)].rearrange(
                            "p (hh ee) -> p hh ee", hh=16)[:, hh0:hh0 + 8, 0:64],
                        bank[:, :].rearrange("p (hh ee) -> p hh ee", hh=8)),
                         reads=[bkey], writes=[f"v{tl}"], inc=1)
            P.op("vector", lambda h, tl=tl: h.memset(
                v_sb[:, 1040 * tl:1040 * (tl + 1)].rearrange(
                    "p (hh ee) -> p hh ee", hh=16)[:, :, 64:65], 1.0),
                 writes=[f"v{tl}"], inc=1)
            for hh in range(H):
                sl, skey, tp = s_region(hh)
                P.op("tensor", lambda h, hh=hh, tl=tl, sl=sl, tp=tp: h.matmul(
                    sl,
                    k_sb[:, 1024 * tl + 64 * hh:1024 * tl + 64 * (hh + 1)],
                    v_sb[:, 1040 * tl + 65 * hh:1040 * tl + 65 * (hh + 1)],
                    start=True, stop=True, tile_position=tp, skip_group_check=True),
                     reads=[f"k{tl}", f"v{tl}"],
                     writes=[skey] if hh in (0, 1, 12, 13, 14, 15) else [],
                     inc=1 if hh in (0, 1, 12, 13, 14, 15) else 0)
            if t == 0:
                P.op("vector", lambda h: h.tensor_copy(pf_sb[:, 0:455], pS[:, 0:455]),
                     reads=["pS0"], writes=["pf0"], inc=1)
                P.op("vector", lambda h: h.tensor_copy(pf_sb[:, 455:520], pS[:, 512:577]),
                     reads=["pS1"], writes=["pf0"], inc=1)
            else:
                prev = carry if tl == 0 else pf_sb[:, 520 * (tl - 1):520 * tl]
                pkey = "carry" if tl == 0 else f"pf{tl - 1}"
                P.op("vector", lambda h, tl=tl, prev=prev: h.tensor_add(
                    pf_sb[:, 520 * tl:520 * tl + 455], prev[:, 0:455], pS[:, 0:455]),
                     reads=["pS0", pkey], writes=[f"pf{tl}"], inc=1)
                P.op("vector", lambda h, tl=tl, prev=prev: h.tensor_add(
                    pf_sb[:, 520 * tl + 455:520 * tl + 520], prev[:, 455:520], pS[:, 512:577]),
                     reads=["pS1", pkey], writes=[f"pf{tl}"], inc=1)
            if sc < 3:
                dma("gpsimd", d_xnb[:, :, 1024 * sc + 128 * tl:1024 * sc + 128 * (tl + 1)],
                    xnb[:, :].rearrange("p (ec c) -> p ec c", ec=8)[:, :, 128 * tl:128 * (tl + 1)],
                    [f"xnb{tl}"], [])
                dma("gpsimd", d_k[:, t, :], k_sb[:, 1024 * tl:1024 * (tl + 1)],
                    [f"k{tl}"], [])
                dma("gpsimd", d_v[:, t, :], v_sb[:, 1040 * tl:1040 * (tl + 1)],
                    [f"v{tl}"], [])
            dma("gpsimd", d_pf[:, t, :], pf_sb[:, 520 * tl:520 * (tl + 1)],
                [f"pf{tl}"], [])
            if tl == 7:
                P.op("vector", lambda h: h.tensor_copy(carry[:, :], pf_sb[:, 520 * 7:520 * 8]),
                     reads=["pf7"], writes=["carry"], inc=1)

        ln_a(0)
        ln_b(0)
        for t in range(NT):
            if t + 1 < NT:
                ln_a(t + 1)
            heavy_stage(t)
            if t + 1 < NT:
                ln_b(t + 1)
            wi = t + 16
            if t < 8:
                w_slice("q", t, tr1, "tr1", wi % 2)
            elif t < 16:
                w_slice("pr", t - 8, tr1, "tr1", wi % 2)
            elif t < 24:
                e = t - 16
                slot = wi % 2
                stg = wstg[:, 1024 * slot:1024 * (slot + 1)]
                skey = f"wstg{slot}"
                P.op("scalar",
                     lambda h, e=e, stg=stg: h.activation(
                         kq[:, 1024 * e:1024 * (e + 1)], stg, AF.Copy, scale=cols[:, 2:3]),
                     reads=[skey, "c_sk"], writes=["kq"], inc=1)
            if wi + 2 < len(WPLAN):
                w_dma(*WPLAN[wi + 2], (wi + 2) % 2)

        dma("gpsimd", cc_in.rearrange("(g p) e -> p g e", p=128),
            pf_sb[:, 520 * 7:520 * 8].rearrange("p (g e) -> p g e", g=8),
            ["pf7"], ["cc_in"])
        P.op("gpsimd", lambda h: h.collective_compute(
            "AllGather", ALU.bypass,
            replica_groups=[[0, 1], [2, 3], [4, 5], [6, 7]],
            ins=[cc_in], outs=[cc_out]),
             reads=["cc_in"], writes=["cc_out"], sem_name=CC, inc=1)
        dma("sync", sinit[:, :].rearrange("p (g e) -> p g e", g=8),
            cc_out[0:1024, :].rearrange("(g p) e -> p g e", p=128),
            ["cc_out"], ["sinit_r"])
        P.op("vector", lambda h: h.tensor_scalar_mul(sinit[:, :], sinit[:, :], st[:, 2:3]),
             reads=["sinit_r", "st_mk"], writes=["sinit"], inc=1)
        P.op("vector", lambda h: h.tensor_copy(sinit_bf[:, :], sinit[:, :]),
             reads=["sinit"], writes=["sinitb"], inc=1)

        q_sb = wq_ar
        xnb_keys = [f"xnb{i}" for i in range(8)]
        SCMF = ["scmF0", "scmF1", "scmF2", "scmF3"]
        SCMR = ["scmR0", "scmR1", "scmR2", "scmR3"]

        def b_load_slot(sc, i, skip_xnb=False, pf_only=False):
            if not pf_only:
                if not skip_xnb:
                    dma("sync",
                        xnb[:, :].rearrange("p (ec c) -> p ec c", ec=8)[:, :, 128 * i:128 * (i + 1)],
                        d_xnb[:, :, 1024 * sc + 128 * i:1024 * sc + 128 * (i + 1)],
                        [], [f"xnb{i}"])
                dma("sync", k_sb[:, 1024 * i:1024 * (i + 1)], d_k[:, 8 * sc + i, :],
                    [], [f"k{i}"])
                dma("sync", v_sb[:, 1040 * i:1040 * (i + 1)], d_v[:, 8 * sc + i, :],
                    [], [f"v{i}"])
            cg = 8 * sc + i - 1
            if cg >= 0:
                dma("sync", pf_sb[:, 520 * i:520 * (i + 1)], d_pf[:, cg, :],
                    [], [f"pf{i}"])
                P.op("vector", lambda h, i=i: h.tensor_add(
                    pf_sb[:, 520 * i:520 * (i + 1)], pf_sb[:, 520 * i:520 * (i + 1)],
                    sinit[:, :]),
                     reads=["sinit", f"pf{i}"], writes=[f"pf{i}"], inc=1)
                P.op("scalar", lambda h, i=i: h.copy(
                    pf_bf[:, 520 * i:520 * (i + 1)], pf_sb[:, 520 * i:520 * (i + 1)]),
                     reads=[f"pf{i}"], writes=[f"pfb{i}"], inc=1)

        def b_qproj(sc):
            for qc in range(8):
                lob, lkey = (pA, "pA") if qc % 2 == 0 else (pC, "pC")
                hib, hkey = (pB, "pB") if qc % 2 == 0 else (pD, "pD")
                for e in range(8):
                    P.op("tensor", lambda h, e=e, qc=qc, lob=lob: h.matmul(
                        lob[:, :], watq[:, 1024 * e + 128 * qc:1024 * e + 128 * (qc + 1)],
                        xnb[:, 1024 * e:1024 * e + 512],
                        start=(e == 0), stop=(e == 7), skip_group_check=True),
                         reads=["w_q"] + xnb_keys,
                         writes=[lkey] if e in (0, 7) else [], inc=1 if e in (0, 7) else 0)
                    P.op("tensor", lambda h, e=e, qc=qc, hib=hib: h.matmul(
                        hib[:, :], watq[:, 1024 * e + 128 * qc:1024 * e + 128 * (qc + 1)],
                        xnb[:, 1024 * e + 512:1024 * e + 1024],
                        start=(e == 0), stop=(e == 7), skip_group_check=True),
                         reads=["w_q"] + xnb_keys,
                         writes=[hkey] if e in (0, 7) else [], inc=1 if e in (0, 7) else 0)
                for half, (bank, bkey) in enumerate(((lob, lkey), (hib, hkey))):
                    pe = psc[:, 1024 * (qc % 2):1024 * (qc % 2) + 512]
                    pr = psc[:, 1024 * (qc % 2) + 512:1024 * (qc % 2) + 1024]
                    pky = f"psc{qc % 2}"
                    P.op("scalar", lambda h, bank=bank, pe=pe: h.activation(pe, bank[:, :], AF.Exp),
                         reads=[bkey], writes=[pky + "e"], inc=1)
                    P.op("scalar", lambda h, bank=bank, pr=pr: h.activation(pr, bank[:, :], AF.Relu),
                         reads=[bkey], writes=[pky + "r"], inc=1)
                    P.op("vector", lambda h, pe=pe, pr=pr, qc=qc, half=half: h.scalar_tensor_tensor(
                        q_sb[:, 1024 * qc + 512 * half:1024 * qc + 512 * (half + 1)],
                        pe, 1.0, pr, ALU.min, ALU.add),
                         reads=[pky + "e", pky + "r", bkey], writes=[f"q{qc}"], inc=1)

        def b_ffn_quant(tt):
            xqs = xq[:, 1024 * (tt % 2):1024 * (tt % 2 + 1)]
            P.op("scalar", lambda h, tt=tt, xqs=xqs: h.activation(
                xqs.rearrange("p (ec c) -> p ec c", ec=8),
                xnb[:, :].rearrange("p (ec c) -> p ec c", ec=8)[:, :, 128 * tt:128 * (tt + 1)],
                AF.Copy, scale=cols[:, 1:2]),
                 reads=[f"xnb{tt}", "c_si"], writes=[f"xq{tt % 2}"], inc=1)

        def b_ffn(sc):
            b_ffn_quant(0)
            for tt in range(8):
                xqs = xq[:, 1024 * (tt % 2):1024 * (tt % 2 + 1)]
                xqkey = f"xq{tt % 2}"
                if tt + 1 < 8:
                    b_ffn_quant(tt + 1)
                lob, lkey = (pA, "pA") if tt % 2 == 0 else (pC, "pC")
                hib, hkey = (pB, "pB") if tt % 2 == 0 else (pD, "pD")
                for e in range(8):
                    P.op("tensor", lambda h, e=e, xqs=xqs, lob=lob: h.matmul(
                        lob[:, :], xqs[:, 128 * e:128 * (e + 1)],
                        kq[:, 1024 * e:1024 * e + 512],
                        start=(e == 0), stop=(e == 7), skip_group_check=True),
                         reads=[xqkey, "kq"],
                         writes=[lkey] if e in (0, 7) else [], inc=1 if e in (0, 7) else 0)
                    P.op("tensor", lambda h, e=e, xqs=xqs, hib=hib: h.matmul(
                        hib[:, :], xqs[:, 128 * e:128 * (e + 1)],
                        kq[:, 1024 * e + 512:1024 * e + 1024],
                        start=(e == 0), stop=(e == 7), skip_group_check=True),
                         reads=[xqkey, "kq"],
                         writes=[hkey] if e in (0, 7) else [], inc=1 if e in (0, 7) else 0)
                for half, (bank, bkey) in enumerate(((lob, lkey), (hib, hkey))):
                    j = 2 * (tt % 2) + half
                    dst = scm[:, 512 * j:512 * (j + 1)]
                    P.op("scalar", lambda h, bank=bank, dst=dst: h.activation(
                        dst, bank[:, :], AF.Copy, scale=cols[:, 3:4]),
                         reads=[bkey, "c_sp"], writes=[f"scmF{j}"] + SCMR, inc=1)
                    P.op("vector", lambda h, dst=dst, tt=tt, half=half: h.tensor_add(
                        q_sb[:, 8192 + 1024 * tt + 512 * half:8192 + 1024 * tt + 512 * (half + 1)],
                        dst, fbb[:, 512 * half:512 * (half + 1)]),
                         reads=[f"scmF{j}", "fbb"], writes=[f"ffn{tt}"], inc=1)

        def b_ktr(c):
            kfb = kf if c % 2 == 0 else yt
            for g in range(8):
                P.op("tensor", lambda h, g=g, c=c: h.transpose(
                    tr1[:, 128 * g:128 * (g + 1)],
                    k_sb[:, 1024 * c + 128 * g:1024 * c + 128 * (g + 1)], ident[:, :]),
                     reads=[f"k{c}", "ident"], writes=["tr1"] if g in (0, 7) else [],
                     inc=1 if g in (0, 7) else 0)
            P.op("scalar", lambda h, kfb=kfb: h.copy(kfb[:, :], tr1[:, :]),
                 reads=["tr1"], writes=[f"kf{c % 2}"], inc=1)

        def num_region(c, hh):
            if c % 2 == 0:
                nb, nkey = (pC, "pC") if hh < 8 else (pD, "pD")
            else:
                nb, nkey = (pS[:, 0:512], "pS0") if hh < 8 else (pS[:, 512:1024], "pS1")
            return nb, nkey, 64 * (hh % 8)

        def b_scan_core(sc, c):
            kfb = kf if c % 2 == 0 else yt
            kkey = f"kf{c % 2}"
            dkey = "dnE" if c % 2 == 0 else "dnO"
            dbase = 16 * (c % 2)
            def scT_round(r):
                par = r % 2
                heads = [8 * (r // 2) + par + 2 * i for i in range(4)]
                bank, bkey = (pA, "pA") if par == 0 else (pB, "pB")
                for i, hh in enumerate(heads):
                    g = hh // 2
                    P.op("tensor", lambda h, g=g, par=par, c=c, bank=bank, i=i, kfb=kfb:
                         h.matmul(
                             bank[:, 128 * i:128 * (i + 1)],
                             kfb[64 * par:64 * (par + 1), 128 * g:128 * (g + 1)],
                             q_sb[64 * par:64 * (par + 1),
                                  1024 * g + 128 * c:1024 * g + 128 * (c + 1)],
                             start=True, stop=True, skip_group_check=True),
                         reads=[kkey, f"q{g}"],
                         writes=[bkey] if i in (0, 3) else [],
                         inc=1 if i in (0, 3) else 0)
                wkeys = [SCMR[r]] + (SCMF if c == 0 else [])
                P.op("vector", lambda h, bank=bank, r=r, par=par: h.tensor_tensor(
                    scm[:, :].rearrange("p (s c2) -> p s c2", s=16)[
                        :, 8 * (r // 2) + par:8 * (r // 2) + par + 7:2, :],
                    bank[:, :].rearrange("p (s c2) -> p s c2", s=4),
                    mask[:, :].unsqueeze(1).broadcast_to([128, 4, 128]),
                    ALU.mult),
                     reads=[bkey, "mask"], writes=wkeys, inc=1)

            def num_round(r):
                par = r % 2
                heads = [8 * (r // 2) + par + 2 * i for i in range(4)]
                dkey = "dnE" if c % 2 == 0 else "dnO"
                dbase = 16 * (c % 2)
                for hh in heads:
                    g, p2 = hh // 2, hh % 2
                    nb, nkey, noff = num_region(c, hh)
                    if sc == 0 and c == 0:
                        rhs = sinit_bf[64 * p2:64 * (p2 + 1), 65 * g:65 * (g + 1)]
                        rkey = "sinitb"
                    else:
                        pfo = 520 * c + 65 * g
                        rhs = pf_bf[64 * p2:64 * (p2 + 1), pfo:pfo + 65]
                        rkey = f"pfb{c}"
                    P.op("tensor", lambda h, hh=hh, c=c, nb=nb, noff=noff: h.matmul(
                        nb[:, noff:noff + 64],
                        scm[:, 128 * hh:128 * (hh + 1)],
                        v_sb[:, 1040 * c + 65 * hh:1040 * c + 65 * hh + 64],
                        start=True, stop=False, skip_group_check=True),
                         reads=[SCMR[r], f"v{c}"], writes=[nkey], inc=1)
                    P.op("tensor", lambda h, hh=hh, c=c, dbase=dbase: h.matmul(
                        den_ps[:, dbase + hh:dbase + hh + 1],
                        scm[:, 128 * hh:128 * (hh + 1)],
                        v_sb[:, 1040 * c + 65 * hh + 64:1040 * c + 65 * (hh + 1)],
                        start=True, stop=False, skip_group_check=True),
                         reads=[SCMR[r], f"v{c}"], writes=[dkey], inc=1)
                    P.op("tensor", lambda h, g=g, p2=p2, c=c, nb=nb, noff=noff, rhs=rhs:
                         h.matmul(
                             nb[:, noff:noff + 64],
                             q_sb[64 * p2:64 * (p2 + 1),
                                  1024 * g + 128 * c:1024 * g + 128 * (c + 1)],
                             rhs[:, 0:64], start=False, stop=True, skip_group_check=True),
                         reads=[f"q{g}", rkey], writes=[nkey], inc=1)
                    P.op("tensor", lambda h, g=g, p2=p2, c=c, hh=hh, dbase=dbase, rhs=rhs:
                         h.matmul(
                             den_ps[:, dbase + hh:dbase + hh + 1],
                             q_sb[64 * p2:64 * (p2 + 1),
                                  1024 * g + 128 * c:1024 * g + 128 * (c + 1)],
                             rhs[:, 64:65], start=False, stop=True, skip_group_check=True),
                         reads=[f"q{g}", rkey], writes=[dkey], inc=1)

            scT_round(0)
            scT_round(1)
            num_round(0)
            scT_round(2)
            num_round(1)
            scT_round(3)
            num_round(2)
            num_round(3)

        def b_scan_tail(sc, c):
            ysl = wstg[:, 1024 * (c % 2):1024 * (c % 2 + 1)]
            ykey = f"ytw{c % 2}"
            dkey = "dnE" if c % 2 == 0 else "dnO"
            dbase = 16 * (c % 2)
            P.op("vector", lambda h, dbase=dbase: h.tensor_scalar_add(
                st[:, 32:48], den_ps[:, dbase:dbase + 16], 1e-6),
                 reads=[dkey], writes=["dens2"], inc=1)
            P.op("vector", lambda h: h.reciprocal(st[:, 48:64], st[:, 32:48]),
                 reads=["dens2"], writes=["rec"], inc=1)
            for bh in range(2):
                nb, nkey, _ = num_region(c, 8 * bh)
                P.op("vector", lambda h, nb=nb, bh=bh, ysl=ysl: h.tensor_tensor(
                    ysl[:, 512 * bh:512 * (bh + 1)].rearrange("p (s e) -> p s e", s=8),
                    nb[:, :].rearrange("p (s e) -> p s e", s=8),
                    st[:, 48 + 8 * bh:56 + 8 * bh].unsqueeze(2).broadcast_to([128, 8, 64]),
                    ALU.mult),
                     reads=[nkey, "rec"], writes=[ykey], inc=1)
            for g in range(8):
                P.op("tensor", lambda h, g=g, ysl=ysl: h.transpose(
                    tr1[:, 128 * g:128 * (g + 1)], ysl[:, 128 * g:128 * (g + 1)], ident[:, :]),
                     reads=[ykey, "ident"], writes=["tr1"] if g in (0, 7) else [],
                     inc=1 if g in (0, 7) else 0)
            P.op("scalar", lambda h, c=c: h.copy(
                xnb[:, :].rearrange("p (ec c2) -> p ec c2", ec=8)[:, :, 128 * c:128 * (c + 1)],
                tr1[:, :].rearrange("p (ec c2) -> p ec c2", ec=8)),
                 reads=["tr1"], writes=[f"xnb{c}"], inc=1)

        def b_xpre(sc, c):
            tglob = 8 * sc + c
            slot = c % 4
            dma("sync", xt[:, 1024 * slot:1024 * (slot + 1)],
                x[128 * tglob:128 * (tglob + 1), :], [], [f"xt{slot}"])

        def b_outproj(sc, c):
            tglob = 8 * sc + c
            slot = c % 4
            xts = xt[:, 1024 * slot:1024 * (slot + 1)]
            xkey = f"xt{slot}"
            ot = outt[:, 1024 * (c % 2):1024 * (c % 2 + 1)]
            s2 = 2 + c % 2
            ot2 = outt[:, 1024 * s2:1024 * (s2 + 1)]
            okey, okey2 = f"ot{c % 2}", f"ot{2 + c % 2}"
            for half, (bank, bkey) in enumerate(((pA, "pA"), (pB, "pB"))):
                for e in range(8):
                    P.op("tensor", lambda h, e=e, c=c, bank=bank, half=half: h.matmul(
                        bank[:, :], xnb[:, 1024 * e + 128 * c:1024 * e + 128 * (c + 1)],
                        wprT[:, 1024 * e + 512 * half:1024 * e + 512 * (half + 1)],
                        start=(e == 0), stop=(e == 7), skip_group_check=True),
                         reads=[f"xnb{c}", "w_pr"],
                         writes=[bkey] if e in (0, 7) else [], inc=1 if e in (0, 7) else 0)
                P.op("vector", lambda h, ot=ot, xts=xts, bank=bank, half=half: h.tensor_add(
                    ot[:, 512 * half:512 * (half + 1)], xts[:, 512 * half:512 * (half + 1)],
                    bank[:, :]),
                     reads=[xkey, bkey], writes=[okey], inc=1)
            P.op("gpsimd", lambda h, ot=ot, ot2=ot2, c=c: h.tensor_tensor(
                ot2[:, :], ot[:, :],
                q_sb[:, 8192 + 1024 * c:8192 + 1024 * (c + 1)], ALU.add),
                 reads=[okey, f"ffn{c}"], writes=[okey2], inc=1)
            dma("gpsimd", out[128 * tglob:128 * (tglob + 1), :], ot2[:, :], [okey2], [])

        SC_ORDER = [3, 2, 1, 0]
        for si, sc in enumerate(SC_ORDER):
            if si == 0:
                for i in range(8):
                    b_load_slot(sc, i, pf_only=True)
            b_qproj(sc)
            b_ffn(sc)
            b_ktr(0)
            if si == 0:
                b_xpre(sc, 0)
                b_xpre(sc, 1)
            for c in range(8):
                i2 = 8 * si + c + 2
                if i2 < NT:
                    b_xpre(SC_ORDER[i2 // 8], i2 % 8)
                b_scan_core(sc, c)
                if c + 1 < 8:
                    b_ktr(c + 1)
                b_scan_tail(sc, c)
                b_outproj(sc, c)
                if si + 1 < NSC:
                    b_load_slot(SC_ORDER[si + 1], c)

        for i in range(4):
            for base, eng2 in ((STQ, "gpsimd"), (LD, "sync")):
                sn = f"{base}{i}"
                if sn in P.cnt and P.cnt[sn]:
                    P.raw(eng2, lambda h, sn=sn, v=P.cnt[sn]: h.wait_ge(P.sems[sn], v))
        P.flush()
    return nc


def kernel(L0_attn_x, L0_attn_w_attn, L0_attn_w_proj, L0_ffn_kernel,
           L0_ffn_input_scale, L0_ffn_kernel_scale, L0_ffn_bias):
    x = np.ascontiguousarray(np.asarray(L0_attn_x, dtype=np.float32))
    wat = np.ascontiguousarray(np.asarray(L0_attn_w_attn, dtype=np.float32))
    wpr = np.ascontiguousarray(np.asarray(L0_attn_w_proj, dtype=np.float32))
    fk = np.ascontiguousarray(np.asarray(L0_ffn_kernel))
    fb = np.asarray(L0_ffn_bias)
    s_in = np.asarray(L0_ffn_input_scale, dtype=np.float32).ravel()[0]
    s_k = np.asarray(L0_ffn_kernel_scale, dtype=np.float32).ravel()[0]
    sin_col = np.ascontiguousarray(np.broadcast_to(s_in, (128, 1))).astype(np.float32)
    sk_col = np.ascontiguousarray(np.broadcast_to(s_k, (128, 1))).astype(np.float32)
    fbr = np.ascontiguousarray(np.broadcast_to(fb.reshape(1, E), (128, E)))

    nc = build_nc()
    in_maps = []
    for c in range(NCORES):
        b, half = divmod(c, 2)
        xs = np.ascontiguousarray(x[b, half * T:(half + 1) * T, :])
        mk = np.full((128, 1), 1.0 if half == 1 else 0.0, dtype=np.float32)
        in_maps.append({"x": xs, "wat": wat, "wpr": wpr, "fk": fk,
                        "s_in": sin_col, "s_k": sk_col, "fbr": fbr, "mkv": mk})
    res = run_bass_kernel_spmd(nc, in_maps, core_ids=list(range(NCORES)))
    outp = np.empty((4, 8192, E), dtype=np.float32)
    for c in range(NCORES):
        b, half = divmod(c, 2)
        outp[b, half * T:(half + 1) * T, :] = res.results[c]["out"]
    return outp


# version 09
# speedup vs baseline: 1.1489x; 1.0002x over previous
import numpy as np
import ml_dtypes
import concourse.bass as bass
import concourse.mybir as mybir
import concourse.bass_utils as _bu
from concourse.bass_utils import run_bass_kernel_spmd


F32 = mybir.dt.float32
BF16 = mybir.dt.bfloat16
FP8 = mybir.dt.float8e4
AF = mybir.ActivationFunctionType
ALU = mybir.AluOpType

NCORES = 8
T = 4096
NT = 32
NSC = 4
E = 1024
H = 16

ENGS = ("tensor", "scalar", "vector", "gpsimd", "sync")
import os
FULL_SERIAL = os.environ.get("FULL_SERIAL", "0") == "1"


class Prog:

    def __init__(self, nc):
        self.nc = nc
        self.sems = {}
        self.cnt = {}
        self.items = {e: [] for e in ENGS}
        self.last_w = {}
        self.readers = {}
        self.waited = {e: {} for e in ENGS}
        self.dma_sems = set()

    def sem(self, name):
        if name not in self.sems:
            self.sems[name] = self.nc.alloc_semaphore(name=name)
            self.cnt[name] = 0
        return self.sems[name]

    def _need(self, eng, sem_name, val, waits):
        if val <= 0:
            return
        w = self.waited[eng]
        if w.get(sem_name, 0) >= val:
            return
        w[sem_name] = val
        waits.append((sem_name, val))

    def op(self, eng, fn, reads=(), writes=(), sem_name=None, inc=0):
        if sem_name is None:
            sem_name = "e_" + eng
        self.sem(sem_name)
        waits = []
        if FULL_SERIAL:
            for sname in list(self.cnt):
                self._need(eng, sname, self.cnt[sname], waits)

        def dep(sname, v):
            self._need(eng, sname, v, waits)

        for k in reads:
            lw = self.last_w.get(k)
            if lw:
                dep(lw[0], lw[1])
        for k in writes:
            lw = self.last_w.get(k)
            if lw:
                dep(lw[0], lw[1])
            for sname, v in self.readers.get(k, {}).items():
                dep(sname, v)
        val = None
        if inc:
            self.cnt[sem_name] += inc
            val = self.cnt[sem_name]
        sems = self.sems

        def thunk(h, waits=waits, fn=fn, sem_name=sem_name, inc=inc):
            for sn, v in waits:
                h.wait_ge(sems[sn], v)
            ins = fn(h)
            if inc and ins is not None:
                ins.then_inc(sems[sem_name], inc)

        self.items[eng].append(thunk)
        if inc:
            for k in reads:
                self.readers.setdefault(k, {})[sem_name] = val
            for k in writes:
                self.last_w[k] = (sem_name, val)
                self.readers[k] = {}
        return val

    def raw(self, eng, fn):
        self.items[eng].append(fn)

    def flush(self):
        nc = self.nc
        items = self.items
        with nc.Block() as blk:
            if items["gpsimd"]:
                blk.gpsimd(lambda h, l=tuple(items["gpsimd"]): [f(h) for f in l])
            if items["sync"]:
                blk.sync(lambda h, l=tuple(items["sync"]): [f(h) for f in l])
            if items["tensor"]:
                blk.tensor(lambda h, l=tuple(items["tensor"]): [f(h) for f in l])
            if items["scalar"]:
                blk.scalar(lambda h, l=tuple(items["scalar"]): [f(h) for f in l])
            if items["vector"]:
                blk.vector(lambda h, l=tuple(items["vector"]): [f(h) for f in l])
        self.items = {e: [] for e in ENGS}


def build_nc():
    nc = bass.Bass("TRN2", target_bir_lowering=False)

    x = nc.dram_tensor("x", [T, E], F32, kind="ExternalInput").ap()
    wat = nc.dram_tensor("wat", [3 * E, E], F32, kind="ExternalInput").ap()
    wpr = nc.dram_tensor("wpr", [E, E], F32, kind="ExternalInput").ap()
    fk = nc.dram_tensor("fk", [E, E], BF16, kind="ExternalInput").ap()
    s_in = nc.dram_tensor("s_in", [128, 1], F32, kind="ExternalInput").ap()
    s_k = nc.dram_tensor("s_k", [128, 1], F32, kind="ExternalInput").ap()
    fbr = nc.dram_tensor("fbr", [128, E], BF16, kind="ExternalInput").ap()
    mkv = nc.dram_tensor("mkv", [128, 1], F32, kind="ExternalInput").ap()
    out = nc.dram_tensor("out", [T, E], F32, kind="ExternalOutput").ap()

    d_xnb = nc.dram_tensor("d_xnb", [128, 8, T], BF16).ap()
    d_k = nc.dram_tensor("d_k", [128, NT, 1024], BF16).ap()
    d_v = nc.dram_tensor("d_v", [128, NT, 1040], BF16).ap()
    d_pf = nc.dram_tensor("d_pf", [128, NT, 520], F32).ap()
    cc_in = nc.dram_tensor("cc_in", [1024, 65], F32).ap()
    cc_out = nc.dram_tensor("cc_out", [2048, 65], F32).ap()

    from contextlib import ExitStack
    ctx = ExitStack()
    sbuf = lambda n, s, d: ctx.enter_context(nc.sbuf_tensor(n, s, d))
    psum = lambda n, s, d: ctx.enter_context(nc.psum_tensor(n, s, d))
    with ctx:
        wq_ar = sbuf("wq_ar", [128, 16384], BF16)
        xnb = sbuf("xnb", [128, 8192], BF16)
        watq = sbuf("watq", [128, 8192], BF16)
        wprT = sbuf("wprT", [128, 8192], BF16)
        kq = sbuf("kq", [128, 8192], FP8)
        xq = sbuf("xq", [128, 2048], FP8)
        k_sb = sbuf("k_sb", [128, 8192], BF16)
        v_sb = sbuf("v_sb", [128, 8320], BF16)
        pf_sb = sbuf("pf_sb", [128, 4160], F32)
        pf_bf = sbuf("pf_bf", [128, 4160], BF16)
        carry = sbuf("carry", [128, 520], F32)
        sinit = sbuf("sinit", [128, 520], F32)
        sinit_bf = sbuf("sinit_bf", [128, 520], BF16)
        xt = sbuf("xt", [128, 4096], F32)
        outt = sbuf("outt", [128, 4096], F32)
        scm = sbuf("scm", [128, 2048], BF16)
        kf = sbuf("kf", [128, 1024], BF16)
        yt = sbuf("yt", [128, 1024], BF16)
        wstg = sbuf("wstg", [128, 2048], BF16)
        psc = sbuf("psc", [128, 2048], F32)
        st = sbuf("st", [128, 64], F32)
        cols = sbuf("cols", [128, 8], F32)
        iot = sbuf("iot", [128, 128], F32)
        ident = sbuf("ident", [128, 128], BF16)
        mask = sbuf("mask", [128, 128], BF16)
        fbb = sbuf("fbb", [128, 1024], BF16)

        pA = psum("pA", [128, 512], F32)
        pB = psum("pB", [128, 512], F32)
        pC = psum("pC", [128, 512], F32)
        pD = psum("pD", [128, 512], F32)
        pS = psum("pS", [128, 1024], F32)
        tr1 = psum("tr1", [128, 1024], BF16)
        den_ps = psum("den_ps", [128, 512], F32)

        P = Prog(nc)
        LD, STQ, CC = "q_ld", "q_st", "q_cc"

        rr = {"sync": 0, "gpsimd": 0}

        def dma(queue, dst, src, reads, writes):
            base = LD if queue == "sync" else STQ
            idx = rr[queue]
            rr[queue] = (idx + 1) % 4
            sem_name = f"{base}{idx}"
            P.sem(sem_name)
            prev = P.cnt[sem_name]

            def fn(h, d=dst, s=src, sem_name=sem_name, prev=prev):
                if prev:
                    h.wait_ge(P.sems[sem_name], prev)
                return h.dma_start(d, s)

            P.op(queue, fn, reads=reads, writes=writes, sem_name=sem_name, inc=16)

        P.op("gpsimd", lambda h: h.iota(iot[:, :], [[1, 128]], channel_multiplier=-1,
                                        allow_small_or_imprecise_dtypes=True),
             writes=["iot"], inc=1)
        P.op("vector", lambda h: h.tensor_single_scalar(mask[:, :], iot[:, :], 0.0, ALU.is_ge),
             reads=["iot"], writes=["mask"], inc=1)
        P.op("vector", lambda h: h.tensor_single_scalar(ident[:, :], iot[:, :], 0.0, ALU.is_equal),
             reads=["iot"], writes=["ident"], inc=1)
        P.op("vector", lambda h: h.memset(cols[:, 0:1], -0.5), writes=["c_nh"], inc=1)
        P.op("vector", lambda h: h.memset(cols[:, 5:6], 1e-5), writes=["c_eps"], inc=1)
        dma("sync", st[:, 0:1], s_in, [], ["st_sc"])
        dma("sync", st[:, 1:2], s_k, [], ["st_sc"])
        dma("sync", st[:, 2:3], mkv, [], ["st_mk"])
        dma("sync", fbb[:, :], fbr, [], ["fbb"])
        P.op("vector", lambda h: h.reciprocal(cols[:, 1:2], st[:, 0:1]),
             reads=["st_sc"], writes=["c_si"], inc=1)
        P.op("vector", lambda h: h.reciprocal(cols[:, 2:3], st[:, 1:2]),
             reads=["st_sc"], writes=["c_sk"], inc=1)
        P.op("vector", lambda h: h.tensor_mul(cols[:, 3:4], st[:, 0:1], st[:, 1:2]),
             reads=["st_sc"], writes=["c_sp"], inc=1)

        def w_dma(kind, r, slot):
            if kind == "kv":
                srcw = wat[1024 + 128 * r:1024 + 128 * (r + 1), :]
            elif kind == "q":
                srcw = wat[128 * r:128 * (r + 1), :]
            elif kind == "fk":
                srcw = fk[128 * r:128 * (r + 1), :]
            else:
                srcw = wpr[128 * r:128 * (r + 1), :]
            dma("gpsimd", wstg[:, 1024 * slot:1024 * (slot + 1)], srcw, [], [f"wstg{slot}"])

        def w_slice(kind, r, trp, trkey, slot):
            stg = wstg[:, 1024 * slot:1024 * (slot + 1)]
            skey = f"wstg{slot}"
            for e in range(8):
                P.op("tensor",
                     lambda h, e=e, stg=stg, trp=trp: h.transpose(
                         trp[:, 128 * e:128 * (e + 1)], stg[:, 128 * e:128 * (e + 1)],
                         ident[:, :]),
                     reads=[skey, "ident"],
                     writes=[trkey] if e in (0, 7) else [],
                     inc=1 if e in (0, 7) else 0)
            dstt, stride, dkey = {"kv": (wq_ar, 2048, "w_kv"),
                                  "q": (watq, 1024, "w_q"),
                                  "pr": (wprT, 1024, "w_pr")}[kind]
            base = 128 * r
            P.op("vector",
                 lambda h, trp=trp, dstt=dstt, stride=stride, base=base: h.tensor_copy(
                     dstt[:, :].rearrange("p (ec c) -> p ec c", c=stride)[:, :, base:base + 128],
                     trp[:, :].rearrange("p (ec c) -> p ec c", ec=8)),
                 reads=[trkey], writes=[dkey], inc=1)

        WPLAN = ([("kv", r) for r in range(16)] + [("q", r) for r in range(8)]
                 + [("pr", r) for r in range(8)] + [("fk", r) for r in range(8)])
        w_dma(*WPLAN[0], 0)
        w_dma(*WPLAN[1], 1)
        for r in range(16):
            w_slice("kv", r, tr1, "tr1", r % 2)
            if r + 2 < len(WPLAN):
                w_dma(*WPLAN[r + 2], (r + 2) % 2)

        kvbank = [(pA, "pA"), (pB, "pB"), (pC, "pC"), (pD, "pD")]

        def s_region(hh):
            g, par = hh // 2, hh % 2
            off = 65 * g if g < 7 else 512 + 65 * (g - 7)
            key = "pS0" if g < 7 else "pS1"
            return pS[64 * par:64 * (par + 1), off:off + 65], key, (0, 64 * par)

        def pf_col(g):
            return 65 * g

        def ln_a(t):
            xslot = t % 4
            xts = xt[:, 1024 * xslot:1024 * (xslot + 1)]
            xkey = f"xt{xslot}"
            dma("sync", xts, x[128 * t:128 * (t + 1), :], [], [xkey])
            P.op("vector", lambda h, xts=xts: h.bn_stats(st[:, 16:22], xts[:, 0:512]),
                 reads=[xkey], writes=["stA"], inc=1)
            P.op("vector", lambda h, xts=xts: h.bn_stats(st[:, 22:28], xts[:, 512:1024]),
                 reads=[xkey], writes=["stA"], inc=1)
            P.op("vector", lambda h: h.bn_aggr(st[:, 12:14], st[:, 16:28]),
                 reads=["stA"], writes=["stB"], inc=1)
            P.op("scalar", lambda h: h.activation(st[:, 15:16], st[:, 13:14], AF.Ln,
                                                  bias=cols[:, 5:6]),
                 reads=["stB", "c_eps"], writes=["stD"], inc=1)
            P.op("scalar", lambda h: h.activation(st[:, 16:17], st[:, 15:16], AF.Exp,
                                                  scale=cols[:, 0:1]),
                 reads=["stD", "c_nh"], writes=["stE"], inc=1)

        def ln_b(t):
            slot = t % 2
            xslot = t % 4
            xts = xt[:, 1024 * xslot:1024 * (xslot + 1)]
            P.op("vector", lambda h, xts=xts, slot=slot: h.tensor_scalar(
                scm[:, 1024 * slot:1024 * (slot + 1)], xts, st[:, 12:13], st[:, 16:17],
                ALU.subtract, ALU.mult),
                 reads=[f"xt{xslot}", "stB", "stE"], writes=[f"ytm{slot}"], inc=1)

        def heavy_stage(t):
            sc, tl = divmod(t, 8)
            slot = t % 2
            ytm = scm[:, 1024 * slot:1024 * (slot + 1)]
            ykey = f"ytm{slot}"
            for e in range(8):
                P.op("tensor", lambda h, e=e, ytm=ytm: h.transpose(
                    tr1[:, 128 * e:128 * (e + 1)], ytm[:, 128 * e:128 * (e + 1)], ident[:, :]),
                     reads=[ykey, "ident"], writes=["tr1"] if e in (0, 7) else [],
                     inc=1 if e in (0, 7) else 0)
            P.op("scalar", lambda h, tl=tl: h.copy(
                xnb[:, :].rearrange("p (ec c) -> p ec c", ec=8)[:, :, 128 * tl:128 * (tl + 1)],
                tr1[:, :].rearrange("p (ec c) -> p ec c", ec=8)),
                 reads=["tr1"], writes=[f"xnb{tl}"], inc=1)
            for n in range(4):
                bank, bkey = kvbank[n]
                for e in range(8):
                    P.op("tensor", lambda h, e=e, n=n, tl=tl, bank=bank: h.matmul(
                        bank[:, :],
                        xnb[:, 1024 * e + 128 * tl:1024 * e + 128 * (tl + 1)],
                        wq_ar[:, 2048 * e + 512 * n:2048 * e + 512 * (n + 1)],
                        start=(e == 0), stop=(e == 7), skip_group_check=True),
                         reads=[f"xnb{tl}", "w_kv"],
                         writes=[bkey] if e in (0, 7) else [],
                         inc=1 if e in (0, 7) else 0)
                if n < 2:
                    pe = psc[:, 1024 * (n % 2):1024 * (n % 2) + 512]
                    pr = psc[:, 1024 * (n % 2) + 512:1024 * (n % 2) + 1024]
                    pky = f"psc{n % 2}"
                    P.op("scalar", lambda h, bank=bank, pe=pe: h.activation(pe, bank[:, :], AF.Exp),
                         reads=[bkey], writes=[pky + "e"], inc=1)
                    P.op("scalar", lambda h, bank=bank, pr=pr: h.activation(pr, bank[:, :], AF.Relu),
                         reads=[bkey], writes=[pky + "r"], inc=1)
                    P.op("vector", lambda h, pe=pe, pr=pr, tl=tl, n=n: h.scalar_tensor_tensor(
                        k_sb[:, 1024 * tl + 512 * n:1024 * tl + 512 * (n + 1)],
                        pe, 1.0, pr, ALU.min, ALU.add),
                         reads=[pky + "e", pky + "r", bkey], writes=[f"k{tl}"], inc=1)
                else:
                    hh0 = 8 * (n - 2)
                    P.op("vector", lambda h, bank=bank, tl=tl, hh0=hh0: h.tensor_copy(
                        v_sb[:, 1040 * tl:1040 * (tl + 1)].rearrange(
                            "p (hh ee) -> p hh ee", hh=16)[:, hh0:hh0 + 8, 0:64],
                        bank[:, :].rearrange("p (hh ee) -> p hh ee", hh=8)),
                         reads=[bkey], writes=[f"v{tl}"], inc=1)
            P.op("vector", lambda h, tl=tl: h.memset(
                v_sb[:, 1040 * tl:1040 * (tl + 1)].rearrange(
                    "p (hh ee) -> p hh ee", hh=16)[:, :, 64:65], 1.0),
                 writes=[f"v{tl}"], inc=1)
            for hh in range(H):
                sl, skey, tp = s_region(hh)
                P.op("tensor", lambda h, hh=hh, tl=tl, sl=sl, tp=tp: h.matmul(
                    sl,
                    k_sb[:, 1024 * tl + 64 * hh:1024 * tl + 64 * (hh + 1)],
                    v_sb[:, 1040 * tl + 65 * hh:1040 * tl + 65 * (hh + 1)],
                    start=True, stop=True, tile_position=tp, skip_group_check=True),
                     reads=[f"k{tl}", f"v{tl}"],
                     writes=[skey] if hh in (0, 1, 12, 13, 14, 15) else [],
                     inc=1 if hh in (0, 1, 12, 13, 14, 15) else 0)
            if t == 0:
                P.op("vector", lambda h: h.tensor_copy(pf_sb[:, 0:455], pS[:, 0:455]),
                     reads=["pS0"], writes=["pf0"], inc=1)
                P.op("vector", lambda h: h.tensor_copy(pf_sb[:, 455:520], pS[:, 512:577]),
                     reads=["pS1"], writes=["pf0"], inc=1)
            else:
                prev = carry if tl == 0 else pf_sb[:, 520 * (tl - 1):520 * tl]
                pkey = "carry" if tl == 0 else f"pf{tl - 1}"
                P.op("vector", lambda h, tl=tl, prev=prev: h.tensor_add(
                    pf_sb[:, 520 * tl:520 * tl + 455], prev[:, 0:455], pS[:, 0:455]),
                     reads=["pS0", pkey], writes=[f"pf{tl}"], inc=1)
                P.op("vector", lambda h, tl=tl, prev=prev: h.tensor_add(
                    pf_sb[:, 520 * tl + 455:520 * tl + 520], prev[:, 455:520], pS[:, 512:577]),
                     reads=["pS1", pkey], writes=[f"pf{tl}"], inc=1)
            if sc < 3:
                dma("gpsimd", d_xnb[:, :, 1024 * sc + 128 * tl:1024 * sc + 128 * (tl + 1)],
                    xnb[:, :].rearrange("p (ec c) -> p ec c", ec=8)[:, :, 128 * tl:128 * (tl + 1)],
                    [f"xnb{tl}"], [])
                dma("gpsimd", d_k[:, t, :], k_sb[:, 1024 * tl:1024 * (tl + 1)],
                    [f"k{tl}"], [])
                dma("gpsimd", d_v[:, t, :], v_sb[:, 1040 * tl:1040 * (tl + 1)],
                    [f"v{tl}"], [])
            dma("gpsimd", d_pf[:, t, :], pf_sb[:, 520 * tl:520 * (tl + 1)],
                [f"pf{tl}"], [])
            if tl == 7:
                P.op("vector", lambda h: h.tensor_copy(carry[:, :], pf_sb[:, 520 * 7:520 * 8]),
                     reads=["pf7"], writes=["carry"], inc=1)

        ln_a(0)
        ln_b(0)
        for t in range(NT):
            if t + 1 < NT:
                ln_a(t + 1)
            heavy_stage(t)
            if t + 1 < NT:
                ln_b(t + 1)
            wi = t + 16
            if t < 8:
                w_slice("q", t, tr1, "tr1", wi % 2)
            elif t < 16:
                w_slice("pr", t - 8, tr1, "tr1", wi % 2)
            elif t < 24:
                e = t - 16
                slot = wi % 2
                stg = wstg[:, 1024 * slot:1024 * (slot + 1)]
                skey = f"wstg{slot}"
                P.op("scalar",
                     lambda h, e=e, stg=stg: h.activation(
                         kq[:, 1024 * e:1024 * (e + 1)], stg, AF.Copy, scale=cols[:, 2:3]),
                     reads=[skey, "c_sk"], writes=["kq"], inc=1)
            if wi + 2 < len(WPLAN):
                w_dma(*WPLAN[wi + 2], (wi + 2) % 2)

        dma("gpsimd", cc_in.rearrange("(g p) e -> p g e", p=128),
            pf_sb[:, 520 * 7:520 * 8].rearrange("p (g e) -> p g e", g=8),
            ["pf7"], ["cc_in"])
        P.op("gpsimd", lambda h: h.collective_compute(
            "AllGather", ALU.bypass,
            replica_groups=[[0, 1], [2, 3], [4, 5], [6, 7]],
            ins=[cc_in], outs=[cc_out]),
             reads=["cc_in"], writes=["cc_out"], sem_name=CC, inc=1)
        dma("sync", sinit[:, :].rearrange("p (g e) -> p g e", g=8),
            cc_out[0:1024, :].rearrange("(g p) e -> p g e", p=128),
            ["cc_out"], ["sinit_r"])
        P.op("vector", lambda h: h.tensor_scalar_mul(sinit[:, :], sinit[:, :], st[:, 2:3]),
             reads=["sinit_r", "st_mk"], writes=["sinit"], inc=1)
        P.op("vector", lambda h: h.tensor_copy(sinit_bf[:, :], sinit[:, :]),
             reads=["sinit"], writes=["sinitb"], inc=1)

        q_sb = wq_ar
        xnb_keys = [f"xnb{i}" for i in range(8)]
        SCMF = ["scmF0", "scmF1", "scmF2", "scmF3"]
        SCMR = ["scmR0", "scmR1", "scmR2", "scmR3"]

        def b_load_slot(sc, i, skip_xnb=False, pf_only=False):
            if not pf_only:
                if not skip_xnb:
                    dma("sync",
                        xnb[:, :].rearrange("p (ec c) -> p ec c", ec=8)[:, :, 128 * i:128 * (i + 1)],
                        d_xnb[:, :, 1024 * sc + 128 * i:1024 * sc + 128 * (i + 1)],
                        [], [f"xnb{i}"])
                dma("sync", k_sb[:, 1024 * i:1024 * (i + 1)], d_k[:, 8 * sc + i, :],
                    [], [f"k{i}"])
                dma("sync", v_sb[:, 1040 * i:1040 * (i + 1)], d_v[:, 8 * sc + i, :],
                    [], [f"v{i}"])
            cg = 8 * sc + i - 1
            if cg >= 0:
                dma("sync", pf_sb[:, 520 * i:520 * (i + 1)], d_pf[:, cg, :],
                    [], [f"pf{i}"])
                P.op("vector", lambda h, i=i: h.tensor_add(
                    pf_sb[:, 520 * i:520 * (i + 1)], pf_sb[:, 520 * i:520 * (i + 1)],
                    sinit[:, :]),
                     reads=["sinit", f"pf{i}"], writes=[f"pf{i}"], inc=1)
                P.op("scalar", lambda h, i=i: h.copy(
                    pf_bf[:, 520 * i:520 * (i + 1)], pf_sb[:, 520 * i:520 * (i + 1)]),
                     reads=[f"pf{i}"], writes=[f"pfb{i}"], inc=1)

        def b_qproj(sc):
            for qc in range(8):
                lob, lkey = (pA, "pA") if qc % 2 == 0 else (pC, "pC")
                hib, hkey = (pB, "pB") if qc % 2 == 0 else (pD, "pD")
                for e in range(8):
                    P.op("tensor", lambda h, e=e, qc=qc, lob=lob: h.matmul(
                        lob[:, :], watq[:, 1024 * e + 128 * qc:1024 * e + 128 * (qc + 1)],
                        xnb[:, 1024 * e:1024 * e + 512],
                        start=(e == 0), stop=(e == 7), skip_group_check=True),
                         reads=["w_q"] + xnb_keys[0:4],
                         writes=[lkey] if e in (0, 7) else [], inc=1 if e in (0, 7) else 0)
                for e in range(8):
                    P.op("tensor", lambda h, e=e, qc=qc, hib=hib: h.matmul(
                        hib[:, :], watq[:, 1024 * e + 128 * qc:1024 * e + 128 * (qc + 1)],
                        xnb[:, 1024 * e + 512:1024 * e + 1024],
                        start=(e == 0), stop=(e == 7), skip_group_check=True),
                         reads=["w_q"] + xnb_keys[4:8],
                         writes=[hkey] if e in (0, 7) else [], inc=1 if e in (0, 7) else 0)
                for half, (bank, bkey) in enumerate(((lob, lkey), (hib, hkey))):
                    pe = psc[:, 1024 * (qc % 2):1024 * (qc % 2) + 512]
                    pr = psc[:, 1024 * (qc % 2) + 512:1024 * (qc % 2) + 1024]
                    pky = f"psc{qc % 2}"
                    P.op("scalar", lambda h, bank=bank, pe=pe: h.activation(pe, bank[:, :], AF.Exp),
                         reads=[bkey], writes=[pky + "e"], inc=1)
                    P.op("scalar", lambda h, bank=bank, pr=pr: h.activation(pr, bank[:, :], AF.Relu),
                         reads=[bkey], writes=[pky + "r"], inc=1)
                    P.op("vector", lambda h, pe=pe, pr=pr, qc=qc, half=half: h.scalar_tensor_tensor(
                        q_sb[:, 1024 * qc + 512 * half:1024 * qc + 512 * (half + 1)],
                        pe, 1.0, pr, ALU.min, ALU.add),
                         reads=[pky + "e", pky + "r", bkey], writes=[f"q{qc}"], inc=1)

        def b_ffn_quant(tt):
            xqs = xq[:, 1024 * (tt % 2):1024 * (tt % 2 + 1)]
            P.op("scalar", lambda h, tt=tt, xqs=xqs: h.activation(
                xqs.rearrange("p (ec c) -> p ec c", ec=8),
                xnb[:, :].rearrange("p (ec c) -> p ec c", ec=8)[:, :, 128 * tt:128 * (tt + 1)],
                AF.Copy, scale=cols[:, 1:2]),
                 reads=[f"xnb{tt}", "c_si"], writes=[f"xq{tt % 2}"], inc=1)

        def b_ffn(sc):
            b_ffn_quant(0)
            for tt in range(8):
                xqs = xq[:, 1024 * (tt % 2):1024 * (tt % 2 + 1)]
                xqkey = f"xq{tt % 2}"
                if tt + 1 < 8:
                    b_ffn_quant(tt + 1)
                lob, lkey = (pA, "pA") if tt % 2 == 0 else (pC, "pC")
                hib, hkey = (pB, "pB") if tt % 2 == 0 else (pD, "pD")
                for e in range(8):
                    P.op("tensor", lambda h, e=e, xqs=xqs, lob=lob: h.matmul(
                        lob[:, :], xqs[:, 128 * e:128 * (e + 1)],
                        kq[:, 1024 * e:1024 * e + 512],
                        start=(e == 0), stop=(e == 7), skip_group_check=True),
                         reads=[xqkey, "kq"],
                         writes=[lkey] if e in (0, 7) else [], inc=1 if e in (0, 7) else 0)
                    P.op("tensor", lambda h, e=e, xqs=xqs, hib=hib: h.matmul(
                        hib[:, :], xqs[:, 128 * e:128 * (e + 1)],
                        kq[:, 1024 * e + 512:1024 * e + 1024],
                        start=(e == 0), stop=(e == 7), skip_group_check=True),
                         reads=[xqkey, "kq"],
                         writes=[hkey] if e in (0, 7) else [], inc=1 if e in (0, 7) else 0)
                for half, (bank, bkey) in enumerate(((lob, lkey), (hib, hkey))):
                    j = 2 * (tt % 2) + half
                    dst = scm[:, 512 * j:512 * (j + 1)]
                    P.op("scalar", lambda h, bank=bank, dst=dst: h.activation(
                        dst, bank[:, :], AF.Copy, scale=cols[:, 3:4]),
                         reads=[bkey, "c_sp"], writes=[f"scmF{j}"] + SCMR, inc=1)
                    P.op("vector", lambda h, dst=dst, tt=tt, half=half: h.tensor_add(
                        q_sb[:, 8192 + 1024 * tt + 512 * half:8192 + 1024 * tt + 512 * (half + 1)],
                        dst, fbb[:, 512 * half:512 * (half + 1)]),
                         reads=[f"scmF{j}", "fbb"], writes=[f"ffn{tt}"], inc=1)

        def b_ktr(c):
            kfb = kf if c % 2 == 0 else yt
            for g in range(8):
                P.op("tensor", lambda h, g=g, c=c: h.transpose(
                    tr1[:, 128 * g:128 * (g + 1)],
                    k_sb[:, 1024 * c + 128 * g:1024 * c + 128 * (g + 1)], ident[:, :]),
                     reads=[f"k{c}", "ident"], writes=["tr1"] if g in (0, 7) else [],
                     inc=1 if g in (0, 7) else 0)
            P.op("scalar", lambda h, kfb=kfb: h.copy(kfb[:, :], tr1[:, :]),
                 reads=["tr1"], writes=[f"kf{c % 2}"], inc=1)

        def num_region(c, hh):
            if c % 2 == 0:
                nb, nkey = (pC, "pC") if hh < 8 else (pD, "pD")
            else:
                nb, nkey = (pS[:, 0:512], "pS0") if hh < 8 else (pS[:, 512:1024], "pS1")
            return nb, nkey, 64 * (hh % 8)

        def b_scan_core(sc, c, filler=None):
            kfb = kf if c % 2 == 0 else yt
            kkey = f"kf{c % 2}"
            dkey = "dnE" if c % 2 == 0 else "dnO"
            dbase = 16 * (c % 2)
            def scT_round(r):
                par = r % 2
                heads = [8 * (r // 2) + par + 2 * i for i in range(4)]
                bank, bkey = (pA, "pA") if par == 0 else (pB, "pB")
                for i, hh in enumerate(heads):
                    g = hh // 2
                    P.op("tensor", lambda h, g=g, par=par, c=c, bank=bank, i=i, kfb=kfb:
                         h.matmul(
                             bank[:, 128 * i:128 * (i + 1)],
                             kfb[64 * par:64 * (par + 1), 128 * g:128 * (g + 1)],
                             q_sb[64 * par:64 * (par + 1),
                                  1024 * g + 128 * c:1024 * g + 128 * (c + 1)],
                             start=True, stop=True, skip_group_check=True),
                         reads=[kkey, f"q{g}"],
                         writes=[bkey] if i in (0, 3) else [],
                         inc=1 if i in (0, 3) else 0)
                wkeys = [SCMR[r]] + (SCMF if c == 0 else [])
                P.op("vector", lambda h, bank=bank, r=r, par=par: h.tensor_tensor(
                    scm[:, :].rearrange("p (s c2) -> p s c2", s=16)[
                        :, 8 * (r // 2) + par:8 * (r // 2) + par + 7:2, :],
                    bank[:, :].rearrange("p (s c2) -> p s c2", s=4),
                    mask[:, :].unsqueeze(1).broadcast_to([128, 4, 128]),
                    ALU.mult),
                     reads=[bkey, "mask"], writes=wkeys, inc=1)

            def num_round(r):
                par = r % 2
                heads = [8 * (r // 2) + par + 2 * i for i in range(4)]
                dkey = "dnE" if c % 2 == 0 else "dnO"
                dbase = 16 * (c % 2)
                for hh in heads:
                    g, p2 = hh // 2, hh % 2
                    nb, nkey, noff = num_region(c, hh)
                    if sc == 0 and c == 0:
                        rhs = sinit_bf[64 * p2:64 * (p2 + 1), 65 * g:65 * (g + 1)]
                        rkey = "sinitb"
                    else:
                        pfo = 520 * c + 65 * g
                        rhs = pf_bf[64 * p2:64 * (p2 + 1), pfo:pfo + 65]
                        rkey = f"pfb{c}"
                    P.op("tensor", lambda h, hh=hh, c=c, nb=nb, noff=noff: h.matmul(
                        nb[:, noff:noff + 64],
                        scm[:, 128 * hh:128 * (hh + 1)],
                        v_sb[:, 1040 * c + 65 * hh:1040 * c + 65 * hh + 64],
                        start=True, stop=False, skip_group_check=True),
                         reads=[SCMR[r], f"v{c}"], writes=[nkey], inc=1)
                    P.op("tensor", lambda h, hh=hh, c=c, dbase=dbase: h.matmul(
                        den_ps[:, dbase + hh:dbase + hh + 1],
                        scm[:, 128 * hh:128 * (hh + 1)],
                        v_sb[:, 1040 * c + 65 * hh + 64:1040 * c + 65 * (hh + 1)],
                        start=True, stop=False, skip_group_check=True),
                         reads=[SCMR[r], f"v{c}"], writes=[dkey], inc=1)
                    P.op("tensor", lambda h, g=g, p2=p2, c=c, nb=nb, noff=noff, rhs=rhs:
                         h.matmul(
                             nb[:, noff:noff + 64],
                             q_sb[64 * p2:64 * (p2 + 1),
                                  1024 * g + 128 * c:1024 * g + 128 * (c + 1)],
                             rhs[:, 0:64], start=False, stop=True, skip_group_check=True),
                         reads=[f"q{g}", rkey], writes=[nkey], inc=1)
                    P.op("tensor", lambda h, g=g, p2=p2, c=c, hh=hh, dbase=dbase, rhs=rhs:
                         h.matmul(
                             den_ps[:, dbase + hh:dbase + hh + 1],
                             q_sb[64 * p2:64 * (p2 + 1),
                                  1024 * g + 128 * c:1024 * g + 128 * (c + 1)],
                             rhs[:, 64:65], start=False, stop=True, skip_group_check=True),
                         reads=[f"q{g}", rkey], writes=[dkey], inc=1)

            scT_round(0)
            scT_round(1)
            if filler is not None:
                filler()
            num_round(0)
            scT_round(2)
            num_round(1)
            scT_round(3)
            num_round(2)
            num_round(3)

        def b_scan_tail(sc, c):
            ysl = wstg[:, 1024 * (c % 2):1024 * (c % 2 + 1)]
            ykey = f"ytw{c % 2}"
            dkey = "dnE" if c % 2 == 0 else "dnO"
            dbase = 16 * (c % 2)
            P.op("vector", lambda h, dbase=dbase: h.tensor_scalar_add(
                st[:, 32:48], den_ps[:, dbase:dbase + 16], 1e-6),
                 reads=[dkey], writes=["dens2"], inc=1)
            P.op("vector", lambda h: h.reciprocal(st[:, 48:64], st[:, 32:48]),
                 reads=["dens2"], writes=["rec"], inc=1)
            for bh in range(2):
                nb, nkey, _ = num_region(c, 8 * bh)
                P.op("vector", lambda h, nb=nb, bh=bh, ysl=ysl: h.tensor_tensor(
                    ysl[:, 512 * bh:512 * (bh + 1)].rearrange("p (s e) -> p s e", s=8),
                    nb[:, :].rearrange("p (s e) -> p s e", s=8),
                    st[:, 48 + 8 * bh:56 + 8 * bh].unsqueeze(2).broadcast_to([128, 8, 64]),
                    ALU.mult),
                     reads=[nkey, "rec"], writes=[ykey], inc=1)
            for g in range(8):
                P.op("tensor", lambda h, g=g, ysl=ysl: h.transpose(
                    tr1[:, 128 * g:128 * (g + 1)], ysl[:, 128 * g:128 * (g + 1)], ident[:, :]),
                     reads=[ykey, "ident"], writes=["tr1"] if g in (0, 7) else [],
                     inc=1 if g in (0, 7) else 0)
            P.op("scalar", lambda h, c=c: h.copy(
                xnb[:, :].rearrange("p (ec c2) -> p ec c2", ec=8)[:, :, 128 * c:128 * (c + 1)],
                tr1[:, :].rearrange("p (ec c2) -> p ec c2", ec=8)),
                 reads=["tr1"], writes=[f"xnb{c}"], inc=1)

        def b_xpre(sc, c):
            tglob = 8 * sc + c
            slot = c % 4
            dma("sync", xt[:, 1024 * slot:1024 * (slot + 1)],
                x[128 * tglob:128 * (tglob + 1), :], [], [f"xt{slot}"])

        def b_outproj(sc, c):
            tglob = 8 * sc + c
            slot = c % 4
            xts = xt[:, 1024 * slot:1024 * (slot + 1)]
            xkey = f"xt{slot}"
            ot = outt[:, 1024 * (c % 2):1024 * (c % 2 + 1)]
            s2 = 2 + c % 2
            ot2 = outt[:, 1024 * s2:1024 * (s2 + 1)]
            okey, okey2 = f"ot{c % 2}", f"ot{2 + c % 2}"
            for half, (bank, bkey) in enumerate(((pA, "pA"), (pB, "pB"))):
                for e in range(8):
                    P.op("tensor", lambda h, e=e, c=c, bank=bank, half=half: h.matmul(
                        bank[:, :], xnb[:, 1024 * e + 128 * c:1024 * e + 128 * (c + 1)],
                        wprT[:, 1024 * e + 512 * half:1024 * e + 512 * (half + 1)],
                        start=(e == 0), stop=(e == 7), skip_group_check=True),
                         reads=[f"xnb{c}", "w_pr"],
                         writes=[bkey] if e in (0, 7) else [], inc=1 if e in (0, 7) else 0)
                P.op("vector", lambda h, ot=ot, xts=xts, bank=bank, half=half: h.tensor_add(
                    ot[:, 512 * half:512 * (half + 1)], xts[:, 512 * half:512 * (half + 1)],
                    bank[:, :]),
                     reads=[xkey, bkey], writes=[okey], inc=1)
            P.op("gpsimd", lambda h, ot=ot, ot2=ot2, c=c: h.tensor_tensor(
                ot2[:, :], ot[:, :],
                q_sb[:, 8192 + 1024 * c:8192 + 1024 * (c + 1)], ALU.add),
                 reads=[okey, f"ffn{c}"], writes=[okey2], inc=1)
            dma("gpsimd", out[128 * tglob:128 * (tglob + 1), :], ot2[:, :], [okey2], [])

        SC_ORDER = [3, 2, 1, 0]
        for si, sc in enumerate(SC_ORDER):
            if si == 0:
                for i in range(8):
                    b_load_slot(sc, i, pf_only=True)
            b_qproj(sc)
            b_ffn(sc)
            b_ktr(0)
            if si == 0:
                b_xpre(sc, 0)
                b_xpre(sc, 1)
            for c in range(8):
                i2 = 8 * si + c + 2
                if i2 < NT:
                    b_xpre(SC_ORDER[i2 // 8], i2 % 8)
                b_scan_core(sc, c,
                            filler=(lambda cc=c + 1: b_ktr(cc)) if c + 1 < 8 else None)
                b_scan_tail(sc, c)
                b_outproj(sc, c)
                if si + 1 < NSC:
                    b_load_slot(SC_ORDER[si + 1], c)

        for i in range(4):
            for base, eng2 in ((STQ, "gpsimd"), (LD, "sync")):
                sn = f"{base}{i}"
                if sn in P.cnt and P.cnt[sn]:
                    P.raw(eng2, lambda h, sn=sn, v=P.cnt[sn]: h.wait_ge(P.sems[sn], v))
        P.flush()
    return nc


def kernel(L0_attn_x, L0_attn_w_attn, L0_attn_w_proj, L0_ffn_kernel,
           L0_ffn_input_scale, L0_ffn_kernel_scale, L0_ffn_bias):
    x = np.ascontiguousarray(np.asarray(L0_attn_x, dtype=np.float32))
    wat = np.ascontiguousarray(np.asarray(L0_attn_w_attn, dtype=np.float32))
    wpr = np.ascontiguousarray(np.asarray(L0_attn_w_proj, dtype=np.float32))
    fk = np.ascontiguousarray(np.asarray(L0_ffn_kernel))
    fb = np.asarray(L0_ffn_bias)
    s_in = np.asarray(L0_ffn_input_scale, dtype=np.float32).ravel()[0]
    s_k = np.asarray(L0_ffn_kernel_scale, dtype=np.float32).ravel()[0]
    sin_col = np.ascontiguousarray(np.broadcast_to(s_in, (128, 1))).astype(np.float32)
    sk_col = np.ascontiguousarray(np.broadcast_to(s_k, (128, 1))).astype(np.float32)
    fbr = np.ascontiguousarray(np.broadcast_to(fb.reshape(1, E), (128, E)))

    nc = build_nc()
    in_maps = []
    for c in range(NCORES):
        b, half = divmod(c, 2)
        xs = np.ascontiguousarray(x[b, half * T:(half + 1) * T, :])
        mk = np.full((128, 1), 1.0 if half == 1 else 0.0, dtype=np.float32)
        in_maps.append({"x": xs, "wat": wat, "wpr": wpr, "fk": fk,
                        "s_in": sin_col, "s_k": sk_col, "fbr": fbr, "mkv": mk})
    res = run_bass_kernel_spmd(nc, in_maps, core_ids=list(range(NCORES)))
    outp = np.empty((4, 8192, E), dtype=np.float32)
    for c in range(NCORES):
        b, half = divmod(c, 2)
        outp[b, half * T:(half + 1) * T, :] = res.results[c]["out"]
    return outp


# version 10
# speedup vs baseline: 1.1643x; 1.0134x over previous
import numpy as np
import ml_dtypes
import concourse.bass as bass
import concourse.mybir as mybir
import concourse.bass_utils as _bu
from concourse.bass_utils import run_bass_kernel_spmd


F32 = mybir.dt.float32
BF16 = mybir.dt.bfloat16
FP8 = mybir.dt.float8e4
AF = mybir.ActivationFunctionType
ALU = mybir.AluOpType

NCORES = 8
T = 4096
NT = 32
NSC = 4
E = 1024
H = 16

ENGS = ("tensor", "scalar", "vector", "gpsimd", "sync")
import os
FULL_SERIAL = os.environ.get("FULL_SERIAL", "0") == "1"


class Prog:

    def __init__(self, nc):
        self.nc = nc
        self.sems = {}
        self.cnt = {}
        self.items = {e: [] for e in ENGS}
        self.last_w = {}
        self.readers = {}
        self.waited = {e: {} for e in ENGS}
        self.dma_sems = set()

    def sem(self, name):
        if name not in self.sems:
            self.sems[name] = self.nc.alloc_semaphore(name=name)
            self.cnt[name] = 0
        return self.sems[name]

    def _need(self, eng, sem_name, val, waits):
        if val <= 0:
            return
        w = self.waited[eng]
        if w.get(sem_name, 0) >= val:
            return
        w[sem_name] = val
        waits.append((sem_name, val))

    def op(self, eng, fn, reads=(), writes=(), sem_name=None, inc=0):
        if sem_name is None:
            sem_name = "e_" + eng
        self.sem(sem_name)
        waits = []
        if FULL_SERIAL:
            for sname in list(self.cnt):
                self._need(eng, sname, self.cnt[sname], waits)

        def dep(sname, v):
            self._need(eng, sname, v, waits)

        for k in reads:
            lw = self.last_w.get(k)
            if lw:
                dep(lw[0], lw[1])
        for k in writes:
            lw = self.last_w.get(k)
            if lw:
                dep(lw[0], lw[1])
            for sname, v in self.readers.get(k, {}).items():
                dep(sname, v)
        val = None
        if inc:
            self.cnt[sem_name] += inc
            val = self.cnt[sem_name]
        sems = self.sems

        def thunk(h, waits=waits, fn=fn, sem_name=sem_name, inc=inc):
            for sn, v in waits:
                h.wait_ge(sems[sn], v)
            ins = fn(h)
            if inc and ins is not None:
                ins.then_inc(sems[sem_name], inc)

        self.items[eng].append(thunk)
        if inc:
            for k in reads:
                self.readers.setdefault(k, {})[sem_name] = val
            for k in writes:
                self.last_w[k] = (sem_name, val)
                self.readers[k] = {}
        return val

    def raw(self, eng, fn):
        self.items[eng].append(fn)

    def flush(self):
        nc = self.nc
        items = self.items
        with nc.Block() as blk:
            if items["gpsimd"]:
                blk.gpsimd(lambda h, l=tuple(items["gpsimd"]): [f(h) for f in l])
            if items["sync"]:
                blk.sync(lambda h, l=tuple(items["sync"]): [f(h) for f in l])
            if items["tensor"]:
                blk.tensor(lambda h, l=tuple(items["tensor"]): [f(h) for f in l])
            if items["scalar"]:
                blk.scalar(lambda h, l=tuple(items["scalar"]): [f(h) for f in l])
            if items["vector"]:
                blk.vector(lambda h, l=tuple(items["vector"]): [f(h) for f in l])
        self.items = {e: [] for e in ENGS}


def build_nc():
    nc = bass.Bass("TRN2", target_bir_lowering=False)

    x = nc.dram_tensor("x", [T, E], F32, kind="ExternalInput").ap()
    wat = nc.dram_tensor("wat", [3 * E, E], F32, kind="ExternalInput").ap()
    wpr = nc.dram_tensor("wpr", [E, E], F32, kind="ExternalInput").ap()
    fk = nc.dram_tensor("fk", [E, E], BF16, kind="ExternalInput").ap()
    s_in = nc.dram_tensor("s_in", [128, 1], F32, kind="ExternalInput").ap()
    s_k = nc.dram_tensor("s_k", [128, 1], F32, kind="ExternalInput").ap()
    fbr = nc.dram_tensor("fbr", [128, E], BF16, kind="ExternalInput").ap()
    mkv = nc.dram_tensor("mkv", [128, 1], F32, kind="ExternalInput").ap()
    out = nc.dram_tensor("out", [T, E], F32, kind="ExternalOutput").ap()

    d_xnb = nc.dram_tensor("d_xnb", [128, 8, T], BF16).ap()
    d_k = nc.dram_tensor("d_k", [128, NT, 1024], BF16).ap()
    d_v = nc.dram_tensor("d_v", [128, NT, 1040], BF16).ap()
    d_pf = nc.dram_tensor("d_pf", [128, NT, 520], F32).ap()
    cc_in = nc.dram_tensor("cc_in", [1024, 65], F32).ap()
    cc_out = nc.dram_tensor("cc_out", [2048, 65], F32).ap()

    from contextlib import ExitStack
    ctx = ExitStack()
    sbuf = lambda n, s, d: ctx.enter_context(nc.sbuf_tensor(n, s, d))
    psum = lambda n, s, d: ctx.enter_context(nc.psum_tensor(n, s, d))
    with ctx:
        wq_ar = sbuf("wq_ar", [128, 16384], BF16)
        xnb = sbuf("xnb", [128, 8192], BF16)
        watq = sbuf("watq", [128, 8192], BF16)
        wprT = sbuf("wprT", [128, 8192], BF16)
        kq = sbuf("kq", [128, 8192], FP8)
        xq = sbuf("xq", [128, 2048], FP8)
        k_sb = sbuf("k_sb", [128, 8192], BF16)
        v_sb = sbuf("v_sb", [128, 8320], BF16)
        pf_sb = sbuf("pf_sb", [128, 4160], F32)
        pf_bf = sbuf("pf_bf", [128, 4160], BF16)
        carry = sbuf("carry", [128, 520], F32)
        sinit = sbuf("sinit", [128, 520], F32)
        sinit_bf = sbuf("sinit_bf", [128, 520], BF16)
        xt = sbuf("xt", [128, 4096], F32)
        outt = sbuf("outt", [128, 4096], F32)
        scm = sbuf("scm", [128, 2048], BF16)
        kf = sbuf("kf", [128, 1024], BF16)
        yt = sbuf("yt", [128, 1024], BF16)
        wstg = sbuf("wstg", [128, 2048], BF16)
        psc = sbuf("psc", [128, 2048], F32)
        st = sbuf("st", [128, 64], F32)
        cols = sbuf("cols", [128, 8], F32)
        iot = sbuf("iot", [128, 128], F32)
        ident = sbuf("ident", [128, 128], BF16)
        mask = sbuf("mask", [128, 128], BF16)
        fbb = sbuf("fbb", [128, 1024], BF16)

        pA = psum("pA", [128, 512], F32)
        pB = psum("pB", [128, 512], F32)
        pC = psum("pC", [128, 512], F32)
        pD = psum("pD", [128, 512], F32)
        pS = psum("pS", [128, 1024], F32)
        tr1 = psum("tr1", [128, 1024], BF16)
        den_ps = psum("den_ps", [128, 512], F32)

        P = Prog(nc)
        LD, STQ, CC = "q_ld", "q_st", "q_cc"

        rr = {"sync": 0, "gpsimd": 0}

        def dma(queue, dst, src, reads, writes):
            base = LD if queue == "sync" else STQ
            idx = rr[queue]
            rr[queue] = (idx + 1) % 4
            sem_name = f"{base}{idx}"
            P.sem(sem_name)
            prev = P.cnt[sem_name]

            def fn(h, d=dst, s=src, sem_name=sem_name, prev=prev):
                if prev:
                    h.wait_ge(P.sems[sem_name], prev)
                return h.dma_start(d, s)

            P.op(queue, fn, reads=reads, writes=writes, sem_name=sem_name, inc=16)

        P.op("gpsimd", lambda h: h.iota(iot[:, :], [[1, 128]], channel_multiplier=-1,
                                        allow_small_or_imprecise_dtypes=True),
             writes=["iot"], inc=1)
        P.op("vector", lambda h: h.tensor_single_scalar(mask[:, :], iot[:, :], 0.0, ALU.is_ge),
             reads=["iot"], writes=["mask"], inc=1)
        P.op("vector", lambda h: h.tensor_single_scalar(ident[:, :], iot[:, :], 0.0, ALU.is_equal),
             reads=["iot"], writes=["ident"], inc=1)
        P.op("vector", lambda h: h.memset(cols[:, 0:1], -0.5), writes=["c_nh"], inc=1)
        P.op("vector", lambda h: h.memset(cols[:, 5:6], 1e-5), writes=["c_eps"], inc=1)
        dma("sync", st[:, 0:1], s_in, [], ["st_sc"])
        dma("sync", st[:, 1:2], s_k, [], ["st_sc"])
        dma("sync", st[:, 2:3], mkv, [], ["st_mk"])
        dma("sync", fbb[:, :], fbr, [], ["fbb"])
        P.op("vector", lambda h: h.reciprocal(cols[:, 1:2], st[:, 0:1]),
             reads=["st_sc"], writes=["c_si"], inc=1)
        P.op("vector", lambda h: h.reciprocal(cols[:, 2:3], st[:, 1:2]),
             reads=["st_sc"], writes=["c_sk"], inc=1)
        P.op("vector", lambda h: h.tensor_mul(cols[:, 3:4], st[:, 0:1], st[:, 1:2]),
             reads=["st_sc"], writes=["c_sp"], inc=1)

        def w_dma(kind, r, slot):
            if kind == "kv":
                srcw = wat[1024 + 128 * r:1024 + 128 * (r + 1), :]
            elif kind == "q":
                srcw = wat[128 * r:128 * (r + 1), :]
            elif kind == "fk":
                srcw = fk[128 * r:128 * (r + 1), :]
            else:
                srcw = wpr[128 * r:128 * (r + 1), :]
            dma("gpsimd", wstg[:, 1024 * slot:1024 * (slot + 1)], srcw, [], [f"wstg{slot}"])

        def w_slice(kind, r, trp, trkey, slot):
            stg = wstg[:, 1024 * slot:1024 * (slot + 1)]
            skey = f"wstg{slot}"
            for e in range(8):
                P.op("tensor",
                     lambda h, e=e, stg=stg, trp=trp: h.transpose(
                         trp[:, 128 * e:128 * (e + 1)], stg[:, 128 * e:128 * (e + 1)],
                         ident[:, :]),
                     reads=[skey, "ident"],
                     writes=[trkey] if e in (0, 7) else [],
                     inc=1 if e in (0, 7) else 0)
            dstt, stride, dkey = {"kv": (wq_ar, 2048, "w_kv"),
                                  "q": (watq, 1024, "w_q"),
                                  "pr": (wprT, 1024, "w_pr")}[kind]
            base = 128 * r
            P.op("vector",
                 lambda h, trp=trp, dstt=dstt, stride=stride, base=base: h.tensor_copy(
                     dstt[:, :].rearrange("p (ec c) -> p ec c", c=stride)[:, :, base:base + 128],
                     trp[:, :].rearrange("p (ec c) -> p ec c", ec=8)),
                 reads=[trkey], writes=[dkey], inc=1)

        WPLAN = ([("kv", r) for r in range(16)] + [("q", r) for r in range(8)]
                 + [("pr", r) for r in range(8)] + [("fk", r) for r in range(8)])
        w_dma(*WPLAN[0], 0)
        w_dma(*WPLAN[1], 1)
        for r in range(16):
            w_slice("kv", r, tr1, "tr1", r % 2)
            if r + 2 < len(WPLAN):
                w_dma(*WPLAN[r + 2], (r + 2) % 2)

        kvbank = [(pA, "pA"), (pB, "pB"), (pC, "pC"), (pD, "pD")]

        def s_region(hh):
            g, par = hh // 2, hh % 2
            off = 65 * g if g < 7 else 512 + 65 * (g - 7)
            key = "pS0" if g < 7 else "pS1"
            return pS[64 * par:64 * (par + 1), off:off + 65], key, (0, 64 * par)

        def pf_col(g):
            return 65 * g

        def ln_a(t):
            xslot = t % 4
            xts = xt[:, 1024 * xslot:1024 * (xslot + 1)]
            xkey = f"xt{xslot}"
            dma("sync", xts, x[128 * t:128 * (t + 1), :], [], [xkey])
            P.op("vector", lambda h, xts=xts: h.bn_stats(st[:, 16:22], xts[:, 0:512]),
                 reads=[xkey], writes=["stA"], inc=1)
            P.op("vector", lambda h, xts=xts: h.bn_stats(st[:, 22:28], xts[:, 512:1024]),
                 reads=[xkey], writes=["stA"], inc=1)
            P.op("vector", lambda h: h.bn_aggr(st[:, 12:14], st[:, 16:28]),
                 reads=["stA"], writes=["stB"], inc=1)
            P.op("scalar", lambda h: h.activation(st[:, 15:16], st[:, 13:14], AF.Ln,
                                                  bias=cols[:, 5:6]),
                 reads=["stB", "c_eps"], writes=["stD"], inc=1)
            P.op("scalar", lambda h: h.activation(st[:, 16:17], st[:, 15:16], AF.Exp,
                                                  scale=cols[:, 0:1]),
                 reads=["stD", "c_nh"], writes=["stE"], inc=1)

        def ln_b(t):
            slot = t % 2
            xslot = t % 4
            xts = xt[:, 1024 * xslot:1024 * (xslot + 1)]
            P.op("vector", lambda h, xts=xts, slot=slot: h.tensor_scalar(
                scm[:, 1024 * slot:1024 * (slot + 1)], xts, st[:, 12:13], st[:, 16:17],
                ALU.subtract, ALU.mult),
                 reads=[f"xt{xslot}", "stB", "stE"], writes=[f"ytm{slot}"], inc=1)

        def heavy_stage(t):
            sc, tl = divmod(t, 8)
            slot = t % 2
            ytm = scm[:, 1024 * slot:1024 * (slot + 1)]
            ykey = f"ytm{slot}"
            for e in range(8):
                P.op("tensor", lambda h, e=e, ytm=ytm: h.transpose(
                    tr1[:, 128 * e:128 * (e + 1)], ytm[:, 128 * e:128 * (e + 1)], ident[:, :]),
                     reads=[ykey, "ident"], writes=["tr1"] if e in (0, 7) else [],
                     inc=1 if e in (0, 7) else 0)
            P.op("scalar", lambda h, tl=tl: h.copy(
                xnb[:, :].rearrange("p (ec c) -> p ec c", ec=8)[:, :, 128 * tl:128 * (tl + 1)],
                tr1[:, :].rearrange("p (ec c) -> p ec c", ec=8)),
                 reads=["tr1"], writes=[f"xnb{tl}"], inc=1)
            for n in range(4):
                bank, bkey = kvbank[n]
                for e in range(8):
                    P.op("tensor", lambda h, e=e, n=n, tl=tl, bank=bank: h.matmul(
                        bank[:, :],
                        xnb[:, 1024 * e + 128 * tl:1024 * e + 128 * (tl + 1)],
                        wq_ar[:, 2048 * e + 512 * n:2048 * e + 512 * (n + 1)],
                        start=(e == 0), stop=(e == 7), skip_group_check=True),
                         reads=[f"xnb{tl}", "w_kv"],
                         writes=[bkey] if e in (0, 7) else [],
                         inc=1 if e in (0, 7) else 0)
                if n < 2:
                    pe = psc[:, 1024 * (n % 2):1024 * (n % 2) + 512]
                    pr = psc[:, 1024 * (n % 2) + 512:1024 * (n % 2) + 1024]
                    pky = f"psc{n % 2}"
                    P.op("scalar", lambda h, bank=bank, pe=pe: h.activation(pe, bank[:, :], AF.Exp),
                         reads=[bkey], writes=[pky + "e"], inc=1)
                    P.op("scalar", lambda h, bank=bank, pr=pr: h.activation(pr, bank[:, :], AF.Relu),
                         reads=[bkey], writes=[pky + "r"], inc=1)
                    P.op("vector", lambda h, pe=pe, pr=pr, tl=tl, n=n: h.scalar_tensor_tensor(
                        k_sb[:, 1024 * tl + 512 * n:1024 * tl + 512 * (n + 1)],
                        pe, 1.0, pr, ALU.min, ALU.add),
                         reads=[pky + "e", pky + "r", bkey], writes=[f"k{tl}"], inc=1)
                else:
                    hh0 = 8 * (n - 2)
                    P.op("vector", lambda h, bank=bank, tl=tl, hh0=hh0: h.tensor_copy(
                        v_sb[:, 1040 * tl:1040 * (tl + 1)].rearrange(
                            "p (hh ee) -> p hh ee", hh=16)[:, hh0:hh0 + 8, 0:64],
                        bank[:, :].rearrange("p (hh ee) -> p hh ee", hh=8)),
                         reads=[bkey], writes=[f"v{tl}"], inc=1)
            P.op("vector", lambda h, tl=tl: h.memset(
                v_sb[:, 1040 * tl:1040 * (tl + 1)].rearrange(
                    "p (hh ee) -> p hh ee", hh=16)[:, :, 64:65], 1.0),
                 writes=[f"v{tl}"], inc=1)
            for hh in range(H):
                sl, skey, tp = s_region(hh)
                P.op("tensor", lambda h, hh=hh, tl=tl, sl=sl, tp=tp: h.matmul(
                    sl,
                    k_sb[:, 1024 * tl + 64 * hh:1024 * tl + 64 * (hh + 1)],
                    v_sb[:, 1040 * tl + 65 * hh:1040 * tl + 65 * (hh + 1)],
                    start=True, stop=True, tile_position=tp, skip_group_check=True),
                     reads=[f"k{tl}", f"v{tl}"],
                     writes=[skey] if hh in (0, 1, 12, 13, 14, 15) else [],
                     inc=1 if hh in (0, 1, 12, 13, 14, 15) else 0)
            if t == 0:
                P.op("vector", lambda h: h.tensor_copy(pf_sb[:, 0:455], pS[:, 0:455]),
                     reads=["pS0"], writes=["pf0"], inc=1)
                P.op("vector", lambda h: h.tensor_copy(pf_sb[:, 455:520], pS[:, 512:577]),
                     reads=["pS1"], writes=["pf0"], inc=1)
            else:
                prev = carry if tl == 0 else pf_sb[:, 520 * (tl - 1):520 * tl]
                pkey = "carry" if tl == 0 else f"pf{tl - 1}"
                P.op("vector", lambda h, tl=tl, prev=prev: h.tensor_add(
                    pf_sb[:, 520 * tl:520 * tl + 455], prev[:, 0:455], pS[:, 0:455]),
                     reads=["pS0", pkey], writes=[f"pf{tl}"], inc=1)
                P.op("vector", lambda h, tl=tl, prev=prev: h.tensor_add(
                    pf_sb[:, 520 * tl + 455:520 * tl + 520], prev[:, 455:520], pS[:, 512:577]),
                     reads=["pS1", pkey], writes=[f"pf{tl}"], inc=1)
            if sc < 3:
                dma("gpsimd", d_xnb[:, :, 1024 * sc + 128 * tl:1024 * sc + 128 * (tl + 1)],
                    xnb[:, :].rearrange("p (ec c) -> p ec c", ec=8)[:, :, 128 * tl:128 * (tl + 1)],
                    [f"xnb{tl}"], [])
                dma("gpsimd", d_k[:, t, :], k_sb[:, 1024 * tl:1024 * (tl + 1)],
                    [f"k{tl}"], [])
                dma("gpsimd", d_v[:, t, :], v_sb[:, 1040 * tl:1040 * (tl + 1)],
                    [f"v{tl}"], [])
            dma("gpsimd", d_pf[:, t, :], pf_sb[:, 520 * tl:520 * (tl + 1)],
                [f"pf{tl}"], [])
            if tl == 7:
                P.op("vector", lambda h: h.tensor_copy(carry[:, :], pf_sb[:, 520 * 7:520 * 8]),
                     reads=["pf7"], writes=["carry"], inc=1)

        ln_a(0)
        ln_b(0)
        for t in range(NT):
            if t + 1 < NT:
                ln_a(t + 1)
                ln_b(t + 1)
            heavy_stage(t)
            wi = t + 16
            if t < 8:
                w_slice("q", t, tr1, "tr1", wi % 2)
            elif t < 16:
                w_slice("pr", t - 8, tr1, "tr1", wi % 2)
            elif t < 24:
                e = t - 16
                slot = wi % 2
                stg = wstg[:, 1024 * slot:1024 * (slot + 1)]
                skey = f"wstg{slot}"
                P.op("scalar",
                     lambda h, e=e, stg=stg: h.activation(
                         kq[:, 1024 * e:1024 * (e + 1)], stg, AF.Copy, scale=cols[:, 2:3]),
                     reads=[skey, "c_sk"], writes=["kq"], inc=1)
            if wi + 2 < len(WPLAN):
                w_dma(*WPLAN[wi + 2], (wi + 2) % 2)

        dma("gpsimd", cc_in.rearrange("(g p) e -> p g e", p=128),
            pf_sb[:, 520 * 7:520 * 8].rearrange("p (g e) -> p g e", g=8),
            ["pf7"], ["cc_in"])
        P.op("gpsimd", lambda h: h.collective_compute(
            "AllGather", ALU.bypass,
            replica_groups=[[0, 1], [2, 3], [4, 5], [6, 7]],
            ins=[cc_in], outs=[cc_out]),
             reads=["cc_in"], writes=["cc_out"], sem_name=CC, inc=1)
        dma("sync", sinit[:, :].rearrange("p (g e) -> p g e", g=8),
            cc_out[0:1024, :].rearrange("(g p) e -> p g e", p=128),
            ["cc_out"], ["sinit_r"])
        P.op("vector", lambda h: h.tensor_scalar_mul(sinit[:, :], sinit[:, :], st[:, 2:3]),
             reads=["sinit_r", "st_mk"], writes=["sinit"], inc=1)
        P.op("vector", lambda h: h.tensor_copy(sinit_bf[:, :], sinit[:, :]),
             reads=["sinit"], writes=["sinitb"], inc=1)

        q_sb = wq_ar
        xnb_keys = [f"xnb{i}" for i in range(8)]
        SCMF = ["scmF0", "scmF1", "scmF2", "scmF3"]
        SCMR = ["scmR0", "scmR1", "scmR2", "scmR3"]

        def b_load_slot(sc, i, skip_xnb=False, pf_only=False):
            if not pf_only:
                if not skip_xnb:
                    dma("sync",
                        xnb[:, :].rearrange("p (ec c) -> p ec c", ec=8)[:, :, 128 * i:128 * (i + 1)],
                        d_xnb[:, :, 1024 * sc + 128 * i:1024 * sc + 128 * (i + 1)],
                        [], [f"xnb{i}"])
                dma("sync", k_sb[:, 1024 * i:1024 * (i + 1)], d_k[:, 8 * sc + i, :],
                    [], [f"k{i}"])
                dma("sync", v_sb[:, 1040 * i:1040 * (i + 1)], d_v[:, 8 * sc + i, :],
                    [], [f"v{i}"])
            cg = 8 * sc + i - 1
            if cg >= 0:
                dma("sync", pf_sb[:, 520 * i:520 * (i + 1)], d_pf[:, cg, :],
                    [], [f"pf{i}"])
                P.op("vector", lambda h, i=i: h.tensor_add(
                    pf_sb[:, 520 * i:520 * (i + 1)], pf_sb[:, 520 * i:520 * (i + 1)],
                    sinit[:, :]),
                     reads=["sinit", f"pf{i}"], writes=[f"pf{i}"], inc=1)
                P.op("scalar", lambda h, i=i: h.copy(
                    pf_bf[:, 520 * i:520 * (i + 1)], pf_sb[:, 520 * i:520 * (i + 1)]),
                     reads=[f"pf{i}"], writes=[f"pfb{i}"], inc=1)

        def b_qproj(sc):
            for qc in range(8):
                lob, lkey = (pA, "pA") if qc % 2 == 0 else (pC, "pC")
                hib, hkey = (pB, "pB") if qc % 2 == 0 else (pD, "pD")
                for e in range(8):
                    P.op("tensor", lambda h, e=e, qc=qc, lob=lob: h.matmul(
                        lob[:, :], watq[:, 1024 * e + 128 * qc:1024 * e + 128 * (qc + 1)],
                        xnb[:, 1024 * e:1024 * e + 512],
                        start=(e == 0), stop=(e == 7), skip_group_check=True),
                         reads=["w_q"] + xnb_keys[0:4],
                         writes=[lkey] if e in (0, 7) else [], inc=1 if e in (0, 7) else 0)
                for e in range(8):
                    P.op("tensor", lambda h, e=e, qc=qc, hib=hib: h.matmul(
                        hib[:, :], watq[:, 1024 * e + 128 * qc:1024 * e + 128 * (qc + 1)],
                        xnb[:, 1024 * e + 512:1024 * e + 1024],
                        start=(e == 0), stop=(e == 7), skip_group_check=True),
                         reads=["w_q"] + xnb_keys[4:8],
                         writes=[hkey] if e in (0, 7) else [], inc=1 if e in (0, 7) else 0)
                for half, (bank, bkey) in enumerate(((lob, lkey), (hib, hkey))):
                    pe = psc[:, 1024 * (qc % 2):1024 * (qc % 2) + 512]
                    pr = psc[:, 1024 * (qc % 2) + 512:1024 * (qc % 2) + 1024]
                    pky = f"psc{qc % 2}"
                    P.op("scalar", lambda h, bank=bank, pe=pe: h.activation(pe, bank[:, :], AF.Exp),
                         reads=[bkey], writes=[pky + "e"], inc=1)
                    P.op("scalar", lambda h, bank=bank, pr=pr: h.activation(pr, bank[:, :], AF.Relu),
                         reads=[bkey], writes=[pky + "r"], inc=1)
                    P.op("vector", lambda h, pe=pe, pr=pr, qc=qc, half=half: h.scalar_tensor_tensor(
                        q_sb[:, 1024 * qc + 512 * half:1024 * qc + 512 * (half + 1)],
                        pe, 1.0, pr, ALU.min, ALU.add),
                         reads=[pky + "e", pky + "r", bkey], writes=[f"q{qc}"], inc=1)

        def b_ffn_quant(tt):
            xqs = xq[:, 1024 * (tt % 2):1024 * (tt % 2 + 1)]
            P.op("scalar", lambda h, tt=tt, xqs=xqs: h.activation(
                xqs.rearrange("p (ec c) -> p ec c", ec=8),
                xnb[:, :].rearrange("p (ec c) -> p ec c", ec=8)[:, :, 128 * tt:128 * (tt + 1)],
                AF.Copy, scale=cols[:, 1:2]),
                 reads=[f"xnb{tt}", "c_si"], writes=[f"xq{tt % 2}"], inc=1)

        def b_ffn(sc):
            b_ffn_quant(0)
            for tt in range(8):
                xqs = xq[:, 1024 * (tt % 2):1024 * (tt % 2 + 1)]
                xqkey = f"xq{tt % 2}"
                if tt + 1 < 8:
                    b_ffn_quant(tt + 1)
                lob, lkey = (pA, "pA") if tt % 2 == 0 else (pC, "pC")
                hib, hkey = (pB, "pB") if tt % 2 == 0 else (pD, "pD")
                for e in range(8):
                    P.op("tensor", lambda h, e=e, xqs=xqs, lob=lob: h.matmul(
                        lob[:, :], xqs[:, 128 * e:128 * (e + 1)],
                        kq[:, 1024 * e:1024 * e + 512],
                        start=(e == 0), stop=(e == 7), skip_group_check=True),
                         reads=[xqkey, "kq"],
                         writes=[lkey] if e in (0, 7) else [], inc=1 if e in (0, 7) else 0)
                    P.op("tensor", lambda h, e=e, xqs=xqs, hib=hib: h.matmul(
                        hib[:, :], xqs[:, 128 * e:128 * (e + 1)],
                        kq[:, 1024 * e + 512:1024 * e + 1024],
                        start=(e == 0), stop=(e == 7), skip_group_check=True),
                         reads=[xqkey, "kq"],
                         writes=[hkey] if e in (0, 7) else [], inc=1 if e in (0, 7) else 0)
                for half, (bank, bkey) in enumerate(((lob, lkey), (hib, hkey))):
                    j = 2 * (tt % 2) + half
                    dst = scm[:, 512 * j:512 * (j + 1)]
                    P.op("scalar", lambda h, bank=bank, dst=dst: h.activation(
                        dst, bank[:, :], AF.Copy, scale=cols[:, 3:4]),
                         reads=[bkey, "c_sp"], writes=[f"scmF{j}"] + SCMR, inc=1)
                    P.op("vector", lambda h, dst=dst, tt=tt, half=half: h.tensor_add(
                        q_sb[:, 8192 + 1024 * tt + 512 * half:8192 + 1024 * tt + 512 * (half + 1)],
                        dst, fbb[:, 512 * half:512 * (half + 1)]),
                         reads=[f"scmF{j}", "fbb"], writes=[f"ffn{tt}"], inc=1)

        def b_ktr(c):
            kfb = kf if c % 2 == 0 else yt
            for g in range(8):
                P.op("tensor", lambda h, g=g, c=c: h.transpose(
                    tr1[:, 128 * g:128 * (g + 1)],
                    k_sb[:, 1024 * c + 128 * g:1024 * c + 128 * (g + 1)], ident[:, :]),
                     reads=[f"k{c}", "ident"], writes=["tr1"] if g in (0, 7) else [],
                     inc=1 if g in (0, 7) else 0)
            P.op("scalar", lambda h, kfb=kfb: h.copy(kfb[:, :], tr1[:, :]),
                 reads=["tr1"], writes=[f"kf{c % 2}"], inc=1)

        def num_region(c, hh):
            if c % 2 == 0:
                nb, nkey = (pC, "pC") if hh < 8 else (pD, "pD")
            else:
                nb, nkey = (pS[:, 0:512], "pS0") if hh < 8 else (pS[:, 512:1024], "pS1")
            return nb, nkey, 64 * (hh % 8)

        def b_scan_core(sc, c, filler=None):
            kfb = kf if c % 2 == 0 else yt
            kkey = f"kf{c % 2}"
            dkey = "dnE" if c % 2 == 0 else "dnO"
            dbase = 16 * (c % 2)
            def scT_round(r):
                par = r % 2
                heads = [8 * (r // 2) + par + 2 * i for i in range(4)]
                bank, bkey = (pA, "pA") if par == 0 else (pB, "pB")
                for i, hh in enumerate(heads):
                    g = hh // 2
                    P.op("tensor", lambda h, g=g, par=par, c=c, bank=bank, i=i, kfb=kfb:
                         h.matmul(
                             bank[:, 128 * i:128 * (i + 1)],
                             kfb[64 * par:64 * (par + 1), 128 * g:128 * (g + 1)],
                             q_sb[64 * par:64 * (par + 1),
                                  1024 * g + 128 * c:1024 * g + 128 * (c + 1)],
                             start=True, stop=True, skip_group_check=True),
                         reads=[kkey, f"q{g}"],
                         writes=[bkey] if i in (0, 3) else [],
                         inc=1 if i in (0, 3) else 0)
                wkeys = [SCMR[r]] + (SCMF if c == 0 else [])
                P.op("vector", lambda h, bank=bank, r=r, par=par: h.tensor_tensor(
                    scm[:, :].rearrange("p (s c2) -> p s c2", s=16)[
                        :, 8 * (r // 2) + par:8 * (r // 2) + par + 7:2, :],
                    bank[:, :].rearrange("p (s c2) -> p s c2", s=4),
                    mask[:, :].unsqueeze(1).broadcast_to([128, 4, 128]),
                    ALU.mult),
                     reads=[bkey, "mask"], writes=wkeys, inc=1)

            def num_round(r):
                par = r % 2
                heads = [8 * (r // 2) + par + 2 * i for i in range(4)]
                dkey = "dnE" if c % 2 == 0 else "dnO"
                dbase = 16 * (c % 2)
                for hh in heads:
                    g, p2 = hh // 2, hh % 2
                    nb, nkey, noff = num_region(c, hh)
                    if sc == 0 and c == 0:
                        rhs = sinit_bf[64 * p2:64 * (p2 + 1), 65 * g:65 * (g + 1)]
                        rkey = "sinitb"
                    else:
                        pfo = 520 * c + 65 * g
                        rhs = pf_bf[64 * p2:64 * (p2 + 1), pfo:pfo + 65]
                        rkey = f"pfb{c}"
                    P.op("tensor", lambda h, hh=hh, c=c, nb=nb, noff=noff: h.matmul(
                        nb[:, noff:noff + 64],
                        scm[:, 128 * hh:128 * (hh + 1)],
                        v_sb[:, 1040 * c + 65 * hh:1040 * c + 65 * hh + 64],
                        start=True, stop=False, skip_group_check=True),
                         reads=[SCMR[r], f"v{c}"], writes=[nkey], inc=1)
                    P.op("tensor", lambda h, hh=hh, c=c, dbase=dbase: h.matmul(
                        den_ps[:, dbase + hh:dbase + hh + 1],
                        scm[:, 128 * hh:128 * (hh + 1)],
                        v_sb[:, 1040 * c + 65 * hh + 64:1040 * c + 65 * (hh + 1)],
                        start=True, stop=False, skip_group_check=True),
                         reads=[SCMR[r], f"v{c}"], writes=[dkey], inc=1)
                    P.op("tensor", lambda h, g=g, p2=p2, c=c, nb=nb, noff=noff, rhs=rhs:
                         h.matmul(
                             nb[:, noff:noff + 64],
                             q_sb[64 * p2:64 * (p2 + 1),
                                  1024 * g + 128 * c:1024 * g + 128 * (c + 1)],
                             rhs[:, 0:64], start=False, stop=True, skip_group_check=True),
                         reads=[f"q{g}", rkey], writes=[nkey], inc=1)
                    P.op("tensor", lambda h, g=g, p2=p2, c=c, hh=hh, dbase=dbase, rhs=rhs:
                         h.matmul(
                             den_ps[:, dbase + hh:dbase + hh + 1],
                             q_sb[64 * p2:64 * (p2 + 1),
                                  1024 * g + 128 * c:1024 * g + 128 * (c + 1)],
                             rhs[:, 64:65], start=False, stop=True, skip_group_check=True),
                         reads=[f"q{g}", rkey], writes=[dkey], inc=1)

            scT_round(0)
            scT_round(1)
            if filler is not None:
                filler()
            num_round(0)
            scT_round(2)
            num_round(1)
            scT_round(3)
            num_round(2)
            num_round(3)

        def b_scan_tail(sc, c):
            ysl = wstg[:, 1024 * (c % 2):1024 * (c % 2 + 1)]
            ykey = f"ytw{c % 2}"
            dkey = "dnE" if c % 2 == 0 else "dnO"
            dbase = 16 * (c % 2)
            P.op("vector", lambda h, dbase=dbase: h.tensor_scalar_add(
                st[:, 32:48], den_ps[:, dbase:dbase + 16], 1e-6),
                 reads=[dkey], writes=["dens2"], inc=1)
            P.op("vector", lambda h: h.reciprocal(st[:, 48:64], st[:, 32:48]),
                 reads=["dens2"], writes=["rec"], inc=1)
            for bh in range(2):
                nb, nkey, _ = num_region(c, 8 * bh)
                P.op("vector", lambda h, nb=nb, bh=bh, ysl=ysl: h.tensor_tensor(
                    ysl[:, 512 * bh:512 * (bh + 1)].rearrange("p (s e) -> p s e", s=8),
                    nb[:, :].rearrange("p (s e) -> p s e", s=8),
                    st[:, 48 + 8 * bh:56 + 8 * bh].unsqueeze(2).broadcast_to([128, 8, 64]),
                    ALU.mult),
                     reads=[nkey, "rec"], writes=[ykey], inc=1)
            for g in range(8):
                P.op("tensor", lambda h, g=g, ysl=ysl: h.transpose(
                    tr1[:, 128 * g:128 * (g + 1)], ysl[:, 128 * g:128 * (g + 1)], ident[:, :]),
                     reads=[ykey, "ident"], writes=["tr1"] if g in (0, 7) else [],
                     inc=1 if g in (0, 7) else 0)
            P.op("scalar", lambda h, c=c: h.copy(
                xnb[:, :].rearrange("p (ec c2) -> p ec c2", ec=8)[:, :, 128 * c:128 * (c + 1)],
                tr1[:, :].rearrange("p (ec c2) -> p ec c2", ec=8)),
                 reads=["tr1"], writes=[f"xnb{c}"], inc=1)

        def b_xpre(sc, c):
            tglob = 8 * sc + c
            slot = c % 4
            dma("sync", xt[:, 1024 * slot:1024 * (slot + 1)],
                x[128 * tglob:128 * (tglob + 1), :], [], [f"xt{slot}"])

        def b_outproj(sc, c):
            tglob = 8 * sc + c
            slot = c % 4
            xts = xt[:, 1024 * slot:1024 * (slot + 1)]
            xkey = f"xt{slot}"
            ot = outt[:, 1024 * (c % 2):1024 * (c % 2 + 1)]
            s2 = 2 + c % 2
            ot2 = outt[:, 1024 * s2:1024 * (s2 + 1)]
            okey, okey2 = f"ot{c % 2}", f"ot{2 + c % 2}"
            for half, (bank, bkey) in enumerate(((pA, "pA"), (pB, "pB"))):
                for e in range(8):
                    P.op("tensor", lambda h, e=e, c=c, bank=bank, half=half: h.matmul(
                        bank[:, :], xnb[:, 1024 * e + 128 * c:1024 * e + 128 * (c + 1)],
                        wprT[:, 1024 * e + 512 * half:1024 * e + 512 * (half + 1)],
                        start=(e == 0), stop=(e == 7), skip_group_check=True),
                         reads=[f"xnb{c}", "w_pr"],
                         writes=[bkey] if e in (0, 7) else [], inc=1 if e in (0, 7) else 0)
                P.op("vector", lambda h, ot=ot, xts=xts, bank=bank, half=half: h.tensor_add(
                    ot[:, 512 * half:512 * (half + 1)], xts[:, 512 * half:512 * (half + 1)],
                    bank[:, :]),
                     reads=[xkey, bkey], writes=[okey], inc=1)
            P.op("gpsimd", lambda h, ot=ot, ot2=ot2, c=c: h.tensor_tensor(
                ot2[:, :], ot[:, :],
                q_sb[:, 8192 + 1024 * c:8192 + 1024 * (c + 1)], ALU.add),
                 reads=[okey, f"ffn{c}"], writes=[okey2], inc=1)
            dma("gpsimd", out[128 * tglob:128 * (tglob + 1), :], ot2[:, :], [okey2], [])

        SC_ORDER = [3, 2, 1, 0]
        for si, sc in enumerate(SC_ORDER):
            if si == 0:
                for i in range(8):
                    b_load_slot(sc, i, pf_only=True)
            b_qproj(sc)
            b_ffn(sc)
            b_ktr(0)
            if si == 0:
                b_xpre(sc, 0)
                b_xpre(sc, 1)
            for c in range(8):
                i2 = 8 * si + c + 2
                if i2 < NT:
                    b_xpre(SC_ORDER[i2 // 8], i2 % 8)
                b_scan_core(sc, c,
                            filler=(lambda cc=c + 1: b_ktr(cc)) if c + 1 < 8 else None)
                b_scan_tail(sc, c)
                b_outproj(sc, c)
                if si + 1 < NSC:
                    b_load_slot(SC_ORDER[si + 1], c)

        for i in range(4):
            for base, eng2 in ((STQ, "gpsimd"), (LD, "sync")):
                sn = f"{base}{i}"
                if sn in P.cnt and P.cnt[sn]:
                    P.raw(eng2, lambda h, sn=sn, v=P.cnt[sn]: h.wait_ge(P.sems[sn], v))
        P.flush()
    return nc


def kernel(L0_attn_x, L0_attn_w_attn, L0_attn_w_proj, L0_ffn_kernel,
           L0_ffn_input_scale, L0_ffn_kernel_scale, L0_ffn_bias):
    x = np.ascontiguousarray(np.asarray(L0_attn_x, dtype=np.float32))
    wat = np.ascontiguousarray(np.asarray(L0_attn_w_attn, dtype=np.float32))
    wpr = np.ascontiguousarray(np.asarray(L0_attn_w_proj, dtype=np.float32))
    fk = np.ascontiguousarray(np.asarray(L0_ffn_kernel))
    fb = np.asarray(L0_ffn_bias)
    s_in = np.asarray(L0_ffn_input_scale, dtype=np.float32).ravel()[0]
    s_k = np.asarray(L0_ffn_kernel_scale, dtype=np.float32).ravel()[0]
    sin_col = np.ascontiguousarray(np.broadcast_to(s_in, (128, 1))).astype(np.float32)
    sk_col = np.ascontiguousarray(np.broadcast_to(s_k, (128, 1))).astype(np.float32)
    fbr = np.ascontiguousarray(np.broadcast_to(fb.reshape(1, E), (128, E)))

    nc = build_nc()
    in_maps = []
    for c in range(NCORES):
        b, half = divmod(c, 2)
        xs = np.ascontiguousarray(x[b, half * T:(half + 1) * T, :])
        mk = np.full((128, 1), 1.0 if half == 1 else 0.0, dtype=np.float32)
        in_maps.append({"x": xs, "wat": wat, "wpr": wpr, "fk": fk,
                        "s_in": sin_col, "s_k": sk_col, "fbr": fbr, "mkv": mk})
    res = run_bass_kernel_spmd(nc, in_maps, core_ids=list(range(NCORES)))
    outp = np.empty((4, 8192, E), dtype=np.float32)
    for c in range(NCORES):
        b, half = divmod(c, 2)
        outp[b, half * T:(half + 1) * T, :] = res.results[c]["out"]
    return outp


# version 12
# speedup vs baseline: 1.2339x; 1.0598x over previous
import numpy as np
import ml_dtypes
import concourse.bass as bass
import concourse.mybir as mybir
import concourse.bass_utils as _bu
from concourse.bass_utils import run_bass_kernel_spmd


F32 = mybir.dt.float32
BF16 = mybir.dt.bfloat16
FP8 = mybir.dt.float8e4
AF = mybir.ActivationFunctionType
ALU = mybir.AluOpType

NCORES = 8
T = 4096
NT = 32
NSC = 4
E = 1024
H = 16

ENGS = ("tensor", "scalar", "vector", "gpsimd", "sync")
import os
FULL_SERIAL = os.environ.get("FULL_SERIAL", "0") == "1"


class Prog:

    def __init__(self, nc):
        self.nc = nc
        self.sems = {}
        self.cnt = {}
        self.items = {e: [] for e in ENGS}
        self.last_w = {}
        self.readers = {}
        self.waited = {e: {} for e in ENGS}
        self.dma_sems = set()

    def sem(self, name):
        if name not in self.sems:
            self.sems[name] = self.nc.alloc_semaphore(name=name)
            self.cnt[name] = 0
        return self.sems[name]

    def _need(self, eng, sem_name, val, waits):
        if val <= 0:
            return
        w = self.waited[eng]
        if w.get(sem_name, 0) >= val:
            return
        w[sem_name] = val
        waits.append((sem_name, val))

    def op(self, eng, fn, reads=(), writes=(), sem_name=None, inc=0):
        if sem_name is None:
            sem_name = "e_" + eng
        self.sem(sem_name)
        waits = []
        if FULL_SERIAL:
            for sname in list(self.cnt):
                self._need(eng, sname, self.cnt[sname], waits)

        def dep(sname, v):
            self._need(eng, sname, v, waits)

        for k in reads:
            lw = self.last_w.get(k)
            if lw:
                dep(lw[0], lw[1])
        for k in writes:
            lw = self.last_w.get(k)
            if lw:
                dep(lw[0], lw[1])
            for sname, v in self.readers.get(k, {}).items():
                dep(sname, v)
        val = None
        if inc:
            self.cnt[sem_name] += inc
            val = self.cnt[sem_name]
        sems = self.sems

        def thunk(h, waits=waits, fn=fn, sem_name=sem_name, inc=inc):
            for sn, v in waits:
                h.wait_ge(sems[sn], v)
            ins = fn(h)
            if inc and ins is not None:
                ins.then_inc(sems[sem_name], inc)

        self.items[eng].append(thunk)
        if inc:
            for k in reads:
                self.readers.setdefault(k, {})[sem_name] = val
            for k in writes:
                self.last_w[k] = (sem_name, val)
                self.readers[k] = {}
        return val

    def raw(self, eng, fn):
        self.items[eng].append(fn)

    def flush(self):
        nc = self.nc
        items = self.items
        with nc.Block() as blk:
            if items["gpsimd"]:
                blk.gpsimd(lambda h, l=tuple(items["gpsimd"]): [f(h) for f in l])
            if items["sync"]:
                blk.sync(lambda h, l=tuple(items["sync"]): [f(h) for f in l])
            if items["tensor"]:
                blk.tensor(lambda h, l=tuple(items["tensor"]): [f(h) for f in l])
            if items["scalar"]:
                blk.scalar(lambda h, l=tuple(items["scalar"]): [f(h) for f in l])
            if items["vector"]:
                blk.vector(lambda h, l=tuple(items["vector"]): [f(h) for f in l])
        self.items = {e: [] for e in ENGS}


def build_nc():
    nc = bass.Bass("TRN2", target_bir_lowering=False)

    x = nc.dram_tensor("x", [T, E], F32, kind="ExternalInput").ap()
    wat = nc.dram_tensor("wat", [3 * E, E], F32, kind="ExternalInput").ap()
    wpr = nc.dram_tensor("wpr", [E, E], F32, kind="ExternalInput").ap()
    fk = nc.dram_tensor("fk", [E, E], BF16, kind="ExternalInput").ap()
    s_in = nc.dram_tensor("s_in", [128, 1], F32, kind="ExternalInput").ap()
    s_k = nc.dram_tensor("s_k", [128, 1], F32, kind="ExternalInput").ap()
    fbr = nc.dram_tensor("fbr", [128, E], BF16, kind="ExternalInput").ap()
    mkv = nc.dram_tensor("mkv", [128, 1], F32, kind="ExternalInput").ap()
    out = nc.dram_tensor("out", [T, E], F32, kind="ExternalOutput").ap()

    d_xnb = nc.dram_tensor("d_xnb", [128, 8, T], BF16).ap()
    d_k = nc.dram_tensor("d_k", [128, NT, 1024], BF16).ap()
    d_v = nc.dram_tensor("d_v", [128, NT, 1040], BF16).ap()
    d_pf = nc.dram_tensor("d_pf", [128, NT, 520], F32).ap()
    cc_in = nc.dram_tensor("cc_in", [1024, 65], F32).ap()
    cc_out = nc.dram_tensor("cc_out", [2048, 65], F32).ap()

    from contextlib import ExitStack
    ctx = ExitStack()
    sbuf = lambda n, s, d: ctx.enter_context(nc.sbuf_tensor(n, s, d))
    psum = lambda n, s, d: ctx.enter_context(nc.psum_tensor(n, s, d))
    with ctx:
        wq_ar = sbuf("wq_ar", [128, 16384], BF16)
        xnb = sbuf("xnb", [128, 8192], BF16)
        watq = sbuf("watq", [128, 8192], BF16)
        wprT = sbuf("wprT", [128, 8192], BF16)
        kq = sbuf("kq", [128, 8192], FP8)
        xq = sbuf("xq", [128, 2048], FP8)
        k_sb = sbuf("k_sb", [128, 8192], BF16)
        v_sb = sbuf("v_sb", [128, 8320], BF16)
        pf_sb = sbuf("pf_sb", [128, 4160], F32)
        pf_bf = sbuf("pf_bf", [128, 4160], BF16)
        carry = sbuf("carry", [128, 520], F32)
        sinit = sbuf("sinit", [128, 520], F32)
        sinit_bf = sbuf("sinit_bf", [128, 520], BF16)
        xt = sbuf("xt", [128, 4096], F32)
        outt = sbuf("outt", [128, 4096], F32)
        scm = sbuf("scm", [128, 2048], BF16)
        kf = sbuf("kf", [128, 1024], BF16)
        yt = sbuf("yt", [128, 1024], BF16)
        wstg = sbuf("wstg", [128, 2048], BF16)
        psc = sbuf("psc", [128, 2048], F32)
        st = sbuf("st", [128, 64], F32)
        cols = sbuf("cols", [128, 8], F32)
        iot = sbuf("iot", [128, 128], F32)
        ident = sbuf("ident", [128, 128], BF16)
        mask = sbuf("mask", [128, 128], BF16)
        fbb = sbuf("fbb", [128, 1024], BF16)

        pA = psum("pA", [128, 512], F32)
        pB = psum("pB", [128, 512], F32)
        pC = psum("pC", [128, 512], F32)
        pD = psum("pD", [128, 512], F32)
        pS = psum("pS", [128, 1024], F32)
        tr1 = psum("tr1", [128, 1024], BF16)
        den_ps = psum("den_ps", [128, 512], F32)

        P = Prog(nc)
        LD, STQ, CC = "q_ld", "q_st", "q_cc"

        rr = {"sync": 0, "gpsimd": 0}

        def dma(queue, dst, src, reads, writes):
            base = LD if queue == "sync" else STQ
            idx = rr[queue]
            rr[queue] = (idx + 1) % 4
            sem_name = f"{base}{idx}"
            P.sem(sem_name)
            prev = P.cnt[sem_name]

            def fn(h, d=dst, s=src, sem_name=sem_name, prev=prev):
                if prev:
                    h.wait_ge(P.sems[sem_name], prev)
                return h.dma_start(d, s)

            P.op(queue, fn, reads=reads, writes=writes, sem_name=sem_name, inc=16)

        P.op("gpsimd", lambda h: h.iota(iot[:, :], [[1, 128]], channel_multiplier=-1,
                                        allow_small_or_imprecise_dtypes=True),
             writes=["iot"], inc=1)
        P.op("vector", lambda h: h.tensor_single_scalar(mask[:, :], iot[:, :], 0.0, ALU.is_ge),
             reads=["iot"], writes=["mask"], inc=1)
        P.op("vector", lambda h: h.tensor_single_scalar(ident[:, :], iot[:, :], 0.0, ALU.is_equal),
             reads=["iot"], writes=["ident"], inc=1)
        P.op("vector", lambda h: h.memset(cols[:, 0:1], -0.5), writes=["c_nh"], inc=1)
        P.op("vector", lambda h: h.memset(cols[:, 5:6], 1e-5), writes=["c_eps"], inc=1)
        dma("sync", st[:, 0:1], s_in, [], ["st_sc"])
        dma("sync", st[:, 1:2], s_k, [], ["st_sc"])
        dma("sync", st[:, 2:3], mkv, [], ["st_mk"])
        dma("sync", fbb[:, :], fbr, [], ["fbb"])
        P.op("vector", lambda h: h.reciprocal(cols[:, 1:2], st[:, 0:1]),
             reads=["st_sc"], writes=["c_si"], inc=1)
        P.op("vector", lambda h: h.reciprocal(cols[:, 2:3], st[:, 1:2]),
             reads=["st_sc"], writes=["c_sk"], inc=1)
        P.op("vector", lambda h: h.tensor_mul(cols[:, 3:4], st[:, 0:1], st[:, 1:2]),
             reads=["st_sc"], writes=["c_sp"], inc=1)

        def w_dma(kind, r, slot):
            if kind == "kv":
                srcw = wat[1024 + 128 * r:1024 + 128 * (r + 1), :]
            elif kind == "q":
                srcw = wat[128 * r:128 * (r + 1), :]
            elif kind == "fk":
                srcw = fk[128 * r:128 * (r + 1), :]
            else:
                srcw = wpr[128 * r:128 * (r + 1), :]
            dma("gpsimd", wstg[:, 1024 * slot:1024 * (slot + 1)], srcw, [], [f"wstg{slot}"])

        def w_slice(kind, r, trp, trkey, slot):
            stg = wstg[:, 1024 * slot:1024 * (slot + 1)]
            skey = f"wstg{slot}"
            for e in range(8):
                P.op("tensor",
                     lambda h, e=e, stg=stg, trp=trp: h.transpose(
                         trp[:, 128 * e:128 * (e + 1)], stg[:, 128 * e:128 * (e + 1)],
                         ident[:, :]),
                     reads=[skey, "ident"],
                     writes=[trkey] if e in (0, 7) else [],
                     inc=1 if e in (0, 7) else 0)
            dstt, stride, dkey = {"kv": (wq_ar, 2048, "w_kv"),
                                  "q": (watq, 1024, "w_q"),
                                  "pr": (wprT, 1024, "w_pr")}[kind]
            base = 128 * r
            P.op("vector",
                 lambda h, trp=trp, dstt=dstt, stride=stride, base=base: h.tensor_copy(
                     dstt[:, :].rearrange("p (ec c) -> p ec c", c=stride)[:, :, base:base + 128],
                     trp[:, :].rearrange("p (ec c) -> p ec c", ec=8)),
                 reads=[trkey], writes=[dkey], inc=1)

        WPLAN = ([("kv", r) for r in range(16)] + [("q", r) for r in range(8)]
                 + [("pr", r) for r in range(8)] + [("fk", r) for r in range(8)])
        w_dma(*WPLAN[0], 0)
        w_dma(*WPLAN[1], 1)
        for r in range(16):
            w_slice("kv", r, tr1, "tr1", r % 2)
            if r + 2 < len(WPLAN):
                w_dma(*WPLAN[r + 2], (r + 2) % 2)

        kvbank = [(pA, "pA"), (pB, "pB"), (pC, "pC"), (pD, "pD")]

        def s_region(hh):
            g, par = hh // 2, hh % 2
            off = 65 * g if g < 7 else 512 + 65 * (g - 7)
            key = "pS0" if g < 7 else "pS1"
            return pS[64 * par:64 * (par + 1), off:off + 65], key, (0, 64 * par)

        def pf_col(g):
            return 65 * g

        def ln_a(t):
            xslot = t % 4
            xts = xt[:, 1024 * xslot:1024 * (xslot + 1)]
            xkey = f"xt{xslot}"
            dma("sync", xts, x[128 * t:128 * (t + 1), :], [], [xkey])
            P.op("vector", lambda h, xts=xts: h.bn_stats(st[:, 16:22], xts[:, 0:512]),
                 reads=[xkey], writes=["stA"], inc=1)
            P.op("vector", lambda h, xts=xts: h.bn_stats(st[:, 22:28], xts[:, 512:1024]),
                 reads=[xkey], writes=["stA"], inc=1)
            P.op("vector", lambda h: h.bn_aggr(st[:, 12:14], st[:, 16:28]),
                 reads=["stA"], writes=["stB"], inc=1)
            P.op("scalar", lambda h: h.activation(st[:, 15:16], st[:, 13:14], AF.Ln,
                                                  bias=cols[:, 5:6]),
                 reads=["stB", "c_eps"], writes=["stD"], inc=1)
            P.op("scalar", lambda h: h.activation(st[:, 16:17], st[:, 15:16], AF.Exp,
                                                  scale=cols[:, 0:1]),
                 reads=["stD", "c_nh"], writes=["stE"], inc=1)

        def ln_b(t):
            slot = t % 2
            xslot = t % 4
            xts = xt[:, 1024 * xslot:1024 * (xslot + 1)]
            P.op("vector", lambda h, xts=xts, slot=slot: h.tensor_scalar(
                scm[:, 1024 * slot:1024 * (slot + 1)], xts, st[:, 12:13], st[:, 16:17],
                ALU.subtract, ALU.mult),
                 reads=[f"xt{xslot}", "stB", "stE"], writes=[f"ytm{slot}"], inc=1)

        def heavy_stage(t):
            sc, tl = divmod(t, 8)
            slot = t % 2
            ytm = scm[:, 1024 * slot:1024 * (slot + 1)]
            ykey = f"ytm{slot}"
            for e in range(8):
                P.op("tensor", lambda h, e=e, ytm=ytm: h.transpose(
                    tr1[:, 128 * e:128 * (e + 1)], ytm[:, 128 * e:128 * (e + 1)], ident[:, :]),
                     reads=[ykey, "ident"], writes=["tr1"] if e in (0, 7) else [],
                     inc=1 if e in (0, 7) else 0)
            P.op("scalar", lambda h, tl=tl: h.copy(
                xnb[:, :].rearrange("p (ec c) -> p ec c", ec=8)[:, :, 128 * tl:128 * (tl + 1)],
                tr1[:, :].rearrange("p (ec c) -> p ec c", ec=8)),
                 reads=["tr1"], writes=[f"xnb{tl}"], inc=1)
            for n in range(4):
                bank, bkey = kvbank[n]
                for e in range(8):
                    P.op("tensor", lambda h, e=e, n=n, tl=tl, bank=bank: h.matmul(
                        bank[:, :],
                        xnb[:, 1024 * e + 128 * tl:1024 * e + 128 * (tl + 1)],
                        wq_ar[:, 2048 * e + 512 * n:2048 * e + 512 * (n + 1)],
                        start=(e == 0), stop=(e == 7), skip_group_check=True),
                         reads=[f"xnb{tl}", "w_kv"],
                         writes=[bkey] if e in (0, 7) else [],
                         inc=1 if e in (0, 7) else 0)
                if n < 2:
                    pe = psc[:, 1024 * (n % 2):1024 * (n % 2) + 512]
                    pr = psc[:, 1024 * (n % 2) + 512:1024 * (n % 2) + 1024]
                    pky = f"psc{n % 2}"
                    P.op("scalar", lambda h, bank=bank, pe=pe: h.activation(pe, bank[:, :], AF.Exp),
                         reads=[bkey], writes=[pky + "e"], inc=1)
                    P.op("scalar", lambda h, bank=bank, pr=pr: h.activation(pr, bank[:, :], AF.Relu),
                         reads=[bkey], writes=[pky + "r"], inc=1)
                    P.op("vector", lambda h, pe=pe, pr=pr, tl=tl, n=n: h.scalar_tensor_tensor(
                        k_sb[:, 1024 * tl + 512 * n:1024 * tl + 512 * (n + 1)],
                        pe, 1.0, pr, ALU.min, ALU.add),
                         reads=[pky + "e", pky + "r", bkey], writes=[f"k{tl}"], inc=1)
                else:
                    hh0 = 8 * (n - 2)
                    P.op("vector", lambda h, bank=bank, tl=tl, hh0=hh0: h.tensor_copy(
                        v_sb[:, 1040 * tl:1040 * (tl + 1)].rearrange(
                            "p (hh ee) -> p hh ee", hh=16)[:, hh0:hh0 + 8, 0:64],
                        bank[:, :].rearrange("p (hh ee) -> p hh ee", hh=8)),
                         reads=[bkey], writes=[f"v{tl}"], inc=1)
            P.op("vector", lambda h, tl=tl: h.memset(
                v_sb[:, 1040 * tl:1040 * (tl + 1)].rearrange(
                    "p (hh ee) -> p hh ee", hh=16)[:, :, 64:65], 1.0),
                 writes=[f"v{tl}"], inc=1)
            for hh in range(H):
                sl, skey, tp = s_region(hh)
                P.op("tensor", lambda h, hh=hh, tl=tl, sl=sl, tp=tp: h.matmul(
                    sl,
                    k_sb[:, 1024 * tl + 64 * hh:1024 * tl + 64 * (hh + 1)],
                    v_sb[:, 1040 * tl + 65 * hh:1040 * tl + 65 * (hh + 1)],
                    start=True, stop=True, tile_position=tp, skip_group_check=True),
                     reads=[f"k{tl}", f"v{tl}"],
                     writes=[skey] if hh in (0, 1, 12, 13, 14, 15) else [],
                     inc=1 if hh in (0, 1, 12, 13, 14, 15) else 0)
            if t == 0:
                P.op("vector", lambda h: h.tensor_copy(pf_sb[:, 0:455], pS[:, 0:455]),
                     reads=["pS0"], writes=["pf0"], inc=1)
                P.op("vector", lambda h: h.tensor_copy(pf_sb[:, 455:520], pS[:, 512:577]),
                     reads=["pS1"], writes=["pf0"], inc=1)
            else:
                prev = carry if tl == 0 else pf_sb[:, 520 * (tl - 1):520 * tl]
                pkey = "carry" if tl == 0 else f"pf{tl - 1}"
                P.op("vector", lambda h, tl=tl, prev=prev: h.tensor_add(
                    pf_sb[:, 520 * tl:520 * tl + 455], prev[:, 0:455], pS[:, 0:455]),
                     reads=["pS0", pkey], writes=[f"pf{tl}"], inc=1)
                P.op("vector", lambda h, tl=tl, prev=prev: h.tensor_add(
                    pf_sb[:, 520 * tl + 455:520 * tl + 520], prev[:, 455:520], pS[:, 512:577]),
                     reads=["pS1", pkey], writes=[f"pf{tl}"], inc=1)
            if sc < 3:
                dma("gpsimd", d_xnb[:, :, 1024 * sc + 128 * tl:1024 * sc + 128 * (tl + 1)],
                    xnb[:, :].rearrange("p (ec c) -> p ec c", ec=8)[:, :, 128 * tl:128 * (tl + 1)],
                    [f"xnb{tl}"], [])
                dma("gpsimd", d_k[:, t, :], k_sb[:, 1024 * tl:1024 * (tl + 1)],
                    [f"k{tl}"], [])
                dma("gpsimd", d_v[:, t, :], v_sb[:, 1040 * tl:1040 * (tl + 1)],
                    [f"v{tl}"], [])
            dma("gpsimd", d_pf[:, t, :], pf_sb[:, 520 * tl:520 * (tl + 1)],
                [f"pf{tl}"], [])
            if tl == 7:
                P.op("vector", lambda h: h.tensor_copy(carry[:, :], pf_sb[:, 520 * 7:520 * 8]),
                     reads=["pf7"], writes=["carry"], inc=1)

        ln_a(0)
        ln_b(0)
        for t in range(NT):
            if t + 1 < NT:
                ln_a(t + 1)
            heavy_stage(t)
            if t + 1 < NT:
                ln_b(t + 1)
            wi = t + 16
            if t < 8:
                w_slice("q", t, tr1, "tr1", wi % 2)
            elif t < 16:
                w_slice("pr", t - 8, tr1, "tr1", wi % 2)
            elif t < 24:
                e = t - 16
                slot = wi % 2
                stg = wstg[:, 1024 * slot:1024 * (slot + 1)]
                skey = f"wstg{slot}"
                P.op("scalar",
                     lambda h, e=e, stg=stg: h.activation(
                         kq[:, 1024 * e:1024 * (e + 1)], stg, AF.Copy, scale=cols[:, 2:3]),
                     reads=[skey, "c_sk"], writes=["kq"], inc=1)
            if wi + 2 < len(WPLAN):
                w_dma(*WPLAN[wi + 2], (wi + 2) % 2)

        dma("gpsimd", cc_in.rearrange("(g p) e -> p g e", p=128),
            pf_sb[:, 520 * 7:520 * 8].rearrange("p (g e) -> p g e", g=8),
            ["pf7"], ["cc_in"])
        P.op("gpsimd", lambda h: h.collective_compute(
            "AllGather", ALU.bypass,
            replica_groups=[[0, 1], [2, 3], [4, 5], [6, 7]],
            ins=[cc_in], outs=[cc_out]),
             reads=["cc_in"], writes=["cc_out"], sem_name=CC, inc=1)
        dma("sync", sinit[:, :].rearrange("p (g e) -> p g e", g=8),
            cc_out[0:1024, :].rearrange("(g p) e -> p g e", p=128),
            ["cc_out"], ["sinit_r"])
        P.op("vector", lambda h: h.tensor_scalar_mul(sinit[:, :], sinit[:, :], st[:, 2:3]),
             reads=["sinit_r", "st_mk"], writes=["sinit"], inc=1)
        P.op("vector", lambda h: h.tensor_copy(sinit_bf[:, :], sinit[:, :]),
             reads=["sinit"], writes=["sinitb"], inc=1)

        q_sb = wq_ar
        xnb_keys = [f"xnb{i}" for i in range(8)]
        SCMF = ["scmF0", "scmF1", "scmF2", "scmF3"]
        SCMR = ["scmR0", "scmR1", "scmR2", "scmR3"]

        def b_load_slot(sc, i, skip_xnb=False, pf_only=False):
            if not pf_only:
                if not skip_xnb:
                    dma("sync",
                        xnb[:, :].rearrange("p (ec c) -> p ec c", ec=8)[:, :, 128 * i:128 * (i + 1)],
                        d_xnb[:, :, 1024 * sc + 128 * i:1024 * sc + 128 * (i + 1)],
                        [], [f"xnb{i}"])
                dma("sync", k_sb[:, 1024 * i:1024 * (i + 1)], d_k[:, 8 * sc + i, :],
                    [], [f"k{i}"])
                dma("sync", v_sb[:, 1040 * i:1040 * (i + 1)], d_v[:, 8 * sc + i, :],
                    [], [f"v{i}"])
            cg = 8 * sc + i - 1
            if cg >= 0:
                dma("sync", pf_sb[:, 520 * i:520 * (i + 1)], d_pf[:, cg, :],
                    [], [f"pf{i}"])
                P.op("vector", lambda h, i=i: h.tensor_add(
                    pf_sb[:, 520 * i:520 * (i + 1)], pf_sb[:, 520 * i:520 * (i + 1)],
                    sinit[:, :]),
                     reads=["sinit", f"pf{i}"], writes=[f"pf{i}"], inc=1)
                P.op("scalar", lambda h, i=i: h.copy(
                    pf_bf[:, 520 * i:520 * (i + 1)], pf_sb[:, 520 * i:520 * (i + 1)]),
                     reads=[f"pf{i}"], writes=[f"pfb{i}"], inc=1)

        def b_qproj(sc):
            for qc in range(8):
                lob, lkey = (pA, "pA") if qc % 2 == 0 else (pC, "pC")
                hib, hkey = (pB, "pB") if qc % 2 == 0 else (pD, "pD")
                for e in range(8):
                    P.op("tensor", lambda h, e=e, qc=qc, lob=lob: h.matmul(
                        lob[:, :], watq[:, 1024 * e + 128 * qc:1024 * e + 128 * (qc + 1)],
                        xnb[:, 1024 * e:1024 * e + 512],
                        start=(e == 0), stop=(e == 7), skip_group_check=True),
                         reads=["w_q"] + xnb_keys[0:4],
                         writes=[lkey] if e in (0, 7) else [], inc=1 if e in (0, 7) else 0)
                for e in range(8):
                    P.op("tensor", lambda h, e=e, qc=qc, hib=hib: h.matmul(
                        hib[:, :], watq[:, 1024 * e + 128 * qc:1024 * e + 128 * (qc + 1)],
                        xnb[:, 1024 * e + 512:1024 * e + 1024],
                        start=(e == 0), stop=(e == 7), skip_group_check=True),
                         reads=["w_q"] + xnb_keys[4:8],
                         writes=[hkey] if e in (0, 7) else [], inc=1 if e in (0, 7) else 0)
                for half, (bank, bkey) in enumerate(((lob, lkey), (hib, hkey))):
                    pe = psc[:, 1024 * (qc % 2):1024 * (qc % 2) + 512]
                    pr = psc[:, 1024 * (qc % 2) + 512:1024 * (qc % 2) + 1024]
                    pky = f"psc{qc % 2}"
                    P.op("scalar", lambda h, bank=bank, pe=pe: h.activation(pe, bank[:, :], AF.Exp),
                         reads=[bkey], writes=[pky + "e"], inc=1)
                    P.op("scalar", lambda h, bank=bank, pr=pr: h.activation(pr, bank[:, :], AF.Relu),
                         reads=[bkey], writes=[pky + "r"], inc=1)
                    P.op("vector", lambda h, pe=pe, pr=pr, qc=qc, half=half: h.scalar_tensor_tensor(
                        q_sb[:, 1024 * qc + 512 * half:1024 * qc + 512 * (half + 1)],
                        pe, 1.0, pr, ALU.min, ALU.add),
                         reads=[pky + "e", pky + "r", bkey], writes=[f"q{qc}"], inc=1)

        def b_ffn_quant(tt):
            xqs = xq[:, 1024 * (tt % 2):1024 * (tt % 2 + 1)]
            P.op("scalar", lambda h, tt=tt, xqs=xqs: h.activation(
                xqs.rearrange("p (ec c) -> p ec c", ec=8),
                xnb[:, :].rearrange("p (ec c) -> p ec c", ec=8)[:, :, 128 * tt:128 * (tt + 1)],
                AF.Copy, scale=cols[:, 1:2]),
                 reads=[f"xnb{tt}", "c_si"], writes=[f"xq{tt % 2}"], inc=1)

        def b_ffn(sc):
            b_ffn_quant(0)
            for tt in range(8):
                xqs = xq[:, 1024 * (tt % 2):1024 * (tt % 2 + 1)]
                xqkey = f"xq{tt % 2}"
                if tt + 1 < 8:
                    b_ffn_quant(tt + 1)
                lob, lkey = (pA, "pA") if tt % 2 == 0 else (pC, "pC")
                hib, hkey = (pB, "pB") if tt % 2 == 0 else (pD, "pD")
                for e in range(8):
                    P.op("tensor", lambda h, e=e, xqs=xqs, lob=lob: h.matmul(
                        lob[:, :], xqs[:, 128 * e:128 * (e + 1)],
                        kq[:, 1024 * e:1024 * e + 512],
                        start=(e == 0), stop=(e == 7), skip_group_check=True),
                         reads=[xqkey, "kq"],
                         writes=[lkey] if e in (0, 7) else [], inc=1 if e in (0, 7) else 0)
                    P.op("tensor", lambda h, e=e, xqs=xqs, hib=hib: h.matmul(
                        hib[:, :], xqs[:, 128 * e:128 * (e + 1)],
                        kq[:, 1024 * e + 512:1024 * e + 1024],
                        start=(e == 0), stop=(e == 7), skip_group_check=True),
                         reads=[xqkey, "kq"],
                         writes=[hkey] if e in (0, 7) else [], inc=1 if e in (0, 7) else 0)
                for half, (bank, bkey) in enumerate(((lob, lkey), (hib, hkey))):
                    j = 2 * (tt % 2) + half
                    dst = scm[:, 512 * j:512 * (j + 1)]
                    P.op("scalar", lambda h, bank=bank, dst=dst: h.activation(
                        dst, bank[:, :], AF.Copy, scale=cols[:, 3:4]),
                         reads=[bkey, "c_sp"], writes=[f"scmF{j}"] + SCMR, inc=1)
                    P.op("vector", lambda h, dst=dst, tt=tt, half=half: h.tensor_add(
                        q_sb[:, 8192 + 1024 * tt + 512 * half:8192 + 1024 * tt + 512 * (half + 1)],
                        dst, fbb[:, 512 * half:512 * (half + 1)]),
                         reads=[f"scmF{j}", "fbb"], writes=[f"ffn{tt}"], inc=1)

        def b_ktr(c):
            kfb = kf if c % 2 == 0 else yt
            for g in range(8):
                P.op("tensor", lambda h, g=g, c=c: h.transpose(
                    tr1[:, 128 * g:128 * (g + 1)],
                    k_sb[:, 1024 * c + 128 * g:1024 * c + 128 * (g + 1)], ident[:, :]),
                     reads=[f"k{c}", "ident"], writes=["tr1"] if g in (0, 7) else [],
                     inc=1 if g in (0, 7) else 0)
            P.op("scalar", lambda h, kfb=kfb: h.copy(kfb[:, :], tr1[:, :]),
                 reads=["tr1"], writes=[f"kf{c % 2}"], inc=1)

        def num_slot(c, hh):
            if c % 2 == 0:
                if hh < 7:
                    return pC, "pC", 65 * hh
                if hh < 14:
                    return pD, "pD", 65 * (hh - 7)
                return den_ps, "dE2", 65 * (hh - 14)
            if hh < 7:
                return pS, "pS0", 65 * hh
            if hh < 14:
                return pS, "pS1", 512 + 65 * (hh - 7)
            return den_ps, "dO2", 256 + 65 * (hh - 14)

        def num_regions(c):
            if c % 2 == 0:
                return [(pC[:, 0:455], "pC", 0, 7), (pD[:, 0:455], "pD", 7, 7),
                        (den_ps[:, 0:130], "dE2", 14, 2)]
            return [(pS[:, 0:455], "pS0", 0, 7), (pS[:, 512:967], "pS1", 7, 7),
                    (den_ps[:, 256:386], "dO2", 14, 2)]

        def scT_round(c, r):
            kfb = kf if c % 2 == 0 else yt
            kkey = f"kf{c % 2}"
            par = r % 2
            heads = [8 * (r // 2) + par + 2 * i for i in range(4)]
            bank, bkey = (pA, "pA") if par == 0 else (pB, "pB")
            for i, hh in enumerate(heads):
                g = hh // 2
                P.op("tensor", lambda h, g=g, par=par, c=c, bank=bank, i=i, kfb=kfb:
                     h.matmul(
                         bank[:, 128 * i:128 * (i + 1)],
                         kfb[64 * par:64 * (par + 1), 128 * g:128 * (g + 1)],
                         q_sb[64 * par:64 * (par + 1),
                              1024 * g + 128 * c:1024 * g + 128 * (c + 1)],
                         start=True, stop=True, skip_group_check=True),
                     reads=[kkey, f"q{g}"],
                     writes=[bkey] if i in (0, 3) else [],
                     inc=1 if i in (0, 3) else 0)
            wkeys = [SCMR[r]] + (SCMF if c == 0 else [])
            P.op("vector", lambda h, bank=bank, r=r, par=par: h.tensor_tensor(
                scm[:, :].rearrange("p (s c2) -> p s c2", s=16)[
                    :, 8 * (r // 2) + par:8 * (r // 2) + par + 7:2, :],
                bank[:, :].rearrange("p (s c2) -> p s c2", s=4),
                mask[:, :].unsqueeze(1).broadcast_to([128, 4, 128]),
                ALU.mult),
                 reads=[bkey, "mask"], writes=wkeys, inc=1)

        def b_scan_core(sc, c, filler=None, skip01=False):
            def num_round(r):
                par = r % 2
                heads = [8 * (r // 2) + par + 2 * i for i in range(4)]
                for hh in heads:
                    g, p2 = hh // 2, hh % 2
                    nb, nkey, noff = num_slot(c, hh)
                    if sc == 0 and c == 0:
                        rhs = sinit_bf[64 * p2:64 * (p2 + 1), 65 * g:65 * (g + 1)]
                        rkey = "sinitb"
                    else:
                        pfo = 520 * c + 65 * g
                        rhs = pf_bf[64 * p2:64 * (p2 + 1), pfo:pfo + 65]
                        rkey = f"pfb{c}"
                    P.op("tensor", lambda h, hh=hh, c=c, nb=nb, noff=noff: h.matmul(
                        nb[:, noff:noff + 65],
                        scm[:, 128 * hh:128 * (hh + 1)],
                        v_sb[:, 1040 * c + 65 * hh:1040 * c + 65 * (hh + 1)],
                        start=True, stop=False, skip_group_check=True),
                         reads=[SCMR[r], f"v{c}"], writes=[nkey], inc=1)
                    P.op("tensor", lambda h, g=g, p2=p2, c=c, nb=nb, noff=noff, rhs=rhs:
                         h.matmul(
                             nb[:, noff:noff + 65],
                             q_sb[64 * p2:64 * (p2 + 1),
                                  1024 * g + 128 * c:1024 * g + 128 * (c + 1)],
                             rhs, start=False, stop=True, skip_group_check=True),
                         reads=[f"q{g}", rkey], writes=[nkey], inc=1)

            if not skip01:
                scT_round(c, 0)
                scT_round(c, 1)
            if filler is not None:
                filler()
            num_round(0)
            scT_round(c, 2)
            num_round(1)
            scT_round(c, 3)
            num_round(2)
            num_round(3)

        def b_scan_tail(sc, c):
            ysl = wstg[:, 1024 * (c % 2):1024 * (c % 2 + 1)]
            ykey = f"ytw{c % 2}"
            regions = num_regions(c)
            for rap, rkey, h0, ns in regions:
                P.op("vector", lambda h, rap=rap, h0=h0, ns=ns: h.tensor_scalar_add(
                    st[:, 32 + h0:32 + h0 + ns], rap[:, 64:65 * ns:65], 1e-6),
                     reads=[rkey], writes=["dens2"], inc=1)
            P.op("vector", lambda h: h.reciprocal(st[:, 48:64], st[:, 32:48]),
                 reads=["dens2"], writes=["rec"], inc=1)
            for rap, rkey, h0, ns in regions:
                P.op("vector", lambda h, rap=rap, h0=h0, ns=ns, ysl=ysl: h.tensor_tensor(
                    ysl[:, 64 * h0:64 * (h0 + ns)].rearrange("p (s e) -> p s e", s=ns),
                    rap.rearrange("p (s e) -> p s e", s=ns)[:, :, 0:64],
                    st[:, 48 + h0:48 + h0 + ns].unsqueeze(2).broadcast_to([128, ns, 64]),
                    ALU.mult),
                     reads=[rkey, "rec"], writes=[ykey], inc=1)
            for g in range(8):
                P.op("tensor", lambda h, g=g, ysl=ysl: h.transpose(
                    tr1[:, 128 * g:128 * (g + 1)], ysl[:, 128 * g:128 * (g + 1)], ident[:, :]),
                     reads=[ykey, "ident"], writes=["tr1"] if g in (0, 7) else [],
                     inc=1 if g in (0, 7) else 0)
            P.op("scalar", lambda h, c=c: h.copy(
                xnb[:, :].rearrange("p (ec c2) -> p ec c2", ec=8)[:, :, 128 * c:128 * (c + 1)],
                tr1[:, :].rearrange("p (ec c2) -> p ec c2", ec=8)),
                 reads=["tr1"], writes=[f"xnb{c}"], inc=1)

        def b_xpre(sc, c):
            tglob = 8 * sc + c
            slot = c % 4
            dma("sync", xt[:, 1024 * slot:1024 * (slot + 1)],
                x[128 * tglob:128 * (tglob + 1), :], [], [f"xt{slot}"])

        def b_outproj(sc, c):
            tglob = 8 * sc + c
            slot = c % 4
            xts = xt[:, 1024 * slot:1024 * (slot + 1)]
            xkey = f"xt{slot}"
            ot = outt[:, 1024 * (c % 2):1024 * (c % 2 + 1)]
            s2 = 2 + c % 2
            ot2 = outt[:, 1024 * s2:1024 * (s2 + 1)]
            okey, okey2 = f"ot{c % 2}", f"ot{2 + c % 2}"
            for half, (bank, bkey) in enumerate(((pA, "pA"), (pB, "pB"))):
                for e in range(8):
                    P.op("tensor", lambda h, e=e, c=c, bank=bank, half=half: h.matmul(
                        bank[:, :], xnb[:, 1024 * e + 128 * c:1024 * e + 128 * (c + 1)],
                        wprT[:, 1024 * e + 512 * half:1024 * e + 512 * (half + 1)],
                        start=(e == 0), stop=(e == 7), skip_group_check=True),
                         reads=[f"xnb{c}", "w_pr"],
                         writes=[bkey] if e in (0, 7) else [], inc=1 if e in (0, 7) else 0)
                P.op("vector", lambda h, ot=ot, xts=xts, bank=bank, half=half: h.tensor_add(
                    ot[:, 512 * half:512 * (half + 1)], xts[:, 512 * half:512 * (half + 1)],
                    bank[:, :]),
                     reads=[xkey, bkey], writes=[okey], inc=1)
            P.op("gpsimd", lambda h, ot=ot, ot2=ot2, c=c: h.tensor_tensor(
                ot2[:, :], ot[:, :],
                q_sb[:, 8192 + 1024 * c:8192 + 1024 * (c + 1)], ALU.add),
                 reads=[okey, f"ffn{c}"], writes=[okey2], inc=1)
            dma("gpsimd", out[128 * tglob:128 * (tglob + 1), :], ot2[:, :], [okey2], [])

        SC_ORDER = [3, 2, 1, 0]
        for si, sc in enumerate(SC_ORDER):
            if si == 0:
                for i in range(8):
                    b_load_slot(sc, i, pf_only=True)
            b_qproj(sc)
            b_ffn(sc)
            b_ktr(0)
            if si == 0:
                b_xpre(sc, 0)
                b_xpre(sc, 1)
            for c in range(8):
                i2 = 8 * si + c + 2
                if i2 < NT:
                    b_xpre(SC_ORDER[i2 // 8], i2 % 8)
                b_scan_core(sc, c,
                            filler=(lambda cc=c + 1: b_ktr(cc)) if c + 1 < 8 else None,
                            skip01=(c > 0))
                if c + 1 < 8:
                    scT_round(c + 1, 0)
                    scT_round(c + 1, 1)
                b_scan_tail(sc, c)
                b_outproj(sc, c)
                if si + 1 < NSC:
                    b_load_slot(SC_ORDER[si + 1], c)

        for i in range(4):
            for base, eng2 in ((STQ, "gpsimd"), (LD, "sync")):
                sn = f"{base}{i}"
                if sn in P.cnt and P.cnt[sn]:
                    P.raw(eng2, lambda h, sn=sn, v=P.cnt[sn]: h.wait_ge(P.sems[sn], v))
        P.flush()
    return nc


def kernel(L0_attn_x, L0_attn_w_attn, L0_attn_w_proj, L0_ffn_kernel,
           L0_ffn_input_scale, L0_ffn_kernel_scale, L0_ffn_bias):
    x = np.ascontiguousarray(np.asarray(L0_attn_x, dtype=np.float32))
    wat = np.ascontiguousarray(np.asarray(L0_attn_w_attn, dtype=np.float32))
    wpr = np.ascontiguousarray(np.asarray(L0_attn_w_proj, dtype=np.float32))
    fk = np.ascontiguousarray(np.asarray(L0_ffn_kernel))
    fb = np.asarray(L0_ffn_bias)
    s_in = np.asarray(L0_ffn_input_scale, dtype=np.float32).ravel()[0]
    s_k = np.asarray(L0_ffn_kernel_scale, dtype=np.float32).ravel()[0]
    sin_col = np.ascontiguousarray(np.broadcast_to(s_in, (128, 1))).astype(np.float32)
    sk_col = np.ascontiguousarray(np.broadcast_to(s_k, (128, 1))).astype(np.float32)
    fbr = np.ascontiguousarray(np.broadcast_to(fb.reshape(1, E), (128, E)))

    nc = build_nc()
    in_maps = []
    for c in range(NCORES):
        b, half = divmod(c, 2)
        xs = np.ascontiguousarray(x[b, half * T:(half + 1) * T, :])
        mk = np.full((128, 1), 1.0 if half == 1 else 0.0, dtype=np.float32)
        in_maps.append({"x": xs, "wat": wat, "wpr": wpr, "fk": fk,
                        "s_in": sin_col, "s_k": sk_col, "fbr": fbr, "mkv": mk})
    res = run_bass_kernel_spmd(nc, in_maps, core_ids=list(range(NCORES)))
    outp = np.empty((4, 8192, E), dtype=np.float32)
    for c in range(NCORES):
        b, half = divmod(c, 2)
        outp[b, half * T:(half + 1) * T, :] = res.results[c]["out"]
    return outp
